# version 26
# baseline (speedup 1.0000x reference)
import contextlib
import numpy as np
import ml_dtypes

import concourse.bass as bass
import concourse.mybir as mybir
from concourse.bass_utils import run_bass_kernel_spmd

F32 = mybir.dt.float32
BF16 = mybir.dt.bfloat16
AF = mybir.ActivationFunctionType
ALU = mybir.AluOpType
AX = mybir.AxisListType

B, N, C = 4, 4096, 1024
HEADS, DH, INNER = 8, 64, 512
HID = 2730
NHC = 22
NCORE = 8
OWN = 2048
HALO = 513
DEAD = 127
XW = DEAD + HALO + OWN
QOFF = DEAD + HALO
KOFF = 128
KW = 2560
NKC = 20
NQC = 16
NXT = 21
NEG = -1.0e30
EPS = 1e-5
QK_SCALE = 8.0
RG_PAIRS = [[0, 1], [2, 3], [4, 5], [6, 7]]
ENGINES = ("tensor", "vector", "scalar", "gpsimd", "sync")
_UID = [0]


def _u(name):
    _UID[0] += 1
    return f"{name}_u{_UID[0]}"



def _rotary_tables(t0):
    t = t0 + np.arange(KW, dtype=np.float64)
    half = np.arange(0, DH, 2, dtype=np.float64)
    inv_freq = 1.0 / 10000.0 ** (half / DH)
    fr = t[:, None] * inv_freq[None, :]
    fr = np.concatenate([fr, fr], axis=1)
    base = (half + 0.4 * DH) / (1.4 * DH)
    power = (t - N // 2) / 256.0
    sc = base[None, :] ** power[:, None]
    sc = np.concatenate([sc, sc], axis=1)
    cos, sin = np.cos(fr), np.sin(fr)
    sgn = np.where(np.arange(DH) < DH // 2, -1.0, 1.0)[None, :]
    out = []
    for tab in (cos * sc, sgn * sin * sc, cos / sc, sgn * sin / sc):
        m = np.zeros((128, NKC * DH), dtype=np.float32)
        for j in range(NKC):
            m[:, DH * j:DH * (j + 1)] = tab[128 * j:128 * (j + 1), :]
        out.append(m)
    return out


def _masks():
    r = np.arange(128)
    up = np.where(r[None, :] >= r[:, None], 0.0, NEG).astype(np.float32)
    lo = np.where(r[None, :] <= r[:, None], 0.0, NEG).astype(np.float32)
    bf = ml_dtypes.bfloat16
    return (up.astype(bf), lo.astype(bf),
            up.T.copy().astype(bf), lo.T.copy().astype(bf))


class P:
    def __init__(self, nc, es):
        self.nc = nc
        self.ops = {e: [] for e in ENGINES}
        self.sem = {e: es.enter_context(nc.semaphore(f"sm_{e}")) for e in ENGINES}
        self.cnt = {e: 0 for e in ENGINES}
        self.waited = {}
        self.tiles = {}

    def _edge(self, edges, eng, cnt):
        if cnt is None:
            cnt = self.cnt[eng]
        if cnt > 0:
            edges[eng] = max(edges.get(eng, 0), cnt)

    def emit(self, eng, fn, reads=(), writes=(), dma=False, inc=True):
        edges = {}
        for k in reads:
            st = self.tiles.get(k)
            if st and st["w"] is not None:
                self._edge(edges, *st["w"])
        for k in writes:
            st = self.tiles.get(k)
            if st:
                if st["w"] is not None:
                    self._edge(edges, *st["w"])
                for re_, rc in st["r"].items():
                    self._edge(edges, re_, rc)
        if eng == "tensor":
            edges.pop("tensor", None)
        real = []
        for se, c in edges.items():
            if self.waited.get((se, eng), 0) >= c:
                continue
            real.append((self.sem[se], c))
            self.waited[(se, eng)] = c
        amt = 16 if dma else 1
        if inc:
            self.cnt[eng] += amt
        my = self.cnt[eng]
        me = self.sem[eng]

        def thunk(h, real=real, fn=fn, inc=inc, me=me, amt=amt):
            for s, c in real:
                h.wait_ge(s, c)
            inst = fn(h)
            if inc:
                inst.then_inc(me, amt)

        self.ops[eng].append(thunk)
        wrec = (eng, my if (inc and not dma) else None)
        for k in writes:
            self.tiles[k] = {"w": wrec, "r": {}}
        for k in reads:
            st = self.tiles.setdefault(k, {"w": None, "r": {}})
            if inc and not dma:
                prev = st["r"].get(eng)
                st["r"][eng] = my if prev is not None or eng not in st["r"] \
                    else my
            else:
                st["r"][eng] = None

    def barrier(self):
        counts = dict(self.cnt)
        for e in ENGINES:
            waits = []
            for o in ENGINES:
                if o == e or counts[o] == 0:
                    continue
                if self.waited.get((o, e), 0) < counts[o]:
                    waits.append((self.sem[o], counts[o]))
                    self.waited[(o, e)] = counts[o]
            if waits:
                def thunk(h, waits=waits):
                    for s, c in waits:
                        h.wait_ge(s, c)
                self.ops[e].append(thunk)

    def finalize(self, block):
        for e, reg in (("tensor", block.tensor), ("vector", block.vector),
                       ("scalar", block.scalar), ("gpsimd", block.gpsimd),
                       ("sync", block.sync)):
            ops = self.ops[e]

            def run(h, ops=ops):
                for f in ops:
                    f(h)

            reg(run)


def build_program(dbg=()):
    nc = bass.Bass("TRN2", target_bir_lowering=False)
    dt = nc.dram_tensor

    x_in = dt("x", [XW, C], F32, kind="ExternalInput").ap()
    tabs_in = [dt(f"tabi{i}", [128, NKC * DH], F32, kind="ExternalInput").ap()
               for i in range(4)]
    m_in = [dt(f"maski{i}", [128, 128], BF16, kind="ExternalInput").ap()
            for i in range(4)]
    bigrow_in = dt("bigrow", [1, 128], BF16, kind="ExternalInput").ap()
    gate_in = dt("gate", [128, 1], F32, kind="ExternalInput").ap()
    aw_qkv = dt("a_wqkv", [3 * C, C], F32, kind="ExternalInput").ap()
    aw_c1 = dt("a_wc1", [1, C], F32, kind="ExternalInput").ap()
    wqkv = dt("f_wqkv", [2, C, 3 * INNER], F32, kind="ExternalInput").ap()
    qsc = dt("f_qs", [2, DH], F32, kind="ExternalInput").ap()
    ksc = dt("f_ks", [2, DH], F32, kind="ExternalInput").ap()
    wo = dt("f_wo", [2, INNER, C], F32, kind="ExternalInput").ap()
    lng = dt("f_lng", [2, C], F32, kind="ExternalInput").ap()
    lnb = dt("f_lnb", [2, C], F32, kind="ExternalInput").ap()
    w1 = dt("f_w1", [2, C, 2 * HID], F32, kind="ExternalInput").ap()
    b1 = dt("f_b1", [2, 2 * HID], F32, kind="ExternalInput").ap()
    w2 = dt("f_w2", [2, HID, C], F32, kind="ExternalInput").ap()
    b2 = dt("f_b2", [2, C], F32, kind="ExternalInput").ap()
    out_d = dt("out", [OWN, C], F32, kind="ExternalOutput").ap()

    cc1_in = dt("cc1_in", [1, C], F32)
    cc1_out = dt("cc1_out", [2, C], F32)
    cc2_in = dt("cc2_in", [1, C + 8], F32)
    cc2_out = dt("cc2_out", [2, C + 8], F32)
    cch_in = dt("cch_in", [8 * 128, HALO], F32)
    cch_out = dt("cch_out", [2 * 8 * 128, HALO], F32)

    dbg_out = {}

    def dbg_t(name, shape):
        if name not in dbg_out:
            dbg_out[name] = dt("dbg_" + name, shape, F32,
                               kind="ExternalOutput").ap()
        return dbg_out[name]

    es = contextlib.ExitStack()
    with es, nc.Block() as block:
        S = lambda *a: es.enter_context(nc.sbuf_tensor(*a))
        p = P(nc, es)
        E = p.emit

        xs = [S(f"xs{c}", [128, XW], F32) for c in range(8)]
        masks = [S(f"msk{i}", [128, 128], BF16) for i in range(4)]
        bigrow = S("bigrowS", [1, 128], BF16)
        gate = S("gateS", [128, 1], F32)
        idb = S("idb", [128, 128], BF16)
        idf = S("idf", [128, 128], F32)
        onesb = S("onesb", [1, 128], BF16)
        onesf = S("onesf", [1, 128], F32)
        onecolb = S("onecolb", [128, 1], BF16)
        onesbb = S("onesbb", [128, 128], BF16)
        onef1 = S("onef1", [1, 1], F32)
        epsc = S("epscS", [128, 1], F32)
        vec8 = S("vec8", [128, 8], F32)
        bcols = S("bcolsS", [128, 136], F32)
        qsb = S("qsbS", [128, DH], F32)
        ksb = S("ksbS", [128, DH], F32)

        for i in range(4):
            E("sync", lambda h, d=masks[i], s=m_in[i]:
              h.dma_start(out=d[:], in_=s), writes=[("msk", i)], dma=True)
        E("sync", lambda h: h.dma_start(out=bigrow[:], in_=bigrow_in),
          writes=["bigrow"], dma=True)
        E("sync", lambda h: h.dma_start(out=gate[:], in_=gate_in),
          writes=["gate"], dma=True)
        for t, v in ((idb, 1.0), (idf, 1.0), (onesb, 1.0), (onesf, 1.0),
                     (onecolb, 1.0), (onef1, 1.0), (epsc, EPS),
                     (onesbb, 1.0)):
            E("gpsimd", lambda h, t=t, v=v: h.memset(t[:], v),
              writes=[t.name])
        for t in (idb, idf):
            E("gpsimd", lambda h, t=t: h.affine_select(
                out=t[:], in_=t[:], compare_op=ALU.is_equal, fill=0.0,
                pattern=[[-1, 128]], base=0, channel_multiplier=1),
              reads=[t.name], writes=[t.name])

        phase_a(nc, p, es, x_in, aw_qkv, aw_c1, cc1_in, cc1_out, cc2_in,
                cc2_out, xs, gate, idf, onesb, onesf, onef1, epsc, vec8,
                dbg, dbg_t)

        for l in range(2):
            build_layer(nc, p, es, l, xs, masks, bigrow, gate,
                        idb, idf, onesb, onesf, onecolb, onef1, epsc,
                        onesbb, bcols, qsb, ksb,
                        tabs_in, wqkv, qsc, ksc, wo, lng, lnb,
                        w1, b1, w2, b2, cch_in, cch_out, dbg, dbg_t)

        finale(nc, p, es, x_in, out_d, xs, idf, onesf, epsc, vec8)

        p.finalize(block)
    return nc, dbg_out


def phase_a(nc, p, es, x_in, aw_qkv, aw_c1, cc1_in, cc1_out, cc2_in, cc2_out,
            xs, gate, idf, onesb, onesf, onef1, epsc, vec8, dbg, dbg_t):
    E = p.emit
    with contextlib.ExitStack() as ph:
        Sa = lambda n, *a: ph.enter_context(nc.sbuf_tensor(_u(n), *a))
        PSa = lambda n, *a: ph.enter_context(nc.psum_tensor(_u(n), *a))
        xt = [Sa(f"xtA{s}", [128, C], F32) for s in range(2)]
        ln0 = [Sa(f"ln0_{i}", [128, C], BF16) for i in range(NQC)]
        aw = [Sa(f"awA{i}", [128, C], BF16) for i in range(8)]
        scr = [Sa(f"scrA{s}", [128, C], F32) for s in range(2)]
        scr2 = Sa("scrA2", [128, 16], F32)
        rowA = Sa("rowA", [1, C + 8], F32)
        rowG0 = Sa("rowG0", [1, C + 8], F32)
        rowG1 = Sa("rowG1", [1, C + 8], F32)
        rowB = Sa("rowBA", [1, C], BF16)
        ub = Sa("ubA", [128, C], BF16)
        bb = Sa("bbA", [128, C], BF16)
        vecb = Sa("vecbA", [128, C], F32)
        agcol = Sa("agcolA", [128, NQC], F32)
        colb = Sa("colbA", [128, NQC], BF16)
        scol = Sa("scolA", [128, NQC], F32)
        t1c = Sa("t1cA", [128, 8], F32)
        avc = Sa("avcA", [128, 8], F32)
        stat = Sa("statA", [128, 3 * NXT], F32)
        tiny = Sa("tinyA", [128, 32], F32)
        mrow = Sa("mrowA", [1, 160], F32)
        wc1c = Sa("wc1cA", [128, 8], F32)
        wc1cb = Sa("wc1cbA", [128, 8], BF16)
        psT = [PSa(f"psTA{s}", [128, 512], F32) for s in range(2)]
        psRow = PSa("psRowA", [1, C], F32)
        psB = PSa("psBA", [128, 512], F32)
        psB2 = PSa("psB2A", [128, 512], F32)
        psSm = PSa("psSmA", [128, 128], F32)

        def col_to_cell(colap, colkey, cell, op="max", negate=False):
            E("tensor", lambda h: h.transpose(psSm[0:1, 0:128], colap,
                                              idf[:]),
              reads=[colkey], writes=["psSm"])
            E("scalar", lambda h: h.copy(mrow[0:1, 0:128],
                                         psSm[0:1, 0:128]),
              reads=["psSm"], writes=["mrow"])
            if op == "max":
                E("vector", lambda h: h.reduce_max(
                    mrow[0:1, cell:cell + 1], mrow[0:1, 0:128], axis=AX.X,
                    negate=negate),
                  reads=["mrow"], writes=["mrow"])
            else:
                E("vector", lambda h: h.reduce_sum(
                    mrow[0:1, cell:cell + 1], mrow[0:1, 0:128], axis=AX.X),
                  reads=["mrow"], writes=["mrow"])

        def cell_to_col(cell, tcol):
            E("tensor", lambda h: h.matmul(
                psSm[:, 0:1], onesf[:], mrow[0:1, cell:cell + 1],
                start=True, stop=True, skip_group_check=True),
              reads=["mrow", "onesf"], writes=["psSm"])
            E("scalar", lambda h: h.copy(tiny[:, tcol:tcol + 1],
                                         psSm[:, 0:1]),
              reads=["psSm"], writes=[("tiny", 0)])

        def load_aw(which):
            for i in range(8):
                E("gpsimd", lambda h, d=aw[i],
                  s=aw_qkv[C * which + 128 * i:C * which + 128 * (i + 1), :]:
                  h.dma_start(out=d[:], in_=s), writes=[("aw", i)], dma=True)

        load_aw(0)
        E("sync", lambda h: h.dma_start(out=rowA[0:1, 0:C], in_=aw_c1),
          writes=["rowA"], dma=True)

        def wc1_cols(h):
            last = None
            for i in range(8):
                last = h.matmul(psSm[:, i:i + 1],
                                rowA[0:1, 128 * i:128 * (i + 1)], onef1[:],
                                start=True, stop=True, skip_group_check=True)
            return last
        E("tensor", wc1_cols, reads=["rowA", "onef1"], writes=["psSm"])
        E("scalar", lambda h: h.copy(wc1c[:], psSm[:, 0:8]),
          reads=["psSm"], writes=["wc1c"])
        E("vector", lambda h: h.tensor_copy(wc1cb[:], wc1c[:]),
          reads=["wc1c"], writes=["wc1cb"])

        def u_mm(h):
            last = None
            for i in range(8):
                for hf in range(2):
                    last = h.matmul(
                        psRow[0:1, 512 * hf:512 * (hf + 1)],
                        wc1cb[:, i:i + 1],
                        aw[i][:, 512 * hf:512 * (hf + 1)],
                        start=(i == 0), stop=True, skip_group_check=True)
            return last
        E("tensor", u_mm, reads=["wc1cb"] + [("aw", i) for i in range(8)],
          writes=["psRow"])
        E("scalar", lambda h: h.copy(rowA[0:1, 0:C], psRow[:]),
          reads=["psRow"], writes=["rowA"])
        E("vector", lambda h: h.tensor_copy(rowB[:], rowA[0:1, 0:C]),
          reads=["rowA"], writes=["rowB"])

        def bcast_row_to(dst, dstkey):
            for i in range(2):
                E("tensor", lambda h, i=i: h.matmul(
                    psB[:, :], onesb[:], rowB[0:1, 512 * i:512 * (i + 1)],
                    start=True, stop=True, skip_group_check=True),
                  reads=["rowB", "onesb"], writes=["psB"])
                E("scalar", lambda h, i=i: h.copy(
                    dst[:, 512 * i:512 * (i + 1)], psB[:]),
                  reads=["psB"], writes=[dstkey])

        bcast_row_to(ub, "ub")

        def pass1_tile(i):
            sl = i % 2
            tb = 16 * (i % 2)
            E("sync", lambda h: h.dma_start(
                out=xt[sl][:], in_=x_in[128 * i:128 * (i + 1), :]),
              writes=[("xt", sl)], dma=True)
            E("vector", lambda h, tb=tb: h.bn_stats(
                tiny[:, tb:tb + 6], xt[sl][:, 0:512]),
              reads=[("xt", sl)], writes=[("tiny", tb)])
            E("vector", lambda h, tb=tb: h.bn_stats(
                tiny[:, tb + 6:tb + 12], xt[sl][:, 512:C]),
              reads=[("xt", sl)], writes=[("tiny", tb)])
            E("vector", lambda h, tb=tb: h.bn_aggr(
                stat[:, 3 * i:3 * i + 2], tiny[:, tb:tb + 12]),
              reads=[("tiny", tb)], writes=[("stat", i)])
            E("scalar", lambda h: h.activation(
                stat[:, 3 * i + 2:3 * i + 3], stat[:, 3 * i + 1:3 * i + 2],
                AF.Sqrt, bias=epsc[:, 0:1]),
              reads=[("stat", i)], writes=[("stat", i)])
            E("vector", lambda h: h.reciprocal(
                stat[:, 3 * i + 2:3 * i + 3], stat[:, 3 * i + 2:3 * i + 3]),
              reads=[("stat", i)], writes=[("stat", i)])
            if i < 5:
                return
            o = i - 5
            E("vector", lambda h: h.tensor_scalar(
                ln0[o][:], xt[sl][:], stat[:, 3 * i:3 * i + 1],
                stat[:, 3 * i + 2:3 * i + 3], op0=ALU.subtract, op1=ALU.mult),
              reads=[("xt", sl), ("stat", i)], writes=[("ln0", o)])
            E("gpsimd", lambda h: h.tensor_mul(scr[sl][:], ln0[o][:],
                                               ub[:]),
              reads=[("ln0", o), "ub"], writes=[("scr", sl)])

        def pass1_finish(o):
            sl = (o + 5) % 2
            E("vector", lambda h: h.reduce_sum(agcol[:, o:o + 1],
                                               scr[sl][:], axis=AX.X),
              reads=[("scr", sl)], writes=["agcol"])
            E("scalar", lambda h: h.copy(colb[:, o:o + 1],
                                         agcol[:, o:o + 1]),
              reads=["agcol"], writes=["colb"])

            def m_mm(h):
                last = None
                for hf in range(2):
                    last = h.matmul(
                        psRow[0:1, 512 * hf:512 * (hf + 1)], colb[:, o:o + 1],
                        ln0[o][:, 512 * hf:512 * (hf + 1)],
                        start=(o == 0), stop=True, skip_group_check=True)
                return last
            E("tensor", m_mm, reads=["colb", ("ln0", o)], writes=["psRow"],
              inc=(o == NQC - 1))

        for i in range(NXT):
            pass1_tile(i)
            if i >= 6:
                pass1_finish(i - 6)
        pass1_finish(NQC - 1)

        E("scalar", lambda h: h.copy(rowA[0:1, 0:C], psRow[:]),
          reads=["psRow"], writes=["rowA"])
        E("scalar", lambda h: h.dma_start(out=cc1_in[:, :],
                                          in_=rowA[0:1, 0:C]),
          reads=["rowA"], writes=["cc1_in"], dma=True)
        E("gpsimd", lambda h: h.collective_compute(
            "AllGather", ALU.bypass, replica_groups=RG_PAIRS,
            ins=[cc1_in.ap().opt()], outs=[cc1_out.ap().opt()]),
          reads=["cc1_in"], writes=["cc1_out"])
        load_aw(1)
        E("sync", lambda h: h.dma_start(out=rowG0[0:1, 0:C],
                                        in_=cc1_out[0:1, :]),
          reads=["cc1_out"], writes=["rowG"], dma=True)
        E("sync", lambda h: h.dma_start(out=rowG1[0:1, 0:C],
                                        in_=cc1_out[1:2, :]),
          reads=["cc1_out"], writes=["rowG"], dma=True)
        E("vector", lambda h: h.tensor_add(rowA[0:1, 0:C], rowG0[0:1, 0:C],
                                           rowG1[0:1, 0:C]),
          reads=["rowG"], writes=["rowA"])
        E("vector", lambda h: h.tensor_copy(rowB[:], rowA[0:1, 0:C]),
          reads=["rowA"], writes=["rowB"])
        bcast_row_to(bb, "bb")

        for cc in range(8):
            me = "gpsimd" if cc % 2 == 0 else "vector"
            E(me, lambda h, cc=cc: h.tensor_mul(
                scr[cc % 2][:], aw[cc][:], bb[:]),
              reads=[("aw", cc), "bb"], writes=[("scr", cc % 2)])
            E("vector", lambda h, cc=cc: h.reduce_sum(
                t1c[:, cc:cc + 1], scr[cc % 2][:], axis=AX.X),
              reads=[("scr", cc % 2)], writes=["t1c"])
        E("vector", lambda h: h.tensor_scalar_mul(t1c[:], t1c[:],
                                                  float(C) ** (-0.5)),
          reads=["t1c"], writes=["t1c"])

        def softmax_cols(src, nsrc, dstb, key):
            E("vector", lambda h: h.reduce_max(tiny[:, 0:1], src[:, 0:nsrc],
                                               axis=AX.X),
              reads=[key], writes=[("tiny", 0)])
            col_to_cell(tiny[:, 0:1], ("tiny", 0), 128, "max", negate=True)
            cell_to_col(128, 1)
            E("scalar", lambda h: h.activation(src[:, 0:nsrc], src[:, 0:nsrc],
                                               AF.Exp, bias=tiny[:, 1:2]),
              reads=[key, ("tiny", 0)], writes=[key])
            E("vector", lambda h: h.reduce_sum(tiny[:, 2:3], src[:, 0:nsrc],
                                               axis=AX.X),
              reads=[key], writes=[("tiny", 0)])
            col_to_cell(tiny[:, 2:3], ("tiny", 0), 129, "sum")
            E("vector", lambda h: h.reciprocal(mrow[0:1, 130:131],
                                               mrow[0:1, 129:130]),
              reads=["mrow"], writes=["mrow"])
            cell_to_col(130, 3)
            E("vector", lambda h: h.tensor_scalar_mul(
                src[:, 0:nsrc], src[:, 0:nsrc], tiny[:, 3:4]),
              reads=[key, ("tiny", 0)], writes=[key])
            E("vector", lambda h: h.tensor_copy(dstb[:, 0:nsrc],
                                                src[:, 0:nsrc]),
              reads=[key], writes=["colb"])

        softmax_cols(t1c, 8, colb, "t1c")
        load_aw(2)

        def wt_mm(h):
            last = None
            for cc in range(8):
                for hf in range(2):
                    last = h.matmul(
                        psRow[0:1, 512 * hf:512 * (hf + 1)],
                        colb[:, cc:cc + 1],
                        aw[cc][:, 512 * hf:512 * (hf + 1)],
                        start=(cc == 0), stop=True, skip_group_check=True)
            return last
        E("tensor", wt_mm,
          reads=["colb"] + [("aw", cc) for cc in range(8)],
          writes=["psRow"])
        E("scalar", lambda h: h.copy(rowA[0:1, 0:C], psRow[:]),
          reads=["psRow"], writes=["rowA"])
        E("vector", lambda h: h.tensor_copy(rowB[:], rowA[0:1, 0:C]),
          reads=["rowA"], writes=["rowB"])
        bcast_row_to(bb, "bb")

        for o in range(NQC):
            me = "gpsimd" if o % 2 == 0 else "vector"
            E(me, lambda h, o=o: h.tensor_mul(
                scr[o % 2][:], ln0[o][:], bb[:]),
              reads=[("ln0", o), "bb"], writes=[("scr", o % 2)])
            E("vector", lambda h, o=o: h.reduce_sum(
                scol[:, o:o + 1], scr[o % 2][:], axis=AX.X),
              reads=[("scr", o % 2)], writes=["scol"])
        E("vector", lambda h: h.reduce_max(tiny[:, 0:1], scol[:], axis=AX.X),
          reads=["scol"], writes=[("tiny", 0)])
        col_to_cell(tiny[:, 0:1], ("tiny", 0), 131, "max")
        col_to_cell(tiny[:, 0:1], ("tiny", 0), 132, "max", negate=True)
        cell_to_col(132, 1)
        E("scalar", lambda h: h.activation(scol[:], scol[:], AF.Exp,
                                           bias=tiny[:, 1:2]),
          reads=["scol", ("tiny", 0)], writes=["scol"])
        E("vector", lambda h: h.reduce_sum(tiny[:, 2:3], scol[:], axis=AX.X),
          reads=["scol"], writes=[("tiny", 0)])
        col_to_cell(tiny[:, 2:3], ("tiny", 0), 133, "sum")
        E("vector", lambda h: h.tensor_copy(colb[:], scol[:]),
          reads=["scol"], writes=["colb"])

        def p_mm(h):
            last = None
            for o in range(NQC):
                for hf in range(2):
                    last = h.matmul(
                        psRow[0:1, 512 * hf:512 * (hf + 1)],
                        colb[:, o:o + 1], ln0[o][:, 512 * hf:512 * (hf + 1)],
                        start=(o == 0), stop=True, skip_group_check=True)
            return last
        E("tensor", p_mm,
          reads=["colb"] + [("ln0", o) for o in range(NQC)], writes=["psRow"])
        E("scalar", lambda h: h.copy(rowA[0:1, 0:C], psRow[:]),
          reads=["psRow"], writes=["rowA"])
        E("vector", lambda h: h.tensor_copy(rowA[0:1, C:C + 1],
                                            mrow[0:1, 131:132]),
          reads=["mrow"], writes=["rowA"])
        E("vector", lambda h: h.tensor_copy(rowA[0:1, C + 1:C + 2],
                                            mrow[0:1, 133:134]),
          reads=["mrow"], writes=["rowA"])
        E("scalar", lambda h: h.dma_start(out=cc2_in[:, :],
                                          in_=rowA[0:1, 0:C + 8]),
          reads=["rowA"], writes=["cc2_in"], dma=True)
        E("gpsimd", lambda h: h.collective_compute(
            "AllGather", ALU.bypass, replica_groups=RG_PAIRS,
            ins=[cc2_in.ap().opt()], outs=[cc2_out.ap().opt()]),
          reads=["cc2_in"], writes=["cc2_out"])
        E("sync", lambda h: h.dma_start(out=rowG0[:, :],
                                        in_=cc2_out[0:1, :]),
          reads=["cc2_out"], writes=["rowG"], dma=True)
        E("sync", lambda h: h.dma_start(out=rowG1[:, :],
                                        in_=cc2_out[1:2, :]),
          reads=["cc2_out"], writes=["rowG"], dma=True)
        E("vector", lambda h: h.tensor_max(
            mrow[0:1, 134:135], rowG0[0:1, C:C + 1], rowG1[0:1, C:C + 1]),
          reads=["rowG"], writes=["mrow"])
        for i, rgi in ((0, 0), (1, 1)):
            rg = (rowG0, rowG1)[rgi]
            E("vector", lambda h, rg=rg, i=i: h.tensor_sub(
                mrow[0:1, 135 + i:136 + i], rg[0:1, C:C + 1],
                mrow[0:1, 134:135]),
              reads=["rowG", "mrow"], writes=["mrow"])
            E("scalar", lambda h, i=i: h.activation(
                mrow[0:1, 135 + i:136 + i], mrow[0:1, 135 + i:136 + i],
                AF.Exp),
              reads=["mrow"], writes=["mrow"])
        E("vector", lambda h: h.tensor_mul(
            mrow[0:1, 137:138], mrow[0:1, 135:136], rowG0[0:1, C + 1:C + 2]),
          reads=["mrow", "rowG"], writes=["mrow"])
        E("vector", lambda h: h.scalar_tensor_tensor(
            mrow[0:1, 137:138], rowG1[0:1, C + 1:C + 2], mrow[0:1, 136:137],
            mrow[0:1, 137:138], op0=ALU.mult, op1=ALU.add),
          reads=["rowG", "mrow"], writes=["mrow"])
        E("vector", lambda h: h.reciprocal(mrow[0:1, 138:139],
                                           mrow[0:1, 137:138]),
          reads=["mrow"], writes=["mrow"])
        E("vector", lambda h: h.tensor_scalar_mul(
            rowA[0:1, 0:C], rowG0[0:1, 0:C], mrow[0:1, 135:136]),
          reads=["rowG", "mrow"], writes=["rowA"])
        E("vector", lambda h: h.scalar_tensor_tensor(
            rowA[0:1, 0:C], rowG1[0:1, 0:C], mrow[0:1, 136:137],
            rowA[0:1, 0:C], op0=ALU.mult, op1=ALU.add),
          reads=["rowG", "mrow"], writes=["rowA"])
        E("vector", lambda h: h.tensor_copy(rowB[:], rowA[0:1, 0:C]),
          reads=["rowA"], writes=["rowB"])
        bcast_row_to(bb, "bb")
        for cc in range(8):
            me = "gpsimd" if cc % 2 == 0 else "vector"
            E(me, lambda h, cc=cc: h.tensor_mul(
                scr[cc % 2][:], aw[cc][:], bb[:]),
              reads=[("aw", cc), "bb"], writes=[("scr", cc % 2)])
            E("vector", lambda h, cc=cc: h.reduce_sum(
                avc[:, cc:cc + 1], scr[cc % 2][:], axis=AX.X),
              reads=[("scr", cc % 2)], writes=["avc"])
        cell_to_col(138, 4)
        E("vector", lambda h: h.tensor_scalar_mul(avc[:], avc[:],
                                                  tiny[:, 4:5]),
          reads=["avc", ("tiny", 0)], writes=["avc"])
        E("vector", lambda h: h.reduce_sum(tiny[:, 0:1], avc[:], axis=AX.X),
          reads=["avc"], writes=[("tiny", 0)])
        col_to_cell(tiny[:, 0:1], ("tiny", 0), 139, "sum")
        E("vector", lambda h: h.tensor_scalar_mul(
            mrow[0:1, 139:140], mrow[0:1, 139:140], 1.0 / C),
          reads=["mrow"], writes=["mrow"])
        E("scalar", lambda h: h.square(scr2[:, 0:8], avc[:]),
          reads=["avc"], writes=["scr2"])
        E("vector", lambda h: h.reduce_sum(tiny[:, 2:3], scr2[:, 0:8],
                                           axis=AX.X),
          reads=["scr2"], writes=[("tiny", 0)])
        col_to_cell(tiny[:, 2:3], ("tiny", 0), 140, "sum")
        E("vector", lambda h: h.tensor_scalar_mul(
            mrow[0:1, 140:141], mrow[0:1, 140:141], 1.0 / C),
          reads=["mrow"], writes=["mrow"])
        E("vector", lambda h: h.tensor_mul(
            mrow[0:1, 141:142], mrow[0:1, 139:140], mrow[0:1, 139:140]),
          reads=["mrow"], writes=["mrow"])
        E("vector", lambda h: h.tensor_sub(
            mrow[0:1, 140:141], mrow[0:1, 140:141], mrow[0:1, 141:142]),
          reads=["mrow"], writes=["mrow"])
        E("scalar", lambda h: h.activation(
            mrow[0:1, 140:141], mrow[0:1, 140:141], AF.Sqrt,
            bias=epsc[0:1, 0:1]),
          reads=["mrow"], writes=["mrow"])
        E("vector", lambda h: h.reciprocal(mrow[0:1, 140:141],
                                           mrow[0:1, 140:141]),
          reads=["mrow"], writes=["mrow"])
        cell_to_col(139, 5)
        cell_to_col(140, 6)
        E("vector", lambda h: h.tensor_scalar(
            vec8[:], avc[:], tiny[:, 5:6], tiny[:, 6:7],
            op0=ALU.subtract, op1=ALU.mult),
          reads=["avc", ("tiny", 0)], writes=["vec8"])

        for cc in range(8):
            E("tensor", lambda h, cc=cc: h.transpose(
                psSm[0:1, 0:128], vec8[:, cc:cc + 1], idf[:]),
              reads=["vec8"], writes=["psSm"])
            E("scalar", lambda h: h.copy(mrow[0:1, 0:128],
                                         psSm[0:1, 0:128]),
              reads=["psSm"], writes=["mrow"])
            E("tensor", lambda h, cc=cc: h.matmul(
                psB2[:, 128 * (cc % 4):128 * (cc % 4 + 1)], onesf[:],
                mrow[0:1, 0:128], start=True, stop=True,
                skip_group_check=True),
              reads=["mrow", "onesf"], writes=["psB2"])
            E("scalar", lambda h, cc=cc: h.copy(
                vecb[:, 128 * cc:128 * (cc + 1)],
                psB2[:, 128 * (cc % 4):128 * (cc % 4 + 1)]),
              reads=["psB2"], writes=["vecb"])

        def pass3_tile(i):
            sl = i % 2
            tb = 16 * (i % 2)
            E("sync", lambda h: h.dma_start(
                out=xt[sl][:], in_=x_in[128 * i:128 * (i + 1), :]),
              writes=[("xt", sl)], dma=True)
            E("gpsimd", lambda h: h.tensor_add(scr[sl][:], xt[sl][:],
                                               vecb[:]),
              reads=[("xt", sl), "vecb"], writes=[("scr", sl)])
            E("vector", lambda h, tb=tb: h.bn_stats(
                tiny[:, tb:tb + 6], scr[sl][:, 0:512]),
              reads=[("scr", sl)], writes=[("tiny", tb)])
            E("vector", lambda h, tb=tb: h.bn_stats(
                tiny[:, tb + 6:tb + 12], scr[sl][:, 512:C]),
              reads=[("scr", sl)], writes=[("tiny", tb)])
            E("vector", lambda h, tb=tb: h.bn_aggr(
                tiny[:, tb + 12:tb + 14], tiny[:, tb:tb + 12]),
              reads=[("tiny", tb)], writes=[("tiny", tb)])
            E("scalar", lambda h, tb=tb: h.activation(
                tiny[:, tb + 13:tb + 14], tiny[:, tb + 13:tb + 14],
                AF.Sqrt, bias=epsc[:, 0:1]),
              reads=[("tiny", tb)], writes=[("tiny", tb)])
            E("vector", lambda h, tb=tb: h.reciprocal(
                tiny[:, tb + 13:tb + 14], tiny[:, tb + 13:tb + 14]),
              reads=[("tiny", tb)], writes=[("tiny", tb)])
            E("vector", lambda h, tb=tb: h.tensor_scalar(
                scr[sl][:], scr[sl][:], tiny[:, tb + 12:tb + 13],
                tiny[:, tb + 13:tb + 14], op0=ALU.subtract, op1=ALU.mult),
              reads=[("scr", sl), ("tiny", tb)], writes=[("scr", sl)])
            for cc in range(8):
                sl2 = cc % 2
                E("tensor", lambda h, cc=cc, sl2=sl2: h.transpose(
                    psT[sl2][:, 0:128], scr[sl][:, 128 * cc:128 * (cc + 1)],
                    idf[:]),
                  reads=[("scr", sl)], writes=[("psT", sl2)])
                ee = "vector" if cc >= 6 else "scalar"
                E(ee, lambda h, cc=cc, sl2=sl2: h.copy(
                    xs[cc][:, 128 * i:128 * (i + 1)], psT[sl2][:, 0:128])
                  if cc < 6 else h.tensor_copy(
                      xs[cc][:, 128 * i:128 * (i + 1)], psT[sl2][:, 0:128]),
                  reads=[("psT", sl2)], writes=[("x", cc)])

        for i in range(NXT):
            pass3_tile(i)
        for cc in range(8):
            E("vector", lambda h, cc=cc: h.tensor_scalar_mul(
                xs[cc][:, 0:QOFF], xs[cc][:, 0:QOFF], gate[:, 0:1]),
              reads=[("x", cc), "gate"], writes=[("x", cc)])
        if "y2" in dbg:
            d = dbg_t("y2", [8 * 128, XW])
            for cc in range(8):
                E("sync", lambda h, cc=cc, d=d: h.dma_start(
                    out=d[128 * cc:128 * (cc + 1), :], in_=xs[cc][:]),
                  reads=[("x", cc)], dma=True)
        p.barrier()


def build_layer(nc, p, es, l, xs, masks, bigrow, gate, idb, idf, onesb,
                onesf, onecolb, onef1, epsc, onesbb, bcols, qsb, ksb,
                tabs_in, wqkv, qsc, ksc, wo, lng, lnb,
                w1, b1, w2, b2, cch_in, cch_out, dbg, dbg_t):
    E = p.emit
    half_chunks = [list(range(0, 11)), list(range(11, NHC))]

    with contextlib.ExitStack() as ph:
        Sa = lambda n, *a: ph.enter_context(nc.sbuf_tensor(_u(n), *a))
        PSa = lambda n, *a: ph.enter_context(nc.psum_tensor(_u(n), *a))
        brow = Sa(f"brow{l}", [1, 1024 + 128], F32)
        psCol = PSa(f"psCol{l}", [128, 128], F32)
        pieces = []
        for pi in range(3):
            off0 = 1024 * pi
            n = min(1024, HID - off0)
            cols = []
            c0 = off0 // 128
            for c in range(c0, min(NHC, (off0 + n + 127) // 128)):
                w = min(128, HID - 128 * c)
                cols.append((c, 128 * c - off0, w))
            pieces.append((b1[l, off0:off0 + n].rearrange(
                "(a n) -> a n", a=1), n, cols))
        for pi in range(3):
            off0 = 1024 * pi
            n = min(1024, HID - off0)
            cols = []
            c0 = off0 // 128
            for c in range(c0, min(NHC, (off0 + n + 127) // 128)):
                w = min(128, HID - 128 * c)
                cols.append((22 + c, 128 * c - off0, w))
            pieces.append((b1[l, HID + off0:HID + off0 + n].rearrange(
                "(a n) -> a n", a=1), n, cols))
        for bix, srct in ((44, b2), (52, lng), (60, lnb)):
            cols = [(bix + c, 128 * c, 128) for c in range(8)]
            pieces.append((srct[l, :].rearrange("(a n) -> a n", a=1), C,
                           cols))
        for srcap, n, cols in pieces:
            E("sync", lambda h, srcap=srcap, n=n: h.dma_start(
                out=brow[0:1, 0:n], in_=srcap),
              writes=["brow"], dma=True)

            def colmm(h, cols=cols):
                last = None
                for k, (col, off, w) in enumerate(cols):
                    last = h.matmul(psCol[0:w, k:k + 1],
                                    brow[0:1, off:off + w], onef1[:],
                                    start=True, stop=True,
                                    skip_group_check=True)
                return last
            E("tensor", colmm, reads=["brow", "onef1"], writes=["psCol"])
            for k, (col, off, w) in enumerate(cols):
                E("scalar", lambda h, col=col, k=k, w=w: h.copy(
                    bcols[0:w, col:col + 1], psCol[0:w, k:k + 1]),
                  reads=["psCol"], writes=["bcols"])
        for dst, srct, key in ((qsb, qsc, "qsb"), (ksb, ksc, "ksb")):
            E("sync", lambda h, srct=srct: h.dma_start(
                out=brow[0:1, 1024:1024 + DH],
                in_=srct[l, :].rearrange("(a n) -> a n", a=1)),
              writes=["brow"], dma=True)
            E("tensor", lambda h: h.matmul(
                psCol[:, 0:64], onesf[:], brow[0:1, 1024:1024 + DH],
                start=True, stop=True, skip_group_check=True),
              reads=["brow", "onesf"], writes=["psCol"])
            E("scalar", lambda h, dst=dst: h.copy(dst[:], psCol[:, 0:64]),
              reads=["psCol"], writes=[key])
        p.barrier()

    BC = lambda c: ("bcols",)
    b1a_col = lambda c, w=128: bcols[0:w, c:c + 1]
    b1g_col = lambda c, w=128: bcols[0:w, 22 + c:22 + c + 1]
    b2_col = lambda c: bcols[:, 44 + c:44 + c + 1]
    lng_col = lambda c: bcols[:, 52 + c:52 + c + 1]
    lnb_col = lambda c: bcols[:, 60 + c:60 + c + 1]

    def load_wqkv(g):
        for r, cbase in enumerate((256 * g, INNER + 256 * g,
                                   2 * INNER + 256 * g)):
            E("gpsimd", lambda h, r=r, cbase=cbase: h.dma_start(
                out=wqkv_b[:].rearrange("p (ic n) -> p ic n", ic=8)
                [:, :, 256 * r:256 * (r + 1)],
                in_=wqkv[l, :, cbase:cbase + 256].rearrange(
                    "(ic p) n -> p ic n", p=128)),
              writes=["wqkv_b"], dma=True)

    load_wqkv(0)
    E("gpsimd", lambda h: h.dma_start(
        out=wo_b[:].rearrange("p (ic n) -> p ic n", ic=4),
        in_=wo[l, :, :].rearrange("(ic p) n -> p ic n", p=128)),
      writes=["wo_b"], dma=True)

    oscope = contextlib.ExitStack()
    ofm = [oscope.enter_context(nc.sbuf_tensor(_u(f"ofm{l}_{pp}"),
                                               [128, OWN], BF16))
           for pp in range(4)]
    wqkv_b = oscope.enter_context(
        nc.sbuf_tensor(_u("wqkv_b"), [128, 8 * 768], BF16))
    wo_b = oscope.enter_context(
        nc.sbuf_tensor(_u("wo_b"), [128, 4 * C], BF16))
    with contextlib.ExitStack() as lph:
        Sl = lambda n, *a: lph.enter_context(nc.sbuf_tensor(_u(n), *a))
        tabs = [Sl(f"tab{l}_{i}", [128, NKC * DH], F32) for i in range(4)]
        for i in range(4):
            E("sync", lambda h, d=tabs[i], s=tabs_in[i]:
              h.dma_start(out=d[:], in_=s), writes=[("tab", i)], dma=True)
        for ti, scale_t, skey in ((0, qsb, "qsb"), (2, ksb, "ksb")):
            E("vector", lambda h, ti=ti, scale_t=scale_t: h.tensor_mul(
                tabs[ti][:].rearrange("p (j d) -> p j d", d=DH),
                tabs[ti][:].rearrange("p (j d) -> p j d", d=DH),
                scale_t[:].rearrange("p (o d) -> p o d", o=1)
                .broadcast_to([128, NKC, DH])),
              reads=[("tab", ti), skey], writes=[("tab", ti)])
            E("vector", lambda h, ti=ti, scale_t=scale_t: h.tensor_mul(
                tabs[ti + 1][:].rearrange("p (j d) -> p j d", d=DH)
                [:, :, 0:32],
                tabs[ti + 1][:].rearrange("p (j d) -> p j d", d=DH)
                [:, :, 0:32],
                scale_t[:, 32:64].rearrange("p (o d) -> p o d", o=1)
                .broadcast_to([128, NKC, 32])),
              reads=[("tab", ti + 1), skey], writes=[("tab", ti + 1)])
            E("vector", lambda h, ti=ti, scale_t=scale_t: h.tensor_mul(
                tabs[ti + 1][:].rearrange("p (j d) -> p j d", d=DH)
                [:, :, 32:64],
                tabs[ti + 1][:].rearrange("p (j d) -> p j d", d=DH)
                [:, :, 32:64],
                scale_t[:, 0:32].rearrange("p (o d) -> p o d", o=1)
                .broadcast_to([128, NKC, 32])),
              reads=[("tab", ti + 1), skey], writes=[("tab", ti + 1)])

        for g in range(2):
            qkv_and_attention(nc, p, lph, l, g, xs, tabs, masks, bigrow,
                              idb, idf, onesb, onesf, qsb, ksb, wqkv_b,
                              ofm, load_wqkv, dbg, dbg_t)

    with contextlib.ExitStack() as ph:
        PSa = lambda n, *a: ph.enter_context(nc.psum_tensor(_u(n), *a))
        psWO = [PSa(f"psWO{s}", [128, 512], F32) for s in range(2)]
        for tt in range(4):
            for oc in range(8):
                sl = (tt * 8 + oc) % 2

                def wo_fn(h, tt=tt, oc=oc, sl=sl):
                    last = None
                    for ic in range(4):
                        last = h.matmul(
                            psWO[sl][:, :],
                            wo_b[:, 1024 * ic + 128 * oc:
                                 1024 * ic + 128 * (oc + 1)],
                            ofm[ic][:, 512 * tt:512 * (tt + 1)],
                            start=(ic == 0), stop=True,
                            skip_group_check=True)
                    return last
                E("tensor", wo_fn,
                  reads=["wo_b"] + [("ofm", ic) for ic in range(4)],
                  writes=[("psWO", sl)])
                E("vector", lambda h, tt=tt, oc=oc, sl=sl: h.tensor_add(
                    xs[oc][:, QOFF + 512 * tt:QOFF + 512 * (tt + 1)],
                    xs[oc][:, QOFF + 512 * tt:QOFF + 512 * (tt + 1)],
                    psWO[sl][:, :]),
                  reads=[("psWO", sl), ("x", oc)], writes=[("x", oc)])
        p.barrier()
    oscope.close()

    with contextlib.ExitStack() as ph:
        Sa = lambda n, *a: ph.enter_context(nc.sbuf_tensor(_u(n), *a))
        PSa = lambda n, *a: ph.enter_context(nc.psum_tensor(_u(n), *a))
        lnt = [Sa(f"lnt{c}", [128, OWN], BF16) for c in range(8)]
        hg = [Sa(f"hg{k}", [128, OWN], BF16) for k in range(8)]
        w2h = [Sa(f"w2h{k}", [128, C], BF16) for k in range(6)]
        wa_b = [Sa(f"wa{s}", [128, 8 * 128], BF16) for s in range(4)]
        wg_b = [Sa(f"wg{s}", [128, 8 * 128], BF16) for s in range(4)]
        xbf = [Sa(f"xbf{s}", [128, 1024], BF16) for s in range(2)]
        sqf = [Sa(f"sqf{s}", [128, 1024], BF16) for s in range(2)]
        gel = [Sa("gel0", [128, 512], BF16)] * 2
        scrF = Sa("scrF", [128, 1024], F32)
        scrM = Sa("scrM", [128, 1024], F32)
        scrF2 = Sa("scrF2", [128, 1024], F32)
        scrR2 = Sa("scrR2", [128, 1024], F32)
        psBig = [PSa(f"psBig{s}", [128, 1024], F32) for s in range(2)]
        psW1 = [PSa(f"psW1{s}", [128, 512], F32) for s in range(2)]
        psW2 = [PSa(f"psW2{s}", [128, 512], F32) for s in range(2)]

        for tgh in range(2):
            base = QOFF + 1024 * tgh
            for c in range(8):
                sl = c % 2
                sh = 0 if c < 4 else -1
                rds = [("x", c), ("xh", c)] if (sh and tgh == 0) \
                    else [("x", c)]
                if c % 2 == 0:
                    E("vector", lambda h, c=c, sl=sl, sh=sh, base=base:
                      h.tensor_copy(
                          xbf[sl][:], xs[c][:, base + sh:base + sh + 1024]),
                      reads=rds, writes=[("xbf", sl)])
                else:
                    E("scalar", lambda h, c=c, sl=sl, sh=sh, base=base:
                      h.copy(
                          xbf[sl][:], xs[c][:, base + sh:base + sh + 1024]),
                      reads=rds, writes=[("xbf", sl)])
                E("scalar", lambda h, sl=sl: h.square(sqf[sl][:],
                                                      xbf[sl][:]),
                  reads=[("xbf", sl)], writes=[("sqf", sl)])

                def stat_fn(h, c=c, sl=sl, tgh=tgh):
                    last = None
                    for hf in range(2):
                        if tgh == 0:
                            ps_s = psBig[0][:, 512 * hf:512 * (hf + 1)]
                            ps_q = psBig[1][:, 512 * hf:512 * (hf + 1)]
                        else:
                            ps_s = psW1[hf][:, :]
                            ps_q = psW2[hf][:, :]
                        h.matmul(ps_s, onesbb[:],
                                 xbf[sl][:, 512 * hf:512 * (hf + 1)],
                                 start=(c == 0), stop=True,
                                 skip_group_check=True)
                        last = h.matmul(
                            ps_q, onesbb[:],
                            sqf[sl][:, 512 * hf:512 * (hf + 1)],
                            start=(c == 0), stop=True, skip_group_check=True)
                    return last
                wkeys = [("psBig", 0), ("psBig", 1)] if tgh == 0 else \
                    [("psW1", 0), ("psW1", 1), ("psW2", 0), ("psW2", 1)]
                E("tensor", stat_fn,
                  reads=[("xbf", sl), ("sqf", sl), "onesbb"], writes=wkeys)

        E("vector", lambda h: h.tensor_scalar_mul(
            scrF[:], psBig[0][:, :], 1.0 / C),
          reads=[("psBig", 0)], writes=["scrF"])
        E("vector", lambda h: h.tensor_mul(scrM[:], scrF[:], scrF[:]),
          reads=["scrF"], writes=["scrM"])
        E("vector", lambda h: h.scalar_tensor_tensor(
            psBig[1][:, :], psBig[1][:, :], 1.0 / C, scrM[:],
            op0=ALU.mult, op1=ALU.subtract),
          reads=[("psBig", 1), "scrM"], writes=[("psBig", 1)])
        E("scalar", lambda h: h.activation(
            psBig[1][:, :], psBig[1][:, :], AF.Sqrt, bias=epsc[:, 0:1]),
          reads=[("psBig", 1)], writes=[("psBig", 1)])
        E("vector", lambda h: h.reciprocal(psBig[1][:, :], psBig[1][:, :]),
          reads=[("psBig", 1)], writes=[("psBig", 1)])
        for hf in range(2):
            E("vector", lambda h, hf=hf: h.tensor_scalar_mul(
                scrF2[:, 512 * hf:512 * (hf + 1)], psW1[hf][:, :], 1.0 / C),
              reads=[("psW1", hf)], writes=["scrF2"])
        E("vector", lambda h: h.tensor_mul(scrM[:], scrF2[:], scrF2[:]),
          reads=["scrF2"], writes=["scrM"])
        for hf in range(2):
            E("vector", lambda h, hf=hf: h.scalar_tensor_tensor(
                psW2[hf][:, :], psW2[hf][:, :], 1.0 / C,
                scrM[:, 512 * hf:512 * (hf + 1)],
                op0=ALU.mult, op1=ALU.subtract),
              reads=[("psW2", hf), "scrM"], writes=[("psW2", hf)])
            E("scalar", lambda h, hf=hf: h.activation(
                psW2[hf][:, :], psW2[hf][:, :], AF.Sqrt, bias=epsc[:, 0:1]),
              reads=[("psW2", hf)], writes=[("psW2", hf)])
            E("vector", lambda h, hf=hf: h.reciprocal(
                scrR2[:, 512 * hf:512 * (hf + 1)], psW2[hf][:, :]),
              reads=[("psW2", hf)], writes=["scrR2"])

        for tgh in range(2):
            base = QOFF + 1024 * tgh
            mu, rs = (scrF, psBig[1]) if tgh == 0 else (scrF2, scrR2)
            mukey, rskey = ("scrF", ("psBig", 1)) if tgh == 0 else \
                ("scrF2", "scrR2")
            for c in range(8):
                sh = 0 if c < 4 else -1
                rds2 = [("x", c), ("xh", c), mukey] if (sh and tgh == 0) \
                    else [("x", c), mukey]
                E("vector", lambda h, c=c, sh=sh, base=base, mu=mu:
                  h.tensor_sub(
                      scrM[:], xs[c][:, base + sh:base + sh + 1024], mu[:]),
                  reads=rds2, writes=["scrM"])
                E("vector", lambda h, c=c, rs=rs: h.tensor_mul(
                    scrM[:], scrM[:], rs[:, :]),
                  reads=["scrM", rskey], writes=["scrM"])
                E("vector", lambda h, c=c, tgh=tgh: h.tensor_scalar(
                    lnt[c][:, 1024 * tgh:1024 * (tgh + 1)], scrM[:],
                    lng_col(c), lnb_col(c), op0=ALU.mult, op1=ALU.add),
                  reads=["scrM", "bcols"], writes=[("lnt", c, tgh)])

        if f"lnt{l}" in dbg:
            d = dbg_t(f"lnt{l}", [8 * 128, OWN])
            for c in range(8):
                E("gpsimd", lambda h, c=c, d=d: h.dma_start(
                    out=d[128 * c:128 * (c + 1), :], in_=lnt[c][:]),
                  reads=[("lnt", c, 0), ("lnt", c, 1)], dma=True)

        chunks = list(range(NHC))

        def load_chunk(k, hc):
            w = min(128, HID - 128 * hc)
            wsl = k % 4
            E("gpsimd", lambda h, hc=hc, w=w, wsl=wsl: h.dma_start(
                out=wa_b[wsl][:, 0:8 * w].rearrange(
                    "p (ic n) -> p ic n", ic=8),
                in_=w1[l, :, 128 * hc:128 * hc + w].rearrange(
                    "(ic p) n -> p ic n", p=128)),
              writes=[("wa_b", wsl)], dma=True)
            E("gpsimd", lambda h, hc=hc, w=w, wsl=wsl: h.dma_start(
                out=wg_b[wsl][:, 0:8 * w].rearrange(
                    "p (ic n) -> p ic n", ic=8),
                in_=w1[l, :, HID + 128 * hc:HID + 128 * hc + w]
                .rearrange("(ic p) n -> p ic n", p=128)),
              writes=[("wg_b", wsl)], dma=True)


        def emit_w2_quarter(qi, hcs, is_last):
            for k2, hc2 in enumerate(hcs):
                w2w = min(128, HID - 128 * hc2)
                E("gpsimd", lambda h, hc2=hc2, w2w=w2w, k2=k2: h.dma_start(
                    out=w2h[k2][0:w2w, :],
                    in_=w2[l, 128 * hc2:128 * hc2 + w2w, :]),
                  writes=[("w2h", k2)], dma=True)
            for sub in (3, 2, 1, 0):
                for oc in range(8):
                    sl2 = oc % 2

                    def w2_fn(h, sub=sub, oc=oc, sl2=sl2, hcs=hcs):
                        last = None
                        for k2, hc2 in enumerate(hcs):
                            w2w = min(128, HID - 128 * hc2)
                            last = h.matmul(
                                psW2[sl2][:, :],
                                w2h[k2][0:w2w, 128 * oc:128 * (oc + 1)],
                                hg[hc2 % 8][0:w2w,
                                            512 * sub:512 * (sub + 1)],
                                start=(k2 == 0), stop=True,
                                skip_group_check=True)
                        return last
                    E("tensor", w2_fn,
                      reads=[("w2h", k2) for k2 in range(len(hcs))] +
                            [("hg", hc2 % 8) for hc2 in hcs],
                      writes=[("psW2", sl2)])
                    if not is_last:
                        E("vector", lambda h, oc=oc, sub=sub, sl2=sl2:
                          h.tensor_add(
                              xs[oc][:, QOFF + 512 * sub:
                                     QOFF + 512 * (sub + 1)],
                              xs[oc][:, QOFF + 512 * sub:
                                     QOFF + 512 * (sub + 1)],
                              psW2[sl2][:, :]),
                          reads=[("psW2", sl2), ("x", oc)],
                          writes=[("x", oc)])
                    else:
                        E("vector", lambda h, oc=oc, sub=sub, sl2=sl2:
                          h.scalar_tensor_tensor(
                              xs[oc][:, QOFF + 512 * sub:
                                     QOFF + 512 * (sub + 1)],
                              psW2[sl2][:, :], b2_col(oc),
                              xs[oc][:, QOFF + 512 * sub:
                                     QOFF + 512 * (sub + 1)],
                              op0=ALU.add, op1=ALU.add),
                          reads=[("psW2", sl2), ("x", oc), "bcols"],
                          writes=[("x", oc)])
                if is_last and sub == 2 and l == 0:
                    for c in range(8):
                        E("scalar", lambda h, c=c: h.dma_start(
                            out=cch_in[128 * c:128 * (c + 1), :],
                            in_=xs[c][:, XW - HALO:XW]),
                          reads=[("x", c)], writes=["cch_in"], dma=True)
                    E("gpsimd", lambda h: h.collective_compute(
                        "AllGather", ALU.bypass, replica_groups=RG_PAIRS,
                        ins=[cch_in.ap().opt()], outs=[cch_out.ap().opt()]),
                      reads=["cch_in"], writes=["cch_out"])

        load_chunk(0, 0)
        load_chunk(1, 1)
        for k, hc in enumerate(chunks):
            w = min(128, HID - 128 * hc)
            wsl = k % 4
            kk = k % 8
            if k + 2 < NHC:
                load_chunk(k + 2, hc + 2)
            for sub in range(4):
                par = (4 * k + sub) % 2
                if par == 0:
                    psa, psakey = psW1[0][0:w, :], ("psW1", 0)
                    psg, psgkey = psW1[1][0:w, :], ("psW1", 1)
                else:
                    psa, psakey = psBig[0][0:w, 0:512], ("psBig", 0)
                    psg, psgkey = psBig[0][0:w, 512:1024], ("psBig", 0)

                def w1a_fn(h, w=w, wsl=wsl, sub=sub, psa=psa):
                    last = None
                    for c in range(8):
                        last = h.matmul(
                            psa, wa_b[wsl][:, w * c:w * (c + 1)],
                            lnt[c][:, 512 * sub:512 * (sub + 1)],
                            start=(c == 0), stop=True,
                            skip_group_check=True)
                    return last

                def w1g_fn(h, w=w, wsl=wsl, sub=sub, psg=psg):
                    last = None
                    for c in range(8):
                        last = h.matmul(
                            psg, wg_b[wsl][:, w * c:w * (c + 1)],
                            lnt[c][:, 512 * sub:512 * (sub + 1)],
                            start=(c == 0), stop=True,
                            skip_group_check=True)
                    return last
                E("tensor", w1a_fn,
                  reads=[("wa_b", wsl)] +
                        [("lnt", c, sub // 2) for c in range(8)],
                  writes=[psakey])
                E("tensor", w1g_fn,
                  reads=[("wg_b", wsl)] +
                        [("lnt", c, sub // 2) for c in range(8)],
                  writes=[psgkey])
                E("scalar", lambda h, w=w, hc=hc, sub=sub, psg=psg:
                  h.activation(gel[sub % 2][0:w, :], psg, AF.Gelu,
                               bias=b1g_col(hc, w)),
                  reads=[psgkey, "bcols"], writes=["gel"])
                E("vector", lambda h, w=w, hc=hc, kk=kk, sub=sub, psa=psa:
                  h.scalar_tensor_tensor(
                      hg[kk][0:w, 512 * sub:512 * (sub + 1)],
                      psa, b1a_col(hc, w), gel[sub % 2][0:w, :],
                      op0=ALU.add, op1=ALU.mult),
                  reads=[psakey, "gel", "bcols"],
                  writes=[("hg", kk)])

            if k == 5:
                emit_w2_quarter(0, chunks[0:6], False)
            elif k == 10:
                emit_w2_quarter(1, chunks[6:11], False)
            elif k == 16:
                emit_w2_quarter(2, chunks[11:17], False)
        emit_w2_quarter(3, chunks[17:NHC], True)
        p.barrier()

    if l == 0:
        for c in range(8):
            E("sync", lambda h, c=c: h.dma_start(
                out=xs[c][:, DEAD:DEAD + HALO],
                in_=cch_out[128 * c:128 * (c + 1), :]),
              reads=["cch_out"], writes=[("xh", c)], dma=True)
            E("vector", lambda h, c=c: h.tensor_scalar_mul(
                xs[c][:, 0:QOFF], xs[c][:, 0:QOFF], gate[:, 0:1]),
              reads=[("xh", c), "gate"], writes=[("xh", c)])
    if f"xl{l}" in dbg:
        d = dbg_t(f"xl{l}", [8 * 128, XW])
        for cc in range(8):
            E("sync", lambda h, cc=cc, d=d: h.dma_start(
                out=d[128 * cc:128 * (cc + 1), :], in_=xs[cc][:]),
              reads=[("x", cc)], dma=True)
    p.barrier()


def qkv_and_attention(nc, p, lph, l, g, xs, tabs, masks, bigrow, idb, idf,
                      onesb, onesf, qsb, ksb, wqkv_b, ofm, load_wqkv,
                      dbg, dbg_t):
    E = p.emit
    with contextlib.ExitStack() as ph:
        Sa = lambda n, *a: ph.enter_context(nc.sbuf_tensor(_u(n), *a))
        PSa = lambda n, *a: ph.enter_context(nc.psum_tensor(_u(n), *a))
        xb = [Sa(f"xb{c}", [128, 132], BF16) for c in range(16)]
        qh = [Sa(f"qh{pp}", [128, OWN], BF16) for pp in range(2)]
        kh = [Sa(f"kh{pp}", [128, KW], BF16) for pp in range(2)]
        vx = [Sa(f"vx{j}", [128, 4 * 65], BF16) for j in range(NKC)]
        qkraw = [Sa(f"qkraw{s}", [128, 512], F32) for s in range(2)]
        sqt = [Sa(f"sqt{s}", [128, 512], F32) for s in range(2)]
        t2s = [Sa(f"t2s{s}", [128, 512], F32) for s in range(2)]
        q0s = [Sa(f"q0s{s}", [128, 512], F32) for s in range(2)]
        qkt = [Sa(f"qkt{s}", [128, 512], BF16) for s in range(3)]
        nrm = [Sa(f"nrm{s}", [128, 8], F32) for s in range(2)]
        qps = contextlib.ExitStack()
        PSq = lambda n, *a: qps.enter_context(nc.psum_tensor(_u(n), *a))
        psQKV = [PSq(f"psQKV{s}", [128, 768], F32) for s in range(3)]
        psTrQ = PSq("psTrQ", [128, 128], BF16)
        psTrK = PSq("psTrK", [128, 128], BF16)

        def emit_casts(j):
            xcol = KOFF + 128 * j
            sl8 = 8 * (j % 2)
            halo = xcol - 1 < QOFF
            for c in range(8):
                rds = [("x", c), ("xh", c)] if halo else [("x", c)]
                if c % 2 == 0:
                    E("vector", lambda h, c=c, xcol=xcol, sl8=sl8:
                      h.tensor_copy(xb[sl8 + c][:, 0:129],
                                    xs[c][:, xcol - 1:xcol + 128]),
                      reads=rds, writes=[("xb", sl8 + c)])
                else:
                    E("scalar", lambda h, c=c, xcol=xcol, sl8=sl8:
                      h.copy(xb[sl8 + c][:, 0:129],
                             xs[c][:, xcol - 1:xcol + 128]),
                      reads=rds, writes=[("xb", sl8 + c)])

        def emit_mm(j):
            own = j >= 4
            ps = psQKV[j % 3]
            pskey = ("psQKV", j % 3)
            sl8 = 8 * (j % 2)
            r0 = 0 if own else 1

            def qkv_fn(h, sl8=sl8):
                last = None
                for c in range(8):
                    lhs = xb[sl8 + c][:, 1:129] if c < 4 \
                        else xb[sl8 + c][:, 0:128]
                    for r in range(r0, 3):
                        last = h.matmul(
                            ps[:, 256 * r:256 * (r + 1)], lhs,
                            wqkv_b[:, 768 * c + 256 * r:768 * c + 256 * (r + 1)],
                            start=(c == 0 and (r == r0 or r == 2)),
                            stop=True, skip_group_check=True)
                return last
            E("tensor", qkv_fn,
              reads=[("xb", sl8 + c) for c in range(8)] + ["wqkv_b"],
              writes=[pskey])

        def emit_chunk(j):
            own = j >= 4
            xcol = KOFF + 128 * j
            ps = psQKV[j % 3]
            pskey = ("psQKV", j % 3)
            r0 = 0 if own else 1
            E("scalar", lambda h: h.copy(
                vx[j][:].rearrange("p (hh d) -> p hh d", hh=4)[:, :, 0:64],
                ps[:, 512:768].rearrange("p (hh d) -> p hh d", hh=4)),
              reads=[pskey], writes=[("vx", j)])
            E("gpsimd", lambda h: h.memset(
                vx[j][:].rearrange("p (hh d) -> p hh d", d=65)[:, :, 64:65],
                1.0),
              reads=[("vx", j)], writes=[("vx", j)])
            a = 0 if own else 256
            nh = (512 - a) // 64
            sq = j % 2
            E("scalar", lambda h, a=a, sq=sq: h.square(
                sqt[sq][:, a:512], ps[:, a:512]),
              reads=[pskey], writes=[("sqt", sq)])
            E("scalar", lambda h, a=a, sq=sq: h.copy(qkraw[sq][:, a:512],
                                                     ps[:, a:512]),
              reads=[pskey], writes=[("qkraw", sq)])
            E("vector", lambda h, a=a, nh=nh, sq=sq: h.reduce_sum(
                nrm[sq][:, a // 64:8],
                sqt[sq][:, a:512].rearrange("p (hh d) -> p hh d", d=64),
                axis=AX.X),
              reads=[("sqt", sq)], writes=[("nrm", sq)])
            E("scalar", lambda h, a=a, sq=sq: h.activation(
                nrm[sq][:, a // 64:8], nrm[sq][:, a // 64:8], AF.Sqrt),
              reads=[("nrm", sq)], writes=[("nrm", sq)])
            E("vector", lambda h, a=a, sq=sq: h.tensor_scalar_max(
                nrm[sq][:, a // 64:8], nrm[sq][:, a // 64:8], 1e-12),
              reads=[("nrm", sq)], writes=[("nrm", sq)])
            E("vector", lambda h, a=a, sq=sq: h.reciprocal(
                nrm[sq][:, a // 64:8], nrm[sq][:, a // 64:8]),
              reads=[("nrm", sq)], writes=[("nrm", sq)])
            E("vector", lambda h, a=a, nh=nh, sq=sq: h.tensor_mul(
                q0s[sq][:, a:512].rearrange("p (hh d) -> p hh d", d=64),
                qkraw[sq][:, a:512].rearrange("p (hh d) -> p hh d", d=64),
                nrm[sq][:, a // 64:8].rearrange("p (hh o) -> p hh o", o=1)
                .broadcast_to([128, nh, 64])),
              reads=[("qkraw", sq), ("nrm", sq)], writes=[("q0s", sq)])

            def rot_fn(h, a, b_, ct, st, sq=sq):
                nh2 = (b_ - a) // 64
                h.tensor_tensor(
                    sqt[sq][:, a:b_].rearrange("p (hh d) -> p hh d", d=64),
                    q0s[sq][:, a:b_].rearrange("p (hh d) -> p hh d", d=64),
                    ct[:, DH * j:DH * (j + 1)].rearrange(
                        "p (o d) -> p o d", o=1).broadcast_to([128, nh2, 64]),
                    op=ALU.mult)
                h.tensor_tensor(
                    t2s[sq][:, a:b_].rearrange("p (hh d) -> p hh d", d=64)
                    [:, :, 0:32],
                    q0s[sq][:, a:b_].rearrange("p (hh d) -> p hh d", d=64)
                    [:, :, 32:64],
                    st[:, DH * j:DH * j + 32].rearrange(
                        "p (o d) -> p o d", o=1).broadcast_to([128, nh2, 32]),
                    op=ALU.mult)
                last2 = h.tensor_tensor(
                    t2s[sq][:, a:b_].rearrange("p (hh d) -> p hh d", d=64)
                    [:, :, 32:64],
                    q0s[sq][:, a:b_].rearrange("p (hh d) -> p hh d", d=64)
                    [:, :, 0:32],
                    st[:, DH * j + 32:DH * (j + 1)].rearrange(
                        "p (o d) -> p o d", o=1).broadcast_to([128, nh2, 32]),
                    op=ALU.mult)
                return last2

            if own:
                E("vector", lambda h: rot_fn(h, 0, 256, tabs[0], tabs[1]),
                  reads=[("q0s", sq), ("tab", 0), ("tab", 1)],
                  writes=[("sqt", sq), ("t2s", sq)])
            E("vector", lambda h: rot_fn(h, 256, 512, tabs[2], tabs[3]),
              reads=[("q0s", sq), ("tab", 2), ("tab", 3)],
              writes=[("sqt", sq), ("t2s", sq)])
            a0 = 0 if own else 256
            E("gpsimd", lambda h, a0=a0, j=j, sq=sq: h.tensor_add(
                qkt[j % 3][:, a0:512], sqt[sq][:, a0:512],
                t2s[sq][:, a0:512]),
              reads=[("sqt", sq), ("t2s", sq)], writes=[("qkt", j % 3)])
        def emit_transposes(j):
            own = j >= 4
            for pp in range(2):
                if own:
                    E("tensor", lambda h, pp=pp, j=j: h.transpose(
                        psTrQ[:], qkt[j % 3][:, 128 * pp:128 * (pp + 1)],
                        idb[:]),
                      reads=[("qkt", j % 3)], writes=["psTrQ"])
                    E("scalar", lambda h, pp=pp, j=j: h.copy(
                        qh[pp][:, 128 * (j - 4):128 * (j - 3)],
                        psTrQ[:]),
                      reads=["psTrQ"], writes=[("qh", pp)])
                E("tensor", lambda h, pp=pp, j=j: h.transpose(
                    psTrK[:],
                    qkt[j % 3][:, 256 + 128 * pp:256 + 128 * (pp + 1)],
                    idb[:]),
                  reads=[("qkt", j % 3)], writes=["psTrK"])
                E("scalar", lambda h, pp=pp, j=j: h.copy(
                    kh[pp][:, 128 * j:128 * (j + 1)], psTrK[:]),
                  reads=["psTrK"], writes=[("kh", pp)])

        order = [(jj + 4) % NKC for jj in range(NKC)]
        emit_casts(order[0])
        for i, j in enumerate(order):
            emit_mm(j)
            if i + 1 < len(order):
                emit_casts(order[i + 1])
            emit_chunk(j)
            if i > 1:
                emit_transposes(order[i - 2])
        emit_transposes(order[-2])
        emit_transposes(order[-1])
        p.barrier()
        qps.close()
        if g == 0:
            load_wqkv(1)

        with contextlib.ExitStack() as ph2:
            Sb = lambda n, *a: ph2.enter_context(nc.sbuf_tensor(_u(n), *a))
            PSb = lambda n, *a: ph2.enter_context(nc.psum_tensor(_u(n), *a))
            psS = PSb("psS", [128, 640], F32)
            psST = [PSb(f"psST{s}", [128, 640], F32) for s in range(2)]
            psO = [PSb(f"psO{s}", [128, 128], F32) for s in range(2)]
            PT = [Sb(f"PT{s}", [128, 640], BF16) for s in range(2)]
            negmS = Sb("negmS", [128, 2], F32)
            negmrow = [Sb(f"negmrow{s}", [1, 128], BF16) for s in range(2)]
            negmrow4 = [Sb(f"negmrow4{s}", [1, 512], BF16) for s in range(2)]
            ouq = [Sb(f"ouq{s}", [128, 128], F32) for s in range(2)]
            denr = [Sb(f"denr{s}", [1, 128], F32) for s in range(2)]

            for hh in range(4):
                hglob = 4 * g + hh
                pp, hb = hh // 2, 64 * (hh % 2)

                def av_mm(qc, hglob=hglob):
                    sl = qc % 2

                    def av_fn(h, qc=qc, sl=sl, hglob=hglob):
                        last = None
                        for i in range(5):
                            last = h.matmul(
                                psO[sl][0:65, :],
                                vx[qc + i][:, 65 * (hglob % 4):
                                           65 * (hglob % 4) + 65],
                                PT[sl][:, 128 * i:128 * (i + 1)],
                                start=(i == 0), stop=True,
                                skip_group_check=True)
                        return last
                    E("tensor", av_fn,
                      reads=[("PT", sl)] + [("vx", qc + i)
                                            for i in range(5)],
                      writes=[("psO", sl)])
                    E("scalar", lambda h, sl=sl: h.copy(
                        ouq[sl][0:65, :], psO[sl][0:65, :]),
                      reads=[("psO", sl)], writes=[("ouq", sl)])

                def av_recip(qc):
                    sl = qc % 2
                    E("vector", lambda h, sl=sl: h.reciprocal(
                        denr[sl][:], ouq[sl][64:65, :]),
                      reads=[("ouq", sl)], writes=[("denr", sl)])

                def av_norm(qc, pp=pp, hb=hb):
                    sl = qc % 2
                    tb = qc % 2
                    E("tensor", lambda h, sl=sl, tb=tb: h.matmul(
                        psO[tb][0:64, :], onesf[0:1, 0:64], denr[sl][:],
                        start=True, stop=True, skip_group_check=True),
                      reads=[("denr", sl)], writes=[("psO", tb)])
                    E("vector", lambda h, sl=sl, tb=tb, qc=qc, pp=pp,
                      hb=hb: h.tensor_mul(
                          ofm[2 * g + pp][hb:hb + 64,
                                          128 * qc:128 * (qc + 1)],
                          ouq[sl][0:64, :], psO[tb][0:64, :]),
                      reads=[("ouq", sl), ("psO", tb)],
                      writes=[("ofm", 2 * g + pp)])

                def sweep1(qc):
                    sl = qc % 2

                    def s_fn(h, qc=qc, pp=pp, hb=hb):
                        h.matmul(psS[:, 0:512],
                                 qh[pp][hb:hb + 64,
                                        128 * qc:128 * (qc + 1)],
                                 kh[pp][hb:hb + 64,
                                        128 * qc:128 * qc + 512],
                                 start=True, stop=True,
                                 skip_group_check=True)
                        h.matmul(psS[:, 512:640],
                                 qh[pp][hb:hb + 64,
                                        128 * qc:128 * (qc + 1)],
                                 kh[pp][hb:hb + 64,
                                        128 * qc + 512:128 * qc + 640],
                                 start=True, stop=True,
                                 skip_group_check=True)
                        h.matmul(psS[:, 0:128], idb[:], masks[0][:],
                                 start=False, stop=True,
                                 skip_group_check=True)
                        last = h.matmul(psS[:, 512:640], idb[:],
                                        masks[1][:], start=False,
                                        stop=True, skip_group_check=True)
                        for i in range(max(0, 4 - qc)):
                            last = h.matmul(
                                psS[:, 128 * i:128 * (i + 1)], onesb[:],
                                bigrow[:], start=False, stop=True,
                                skip_group_check=True)
                        return last
                    E("tensor", s_fn,
                      reads=[("qh", pp), ("kh", pp), ("msk", 0),
                             ("msk", 1), "bigrow", "idb", "onesb"],
                      writes=["psS"])
                    E("vector", lambda h, sl=sl: h.reduce_max(
                        negmS[:, sl:sl + 1], psS[:, 0:640], axis=AX.X,
                        negate=True),
                      reads=["psS"], writes=["negmS"])

                def st_a(qc):
                    sl = qc % 2

                    def fn(h, qc=qc, sl=sl, pp=pp, hb=hb):
                        for i in range(5):
                            h.matmul(
                                psST[sl][:, 128 * i:128 * (i + 1)],
                                kh[pp][hb:hb + 64,
                                       128 * (qc + i):128 * (qc + i + 1)],
                                qh[pp][hb:hb + 64,
                                       128 * qc:128 * (qc + 1)],
                                start=(i == 0 or i == 4), stop=True,
                                skip_group_check=True)
                        h.matmul(psST[sl][:, 0:128], idb[:], masks[2][:],
                                 start=False, stop=True,
                                 skip_group_check=True)
                        last = h.matmul(psST[sl][:, 512:640], idb[:],
                                        masks[3][:], start=False,
                                        stop=True, skip_group_check=True)
                        for i in range(max(0, 4 - qc)):
                            last = h.matmul(
                                psST[sl][:, 128 * i:128 * (i + 1)],
                                onesb[:], bigrow[:], start=False,
                                stop=True, skip_group_check=True)
                        return last
                    E("tensor", fn,
                      reads=[("qh", pp), ("kh", pp), ("msk", 2), ("msk", 3),
                             "bigrow", "idb", "onesb"],
                      writes=[("psST", sl)])

                def negm_row(qc):
                    sl = qc % 2
                    E("tensor", lambda h, sl=sl: h.transpose(
                        psS[0:1, 0:128], negmS[:, sl:sl + 1], idf[:]),
                      reads=["negmS"], writes=["psS"])
                    E("scalar", lambda h, sl=sl: h.copy(
                        negmrow[sl][0:1, :], psS[0:1, 0:128]),
                      reads=["psS"], writes=[("negmrow", sl)])
                    E("vector", lambda h, sl=sl: h.tensor_copy(
                        negmrow4[sl][0:1, :].rearrange(
                            "p (o n) -> p o n", o=4),
                        negmrow[sl][0:1, :].rearrange(
                            "p (o n) -> p o n", o=1)
                        .broadcast_to([1, 4, 128])),
                      reads=[("negmrow", sl)], writes=[("negmrow4", sl)])

                def st_b(qc):
                    sl = qc % 2

                    def fn(h, sl=sl):
                        h.matmul(psST[sl][:, 0:512], onesb[:],
                                 negmrow4[sl][0:1, :], start=False,
                                 stop=True, skip_group_check=True)
                        return h.matmul(psST[sl][:, 512:640], onesb[:],
                                        negmrow[sl][0:1, :], start=False,
                                        stop=True, skip_group_check=True)
                    E("tensor", fn,
                      reads=[("negmrow4", sl), ("negmrow", sl)],
                      writes=[("psST", sl)])
                    E("scalar", lambda h, sl=sl: h.activation(
                        PT[sl][:], psST[sl][:], AF.Exp, scale=QK_SCALE),
                      reads=[("psST", sl)], writes=[("PT", sl)])

                for qc in range(NQC):
                    if qc > 2:
                        av_recip(qc - 3)
                    sweep1(qc)
                    st_a(qc)
                    negm_row(qc)
                    if qc > 0:
                        st_b(qc - 1)
                    if qc > 1:
                        av_mm(qc - 2)
                    if qc > 2:
                        av_norm(qc - 3)
                st_b(NQC - 1)
                av_mm(NQC - 2)
                av_recip(NQC - 3)
                av_norm(NQC - 3)
                av_mm(NQC - 1)
                av_recip(NQC - 2)
                av_norm(NQC - 2)
                av_recip(NQC - 1)
                av_norm(NQC - 1)

            p.barrier()


def finale(nc, p, es, x_in, out_d, xs, idf, onesf, epsc, vec8):
    E = p.emit
    with contextlib.ExitStack() as ph:
        Sa = lambda n, *a: ph.enter_context(nc.sbuf_tensor(_u(n), *a))
        PSa = lambda n, *a: ph.enter_context(nc.psum_tensor(_u(n), *a))
        xt = [Sa(f"xtF{s}", [128, C], F32) for s in range(2)]
        ft = [Sa(f"ftF{s}", [128, C], F32) for s in range(2)]
        ot = [Sa(f"otF{s}", [128, C], F32) for s in range(2)]
        tiny = Sa("tinyF", [128, 32], F32)
        mrow = Sa("mrowF", [1, 128], F32)
        vecb = Sa("vecbF", [128, C], F32)
        psF = [PSa(f"psF{s}", [128, 512], F32) for s in range(4)]
        psV = PSa("psVF", [128, 128], F32)

        for cc in range(8):
            E("tensor", lambda h, cc=cc: h.transpose(
                psV[0:1, 0:128], vec8[:, cc:cc + 1], idf[:]),
              reads=["vec8"], writes=["psV"])
            E("scalar", lambda h: h.copy(mrow[0:1, 0:128],
                                         psV[0:1, 0:128]),
              reads=["psV"], writes=["mrowF"])
            E("tensor", lambda h, cc=cc: h.matmul(
                psV[:, 0:128], onesf[:], mrow[0:1, 0:128],
                start=True, stop=True, skip_group_check=True),
              reads=["mrowF", "onesf"], writes=["psV"])
            E("scalar", lambda h, cc=cc: h.copy(
                vecb[:, 128 * cc:128 * (cc + 1)], psV[:, 0:128]),
              reads=["psV"], writes=["vecbF"])

        def fin_tile(i):
            sl = i % 2
            col = QOFF + 128 * i
            E("sync", lambda h: h.dma_start(
                out=xt[sl][:],
                in_=x_in[QOFF + 128 * i:QOFF + 128 * (i + 1), :]),
              writes=[("xtF", sl)], dma=True)
            for cc in range(8):
                pf = 2 * sl + cc // 4
                E("tensor", lambda h, cc=cc, pf=pf: h.transpose(
                    psF[pf][:, 128 * (cc % 4):128 * (cc % 4 + 1)],
                    xs[cc][:, col:col + 128], idf[:]),
                  reads=[("x", cc)], writes=[("psF", pf)],
                  inc=(cc % 4 == 3))
            for hf in range(2):
                E("scalar", lambda h, hf=hf: h.copy(
                    ft[sl][:, 512 * hf:512 * (hf + 1)],
                    psF[2 * sl + hf][:, 0:512]),
                  reads=[("psF", 2 * sl + hf)], writes=[("ftF", sl)])
            tb = 16 * sl
            E("vector", lambda h, tb=tb: h.bn_stats(
                tiny[:, tb:tb + 6], ft[sl][:, 0:512]),
              reads=[("ftF", sl)], writes=[("tinyF", sl)])
            E("vector", lambda h, tb=tb: h.bn_stats(
                tiny[:, tb + 6:tb + 12], ft[sl][:, 512:C]),
              reads=[("ftF", sl)], writes=[("tinyF", sl)])
            E("vector", lambda h, tb=tb: h.bn_aggr(
                tiny[:, tb + 12:tb + 14], tiny[:, tb:tb + 12]),
              reads=[("tinyF", sl)], writes=[("tinyF", sl)])
            E("scalar", lambda h, tb=tb: h.activation(
                tiny[:, tb + 13:tb + 14], tiny[:, tb + 13:tb + 14],
                AF.Sqrt, bias=epsc[:, 0:1]),
              reads=[("tinyF", sl)], writes=[("tinyF", sl)])
            E("vector", lambda h, tb=tb: h.reciprocal(
                tiny[:, tb + 13:tb + 14], tiny[:, tb + 13:tb + 14]),
              reads=[("tinyF", sl)], writes=[("tinyF", sl)])
            E("vector", lambda h, tb=tb: h.tensor_scalar(
                ft[sl][:], ft[sl][:], tiny[:, tb + 12:tb + 13],
                tiny[:, tb + 13:tb + 14],
                op0=ALU.subtract, op1=ALU.mult),
              reads=[("ftF", sl), ("tinyF", sl)], writes=[("ftF", sl)])
            E("vector", lambda h: h.tensor_add(ot[sl][:], xt[sl][:],
                                               vecb[:]),
              reads=[("xtF", sl), "vecbF"], writes=[("otF", sl)])
            E("vector", lambda h: h.tensor_add(ot[sl][:], ot[sl][:],
                                               ft[sl][:]),
              reads=[("otF", sl), ("ftF", sl)], writes=[("otF", sl)])
            E("sync", lambda h: h.dma_start(
                out=out_d[128 * i:128 * (i + 1), :], in_=ot[sl][:]),
              reads=[("otF", sl)], writes=[("outd", i)], dma=True)

        for i in range(NQC):
            fin_tile(i)
        p.barrier()


_BUILT = {}


def _get_program(dbg=()):
    key = tuple(sorted(map(str, dbg)))
    if key not in _BUILT:
        _BUILT[key] = build_program(dbg)
    return _BUILT[key]


def make_core_inputs(inputs, core):
    b, half = core // 2, core % 2
    t0 = half * OWN
    x = np.asarray(inputs["L0_attn_x"])
    xc = np.zeros((XW, C), dtype=np.float32)
    xc[QOFF:] = x[b, t0:t0 + OWN]
    if half == 1:
        xc[DEAD:QOFF] = x[b, t0 - HALO:t0]
    cs_q, sn_q, cs_k, sn_k = _rotary_tables(t0 - 512)
    upS, loS, upT, loT = _masks()
    big = np.full((1, 128), NEG if half == 0 else 0.0, dtype=np.float32)
    return {
        "x": xc,
        "tabi0": cs_q, "tabi1": sn_q, "tabi2": cs_k, "tabi3": sn_k,
        "maski0": np.asarray(upS), "maski1": np.asarray(loS),
        "maski2": np.asarray(upT), "maski3": np.asarray(loT),
        "bigrow": big.astype(ml_dtypes.bfloat16),
        "gate": np.full((128, 1), float(half), dtype=np.float32),
        "a_wqkv": np.asarray(inputs["L0_attn_w_qkv"], dtype=np.float32),
        "a_wc1": np.asarray(inputs["L0_attn_w_conv1"],
                            dtype=np.float32).reshape(1, C),
        "f_wqkv": np.asarray(inputs["L0_ffn_wqkv"], dtype=np.float32),
        "f_qs": np.asarray(inputs["L0_ffn_q_scale"], dtype=np.float32),
        "f_ks": np.asarray(inputs["L0_ffn_k_scale"], dtype=np.float32),
        "f_wo": np.asarray(inputs["L0_ffn_wo"], dtype=np.float32),
        "f_lng": np.asarray(inputs["L0_ffn_ln_g"], dtype=np.float32),
        "f_lnb": np.asarray(inputs["L0_ffn_ln_b"], dtype=np.float32),
        "f_w1": np.asarray(inputs["L0_ffn_w1"], dtype=np.float32),
        "f_b1": np.asarray(inputs["L0_ffn_b1"], dtype=np.float32),
        "f_w2": np.asarray(inputs["L0_ffn_w2"], dtype=np.float32),
        "f_b2": np.asarray(inputs["L0_ffn_b2"], dtype=np.float32),
    }


def run(inputs, dbg=()):
    nc, dbg_out = _get_program(dbg)
    in_maps = [make_core_inputs(inputs, c) for c in range(NCORE)]
    return run_bass_kernel_spmd(nc, in_maps, core_ids=list(range(NCORE)))


def kernel(**inputs):
    res = run(inputs)
    out = np.zeros((B, N, C), dtype=np.float32)
    for c in range(NCORE):
        b, half = c // 2, c % 2
        out[b, half * OWN:(half + 1) * OWN] = res.results[c]["out"]
    return out


# version 28
# speedup vs baseline: 1.0019x; 1.0019x over previous
import contextlib
import numpy as np
import ml_dtypes

import concourse.bass as bass
import concourse.mybir as mybir
from concourse.bass_utils import run_bass_kernel_spmd

F32 = mybir.dt.float32
BF16 = mybir.dt.bfloat16
AF = mybir.ActivationFunctionType
ALU = mybir.AluOpType
AX = mybir.AxisListType

B, N, C = 4, 4096, 1024
HEADS, DH, INNER = 8, 64, 512
HID = 2730
NHC = 22
NCORE = 8
OWN = 2048
HALO = 513
DEAD = 127
XW = DEAD + HALO + OWN
QOFF = DEAD + HALO
KOFF = 128
KW = 2560
NKC = 20
NQC = 16
NXT = 21
NEG = -1.0e30
EPS = 1e-5
QK_SCALE = 8.0
RG_PAIRS = [[0, 1], [2, 3], [4, 5], [6, 7]]
ENGINES = ("tensor", "vector", "scalar", "gpsimd", "sync")
_UID = [0]


def _u(name):
    _UID[0] += 1
    return f"{name}_u{_UID[0]}"



def _rotary_tables(t0):
    t = t0 + np.arange(KW, dtype=np.float64)
    half = np.arange(0, DH, 2, dtype=np.float64)
    inv_freq = 1.0 / 10000.0 ** (half / DH)
    fr = t[:, None] * inv_freq[None, :]
    fr = np.concatenate([fr, fr], axis=1)
    base = (half + 0.4 * DH) / (1.4 * DH)
    power = (t - N // 2) / 256.0
    sc = base[None, :] ** power[:, None]
    sc = np.concatenate([sc, sc], axis=1)
    cos, sin = np.cos(fr), np.sin(fr)
    sgn = np.where(np.arange(DH) < DH // 2, -1.0, 1.0)[None, :]
    out = []
    for tab in (cos * sc, sgn * sin * sc, cos / sc, sgn * sin / sc):
        m = np.zeros((128, NKC * DH), dtype=np.float32)
        for j in range(NKC):
            m[:, DH * j:DH * (j + 1)] = tab[128 * j:128 * (j + 1), :]
        out.append(m)
    return out


def _masks():
    r = np.arange(128)
    up = np.where(r[None, :] >= r[:, None], 0.0, NEG).astype(np.float32)
    lo = np.where(r[None, :] <= r[:, None], 0.0, NEG).astype(np.float32)
    bf = ml_dtypes.bfloat16
    return (up.astype(bf), lo.astype(bf),
            up.T.copy().astype(bf), lo.T.copy().astype(bf))


class P:
    def __init__(self, nc, es):
        self.nc = nc
        self.ops = {e: [] for e in ENGINES}
        self.sem = {e: es.enter_context(nc.semaphore(f"sm_{e}")) for e in ENGINES}
        self.cnt = {e: 0 for e in ENGINES}
        self.waited = {}
        self.tiles = {}

    def _edge(self, edges, eng, cnt):
        if cnt is None:
            cnt = self.cnt[eng]
        if cnt > 0:
            edges[eng] = max(edges.get(eng, 0), cnt)

    def emit(self, eng, fn, reads=(), writes=(), dma=False, inc=True):
        edges = {}
        for k in reads:
            st = self.tiles.get(k)
            if st and st["w"] is not None:
                self._edge(edges, *st["w"])
        for k in writes:
            st = self.tiles.get(k)
            if st:
                if st["w"] is not None:
                    self._edge(edges, *st["w"])
                for re_, rc in st["r"].items():
                    self._edge(edges, re_, rc)
        if eng == "tensor":
            edges.pop("tensor", None)
        real = []
        for se, c in edges.items():
            if self.waited.get((se, eng), 0) >= c:
                continue
            real.append((self.sem[se], c))
            self.waited[(se, eng)] = c
        amt = 16 if dma else 1
        if inc:
            self.cnt[eng] += amt
        my = self.cnt[eng]
        me = self.sem[eng]

        def thunk(h, real=real, fn=fn, inc=inc, me=me, amt=amt):
            for s, c in real:
                h.wait_ge(s, c)
            inst = fn(h)
            if inc:
                inst.then_inc(me, amt)

        self.ops[eng].append(thunk)
        wrec = (eng, my if (inc and not dma) else None)
        for k in writes:
            self.tiles[k] = {"w": wrec, "r": {}}
        for k in reads:
            st = self.tiles.setdefault(k, {"w": None, "r": {}})
            if inc and not dma:
                prev = st["r"].get(eng)
                st["r"][eng] = my if prev is not None or eng not in st["r"] \
                    else my
            else:
                st["r"][eng] = None

    def barrier(self):
        counts = dict(self.cnt)
        for e in ENGINES:
            waits = []
            for o in ENGINES:
                if o == e or counts[o] == 0:
                    continue
                if self.waited.get((o, e), 0) < counts[o]:
                    waits.append((self.sem[o], counts[o]))
                    self.waited[(o, e)] = counts[o]
            if waits:
                def thunk(h, waits=waits):
                    for s, c in waits:
                        h.wait_ge(s, c)
                self.ops[e].append(thunk)

    def finalize(self, block):
        for e, reg in (("tensor", block.tensor), ("vector", block.vector),
                       ("scalar", block.scalar), ("gpsimd", block.gpsimd),
                       ("sync", block.sync)):
            ops = self.ops[e]

            def run(h, ops=ops):
                for f in ops:
                    f(h)

            reg(run)


def build_program(dbg=()):
    nc = bass.Bass("TRN2", target_bir_lowering=False)
    dt = nc.dram_tensor

    x_in = dt("x", [XW, C], F32, kind="ExternalInput").ap()
    tabs_in = [dt(f"tabi{i}", [128, NKC * DH], F32, kind="ExternalInput").ap()
               for i in range(4)]
    m_in = [dt(f"maski{i}", [128, 128], BF16, kind="ExternalInput").ap()
            for i in range(4)]
    bigrow_in = dt("bigrow", [1, 128], BF16, kind="ExternalInput").ap()
    gate_in = dt("gate", [128, 1], F32, kind="ExternalInput").ap()
    aw_qkv = dt("a_wqkv", [3 * C, C], F32, kind="ExternalInput").ap()
    aw_c1 = dt("a_wc1", [1, C], F32, kind="ExternalInput").ap()
    wqkv = dt("f_wqkv", [2, C, 3 * INNER], F32, kind="ExternalInput").ap()
    qsc = dt("f_qs", [2, DH], F32, kind="ExternalInput").ap()
    ksc = dt("f_ks", [2, DH], F32, kind="ExternalInput").ap()
    wo = dt("f_wo", [2, INNER, C], F32, kind="ExternalInput").ap()
    lng = dt("f_lng", [2, C], F32, kind="ExternalInput").ap()
    lnb = dt("f_lnb", [2, C], F32, kind="ExternalInput").ap()
    w1 = dt("f_w1", [2, C, 2 * HID], F32, kind="ExternalInput").ap()
    b1 = dt("f_b1", [2, 2 * HID], F32, kind="ExternalInput").ap()
    w2 = dt("f_w2", [2, HID, C], F32, kind="ExternalInput").ap()
    b2 = dt("f_b2", [2, C], F32, kind="ExternalInput").ap()
    out_d = dt("out", [OWN, C], F32, kind="ExternalOutput").ap()

    cc1_in = dt("cc1_in", [1, C], F32)
    cc1_out = dt("cc1_out", [2, C], F32)
    cc2_in = dt("cc2_in", [1, C + 8], F32)
    cc2_out = dt("cc2_out", [2, C + 8], F32)
    cch_in = dt("cch_in", [8 * 128, HALO], F32)
    cch_out = dt("cch_out", [2 * 8 * 128, HALO], F32)

    dbg_out = {}

    def dbg_t(name, shape):
        if name not in dbg_out:
            dbg_out[name] = dt("dbg_" + name, shape, F32,
                               kind="ExternalOutput").ap()
        return dbg_out[name]

    es = contextlib.ExitStack()
    with es, nc.Block() as block:
        S = lambda *a: es.enter_context(nc.sbuf_tensor(*a))
        p = P(nc, es)
        E = p.emit

        xs = [S(f"xs{c}", [128, XW], F32) for c in range(8)]
        masks = [S(f"msk{i}", [128, 128], BF16) for i in range(4)]
        bigrow = S("bigrowS", [1, 128], BF16)
        gate = S("gateS", [128, 1], F32)
        idb = S("idb", [128, 128], BF16)
        idf = S("idf", [128, 128], F32)
        onesb = S("onesb", [1, 128], BF16)
        onesf = S("onesf", [1, 128], F32)
        onecolb = S("onecolb", [128, 1], BF16)
        onesbb = S("onesbb", [128, 128], BF16)
        onef1 = S("onef1", [1, 1], F32)
        epsc = S("epscS", [128, 1], F32)
        vec8 = S("vec8", [128, 8], F32)
        bcols = S("bcolsS", [128, 136], F32)
        qsb = S("qsbS", [128, DH], F32)
        ksb = S("ksbS", [128, DH], F32)

        for i in range(4):
            E("sync", lambda h, d=masks[i], s=m_in[i]:
              h.dma_start(out=d[:], in_=s), writes=[("msk", i)], dma=True)
        E("sync", lambda h: h.dma_start(out=bigrow[:], in_=bigrow_in),
          writes=["bigrow"], dma=True)
        E("sync", lambda h: h.dma_start(out=gate[:], in_=gate_in),
          writes=["gate"], dma=True)
        for t, v in ((idb, 1.0), (idf, 1.0), (onesb, 1.0), (onesf, 1.0),
                     (onecolb, 1.0), (onef1, 1.0), (epsc, EPS),
                     (onesbb, 1.0)):
            E("gpsimd", lambda h, t=t, v=v: h.memset(t[:], v),
              writes=[t.name])
        for t in (idb, idf):
            E("gpsimd", lambda h, t=t: h.affine_select(
                out=t[:], in_=t[:], compare_op=ALU.is_equal, fill=0.0,
                pattern=[[-1, 128]], base=0, channel_multiplier=1),
              reads=[t.name], writes=[t.name])

        phase_a(nc, p, es, x_in, aw_qkv, aw_c1, cc1_in, cc1_out, cc2_in,
                cc2_out, xs, gate, idf, onesb, onesf, onef1, epsc, vec8,
                dbg, dbg_t)

        for l in range(2):
            build_layer(nc, p, es, l, xs, masks, bigrow, gate,
                        idb, idf, onesb, onesf, onecolb, onef1, epsc,
                        onesbb, bcols, qsb, ksb,
                        tabs_in, wqkv, qsc, ksc, wo, lng, lnb,
                        w1, b1, w2, b2, cch_in, cch_out, dbg, dbg_t)

        finale(nc, p, es, x_in, out_d, xs, idf, onesf, epsc, vec8)

        p.finalize(block)
    return nc, dbg_out


def phase_a(nc, p, es, x_in, aw_qkv, aw_c1, cc1_in, cc1_out, cc2_in, cc2_out,
            xs, gate, idf, onesb, onesf, onef1, epsc, vec8, dbg, dbg_t):
    E = p.emit
    with contextlib.ExitStack() as ph:
        Sa = lambda n, *a: ph.enter_context(nc.sbuf_tensor(_u(n), *a))
        PSa = lambda n, *a: ph.enter_context(nc.psum_tensor(_u(n), *a))
        xt = [Sa(f"xtA{s}", [128, C], F32) for s in range(2)]
        ln0 = [Sa(f"ln0_{i}", [128, C], BF16) for i in range(NQC)]
        aw = [Sa(f"awA{i}", [128, C], BF16) for i in range(8)]
        scr = [Sa(f"scrA{s}", [128, C], F32) for s in range(2)]
        scr2 = Sa("scrA2", [128, 16], F32)
        rowA = Sa("rowA", [1, C + 8], F32)
        rowG0 = Sa("rowG0", [1, C + 8], F32)
        rowG1 = Sa("rowG1", [1, C + 8], F32)
        rowB = Sa("rowBA", [1, C], BF16)
        ub = Sa("ubA", [128, C], BF16)
        bb = Sa("bbA", [128, C], BF16)
        vecb = Sa("vecbA", [128, C], F32)
        agcol = Sa("agcolA", [128, NQC], F32)
        colb = Sa("colbA", [128, NQC], BF16)
        scol = Sa("scolA", [128, NQC], F32)
        t1c = Sa("t1cA", [128, 8], F32)
        avc = Sa("avcA", [128, 8], F32)
        stat = Sa("statA", [128, 3 * NXT], F32)
        tiny = Sa("tinyA", [128, 32], F32)
        mrow = Sa("mrowA", [1, 160], F32)
        wc1c = Sa("wc1cA", [128, 8], F32)
        wc1cb = Sa("wc1cbA", [128, 8], BF16)
        psT = [PSa(f"psTA{s}", [128, 512], F32) for s in range(2)]
        psRow = PSa("psRowA", [1, C], F32)
        psB = PSa("psBA", [128, 512], F32)
        psB2 = PSa("psB2A", [128, 512], F32)
        psSm = PSa("psSmA", [128, 128], F32)

        def col_to_cell(colap, colkey, cell, op="max", negate=False):
            E("tensor", lambda h: h.transpose(psSm[0:1, 0:128], colap,
                                              idf[:]),
              reads=[colkey], writes=["psSm"])
            E("scalar", lambda h: h.copy(mrow[0:1, 0:128],
                                         psSm[0:1, 0:128]),
              reads=["psSm"], writes=["mrow"])
            if op == "max":
                E("vector", lambda h: h.reduce_max(
                    mrow[0:1, cell:cell + 1], mrow[0:1, 0:128], axis=AX.X,
                    negate=negate),
                  reads=["mrow"], writes=["mrow"])
            else:
                E("vector", lambda h: h.reduce_sum(
                    mrow[0:1, cell:cell + 1], mrow[0:1, 0:128], axis=AX.X),
                  reads=["mrow"], writes=["mrow"])

        def cell_to_col(cell, tcol):
            E("tensor", lambda h: h.matmul(
                psSm[:, 0:1], onesf[:], mrow[0:1, cell:cell + 1],
                start=True, stop=True, skip_group_check=True),
              reads=["mrow", "onesf"], writes=["psSm"])
            E("scalar", lambda h: h.copy(tiny[:, tcol:tcol + 1],
                                         psSm[:, 0:1]),
              reads=["psSm"], writes=[("tiny", 0)])

        def load_aw(which):
            for i in range(8):
                E("gpsimd", lambda h, d=aw[i],
                  s=aw_qkv[C * which + 128 * i:C * which + 128 * (i + 1), :]:
                  h.dma_start(out=d[:], in_=s), writes=[("aw", i)], dma=True)

        load_aw(0)
        E("sync", lambda h: h.dma_start(out=rowA[0:1, 0:C], in_=aw_c1),
          writes=["rowA"], dma=True)

        def wc1_cols(h):
            last = None
            for i in range(8):
                last = h.matmul(psSm[:, i:i + 1],
                                rowA[0:1, 128 * i:128 * (i + 1)], onef1[:],
                                start=True, stop=True, skip_group_check=True)
            return last
        E("tensor", wc1_cols, reads=["rowA", "onef1"], writes=["psSm"])
        E("scalar", lambda h: h.copy(wc1c[:], psSm[:, 0:8]),
          reads=["psSm"], writes=["wc1c"])
        E("vector", lambda h: h.tensor_copy(wc1cb[:], wc1c[:]),
          reads=["wc1c"], writes=["wc1cb"])

        def u_mm(h):
            last = None
            for i in range(8):
                for hf in range(2):
                    last = h.matmul(
                        psRow[0:1, 512 * hf:512 * (hf + 1)],
                        wc1cb[:, i:i + 1],
                        aw[i][:, 512 * hf:512 * (hf + 1)],
                        start=(i == 0), stop=True, skip_group_check=True)
            return last
        E("tensor", u_mm, reads=["wc1cb"] + [("aw", i) for i in range(8)],
          writes=["psRow"])
        E("scalar", lambda h: h.copy(rowA[0:1, 0:C], psRow[:]),
          reads=["psRow"], writes=["rowA"])
        E("vector", lambda h: h.tensor_copy(rowB[:], rowA[0:1, 0:C]),
          reads=["rowA"], writes=["rowB"])

        def bcast_row_to(dst, dstkey):
            for i in range(2):
                E("tensor", lambda h, i=i: h.matmul(
                    psB[:, :], onesb[:], rowB[0:1, 512 * i:512 * (i + 1)],
                    start=True, stop=True, skip_group_check=True),
                  reads=["rowB", "onesb"], writes=["psB"])
                E("scalar", lambda h, i=i: h.copy(
                    dst[:, 512 * i:512 * (i + 1)], psB[:]),
                  reads=["psB"], writes=[dstkey])

        bcast_row_to(ub, "ub")

        def pass1_tile(i):
            sl = i % 2
            tb = 16 * (i % 2)
            E("sync", lambda h: h.dma_start(
                out=xt[sl][:], in_=x_in[128 * i:128 * (i + 1), :]),
              writes=[("xt", sl)], dma=True)
            E("vector", lambda h, tb=tb: h.bn_stats(
                tiny[:, tb:tb + 6], xt[sl][:, 0:512]),
              reads=[("xt", sl)], writes=[("tiny", tb)])
            E("vector", lambda h, tb=tb: h.bn_stats(
                tiny[:, tb + 6:tb + 12], xt[sl][:, 512:C]),
              reads=[("xt", sl)], writes=[("tiny", tb)])
            E("vector", lambda h, tb=tb: h.bn_aggr(
                stat[:, 3 * i:3 * i + 2], tiny[:, tb:tb + 12]),
              reads=[("tiny", tb)], writes=[("stat", i)])
            E("scalar", lambda h: h.activation(
                stat[:, 3 * i + 2:3 * i + 3], stat[:, 3 * i + 1:3 * i + 2],
                AF.Sqrt, bias=epsc[:, 0:1]),
              reads=[("stat", i)], writes=[("stat", i)])
            E("vector", lambda h: h.reciprocal(
                stat[:, 3 * i + 2:3 * i + 3], stat[:, 3 * i + 2:3 * i + 3]),
              reads=[("stat", i)], writes=[("stat", i)])
            if i < 5:
                return
            o = i - 5
            E("vector", lambda h: h.tensor_scalar(
                ln0[o][:], xt[sl][:], stat[:, 3 * i:3 * i + 1],
                stat[:, 3 * i + 2:3 * i + 3], op0=ALU.subtract, op1=ALU.mult),
              reads=[("xt", sl), ("stat", i)], writes=[("ln0", o)])
            E("gpsimd", lambda h: h.tensor_mul(scr[sl][:], ln0[o][:],
                                               ub[:]),
              reads=[("ln0", o), "ub"], writes=[("scr", sl)])

        def pass1_finish(o):
            sl = (o + 5) % 2
            E("vector", lambda h: h.reduce_sum(agcol[:, o:o + 1],
                                               scr[sl][:], axis=AX.X),
              reads=[("scr", sl)], writes=["agcol"])
            E("scalar", lambda h: h.copy(colb[:, o:o + 1],
                                         agcol[:, o:o + 1]),
              reads=["agcol"], writes=["colb"])

            def m_mm(h):
                last = None
                for hf in range(2):
                    last = h.matmul(
                        psRow[0:1, 512 * hf:512 * (hf + 1)], colb[:, o:o + 1],
                        ln0[o][:, 512 * hf:512 * (hf + 1)],
                        start=(o == 0), stop=True, skip_group_check=True)
                return last
            E("tensor", m_mm, reads=["colb", ("ln0", o)], writes=["psRow"],
              inc=(o == NQC - 1))

        for i in range(NXT):
            pass1_tile(i)
            if i >= 6:
                pass1_finish(i - 6)
        pass1_finish(NQC - 1)

        E("scalar", lambda h: h.copy(rowA[0:1, 0:C], psRow[:]),
          reads=["psRow"], writes=["rowA"])
        E("scalar", lambda h: h.dma_start(out=cc1_in[:, :],
                                          in_=rowA[0:1, 0:C]),
          reads=["rowA"], writes=["cc1_in"], dma=True)
        E("gpsimd", lambda h: h.collective_compute(
            "AllGather", ALU.bypass, replica_groups=RG_PAIRS,
            ins=[cc1_in.ap().opt()], outs=[cc1_out.ap().opt()]),
          reads=["cc1_in"], writes=["cc1_out"])
        load_aw(1)
        E("sync", lambda h: h.dma_start(out=rowG0[0:1, 0:C],
                                        in_=cc1_out[0:1, :]),
          reads=["cc1_out"], writes=["rowG"], dma=True)
        E("sync", lambda h: h.dma_start(out=rowG1[0:1, 0:C],
                                        in_=cc1_out[1:2, :]),
          reads=["cc1_out"], writes=["rowG"], dma=True)
        E("vector", lambda h: h.tensor_add(rowA[0:1, 0:C], rowG0[0:1, 0:C],
                                           rowG1[0:1, 0:C]),
          reads=["rowG"], writes=["rowA"])
        E("vector", lambda h: h.tensor_copy(rowB[:], rowA[0:1, 0:C]),
          reads=["rowA"], writes=["rowB"])
        bcast_row_to(bb, "bb")

        for cc in range(8):
            me = "gpsimd" if cc % 2 == 0 else "vector"
            E(me, lambda h, cc=cc: h.tensor_mul(
                scr[cc % 2][:], aw[cc][:], bb[:]),
              reads=[("aw", cc), "bb"], writes=[("scr", cc % 2)])
            E("vector", lambda h, cc=cc: h.reduce_sum(
                t1c[:, cc:cc + 1], scr[cc % 2][:], axis=AX.X),
              reads=[("scr", cc % 2)], writes=["t1c"])
        E("vector", lambda h: h.tensor_scalar_mul(t1c[:], t1c[:],
                                                  float(C) ** (-0.5)),
          reads=["t1c"], writes=["t1c"])

        def softmax_cols(src, nsrc, dstb, key):
            E("vector", lambda h: h.reduce_max(tiny[:, 0:1], src[:, 0:nsrc],
                                               axis=AX.X),
              reads=[key], writes=[("tiny", 0)])
            col_to_cell(tiny[:, 0:1], ("tiny", 0), 128, "max", negate=True)
            cell_to_col(128, 1)
            E("scalar", lambda h: h.activation(src[:, 0:nsrc], src[:, 0:nsrc],
                                               AF.Exp, bias=tiny[:, 1:2]),
              reads=[key, ("tiny", 0)], writes=[key])
            E("vector", lambda h: h.reduce_sum(tiny[:, 2:3], src[:, 0:nsrc],
                                               axis=AX.X),
              reads=[key], writes=[("tiny", 0)])
            col_to_cell(tiny[:, 2:3], ("tiny", 0), 129, "sum")
            E("vector", lambda h: h.reciprocal(mrow[0:1, 130:131],
                                               mrow[0:1, 129:130]),
              reads=["mrow"], writes=["mrow"])
            cell_to_col(130, 3)
            E("vector", lambda h: h.tensor_scalar_mul(
                src[:, 0:nsrc], src[:, 0:nsrc], tiny[:, 3:4]),
              reads=[key, ("tiny", 0)], writes=[key])
            E("vector", lambda h: h.tensor_copy(dstb[:, 0:nsrc],
                                                src[:, 0:nsrc]),
              reads=[key], writes=["colb"])

        softmax_cols(t1c, 8, colb, "t1c")
        load_aw(2)

        def wt_mm(h):
            last = None
            for cc in range(8):
                for hf in range(2):
                    last = h.matmul(
                        psRow[0:1, 512 * hf:512 * (hf + 1)],
                        colb[:, cc:cc + 1],
                        aw[cc][:, 512 * hf:512 * (hf + 1)],
                        start=(cc == 0), stop=True, skip_group_check=True)
            return last
        E("tensor", wt_mm,
          reads=["colb"] + [("aw", cc) for cc in range(8)],
          writes=["psRow"])
        E("scalar", lambda h: h.copy(rowA[0:1, 0:C], psRow[:]),
          reads=["psRow"], writes=["rowA"])
        E("vector", lambda h: h.tensor_copy(rowB[:], rowA[0:1, 0:C]),
          reads=["rowA"], writes=["rowB"])
        bcast_row_to(bb, "bb")

        for o in range(NQC):
            me = "gpsimd" if o % 2 == 0 else "vector"
            E(me, lambda h, o=o: h.tensor_mul(
                scr[o % 2][:], ln0[o][:], bb[:]),
              reads=[("ln0", o), "bb"], writes=[("scr", o % 2)])
            E("vector", lambda h, o=o: h.reduce_sum(
                scol[:, o:o + 1], scr[o % 2][:], axis=AX.X),
              reads=[("scr", o % 2)], writes=["scol"])
        E("vector", lambda h: h.reduce_max(tiny[:, 0:1], scol[:], axis=AX.X),
          reads=["scol"], writes=[("tiny", 0)])
        col_to_cell(tiny[:, 0:1], ("tiny", 0), 131, "max")
        col_to_cell(tiny[:, 0:1], ("tiny", 0), 132, "max", negate=True)
        cell_to_col(132, 1)
        E("scalar", lambda h: h.activation(scol[:], scol[:], AF.Exp,
                                           bias=tiny[:, 1:2]),
          reads=["scol", ("tiny", 0)], writes=["scol"])
        E("vector", lambda h: h.reduce_sum(tiny[:, 2:3], scol[:], axis=AX.X),
          reads=["scol"], writes=[("tiny", 0)])
        col_to_cell(tiny[:, 2:3], ("tiny", 0), 133, "sum")
        E("vector", lambda h: h.tensor_copy(colb[:], scol[:]),
          reads=["scol"], writes=["colb"])

        def p_mm(h):
            last = None
            for o in range(NQC):
                for hf in range(2):
                    last = h.matmul(
                        psRow[0:1, 512 * hf:512 * (hf + 1)],
                        colb[:, o:o + 1], ln0[o][:, 512 * hf:512 * (hf + 1)],
                        start=(o == 0), stop=True, skip_group_check=True)
            return last
        E("tensor", p_mm,
          reads=["colb"] + [("ln0", o) for o in range(NQC)], writes=["psRow"])
        E("scalar", lambda h: h.copy(rowA[0:1, 0:C], psRow[:]),
          reads=["psRow"], writes=["rowA"])
        E("vector", lambda h: h.tensor_copy(rowA[0:1, C:C + 1],
                                            mrow[0:1, 131:132]),
          reads=["mrow"], writes=["rowA"])
        E("vector", lambda h: h.tensor_copy(rowA[0:1, C + 1:C + 2],
                                            mrow[0:1, 133:134]),
          reads=["mrow"], writes=["rowA"])
        E("scalar", lambda h: h.dma_start(out=cc2_in[:, :],
                                          in_=rowA[0:1, 0:C + 8]),
          reads=["rowA"], writes=["cc2_in"], dma=True)
        E("gpsimd", lambda h: h.collective_compute(
            "AllGather", ALU.bypass, replica_groups=RG_PAIRS,
            ins=[cc2_in.ap().opt()], outs=[cc2_out.ap().opt()]),
          reads=["cc2_in"], writes=["cc2_out"])
        E("sync", lambda h: h.dma_start(out=rowG0[:, :],
                                        in_=cc2_out[0:1, :]),
          reads=["cc2_out"], writes=["rowG"], dma=True)
        E("sync", lambda h: h.dma_start(out=rowG1[:, :],
                                        in_=cc2_out[1:2, :]),
          reads=["cc2_out"], writes=["rowG"], dma=True)
        E("vector", lambda h: h.tensor_max(
            mrow[0:1, 134:135], rowG0[0:1, C:C + 1], rowG1[0:1, C:C + 1]),
          reads=["rowG"], writes=["mrow"])
        for i, rgi in ((0, 0), (1, 1)):
            rg = (rowG0, rowG1)[rgi]
            E("vector", lambda h, rg=rg, i=i: h.tensor_sub(
                mrow[0:1, 135 + i:136 + i], rg[0:1, C:C + 1],
                mrow[0:1, 134:135]),
              reads=["rowG", "mrow"], writes=["mrow"])
            E("scalar", lambda h, i=i: h.activation(
                mrow[0:1, 135 + i:136 + i], mrow[0:1, 135 + i:136 + i],
                AF.Exp),
              reads=["mrow"], writes=["mrow"])
        E("vector", lambda h: h.tensor_mul(
            mrow[0:1, 137:138], mrow[0:1, 135:136], rowG0[0:1, C + 1:C + 2]),
          reads=["mrow", "rowG"], writes=["mrow"])
        E("vector", lambda h: h.scalar_tensor_tensor(
            mrow[0:1, 137:138], rowG1[0:1, C + 1:C + 2], mrow[0:1, 136:137],
            mrow[0:1, 137:138], op0=ALU.mult, op1=ALU.add),
          reads=["rowG", "mrow"], writes=["mrow"])
        E("vector", lambda h: h.reciprocal(mrow[0:1, 138:139],
                                           mrow[0:1, 137:138]),
          reads=["mrow"], writes=["mrow"])
        E("vector", lambda h: h.tensor_scalar_mul(
            rowA[0:1, 0:C], rowG0[0:1, 0:C], mrow[0:1, 135:136]),
          reads=["rowG", "mrow"], writes=["rowA"])
        E("vector", lambda h: h.scalar_tensor_tensor(
            rowA[0:1, 0:C], rowG1[0:1, 0:C], mrow[0:1, 136:137],
            rowA[0:1, 0:C], op0=ALU.mult, op1=ALU.add),
          reads=["rowG", "mrow"], writes=["rowA"])
        E("vector", lambda h: h.tensor_copy(rowB[:], rowA[0:1, 0:C]),
          reads=["rowA"], writes=["rowB"])
        bcast_row_to(bb, "bb")
        for cc in range(8):
            me = "gpsimd" if cc % 2 == 0 else "vector"
            E(me, lambda h, cc=cc: h.tensor_mul(
                scr[cc % 2][:], aw[cc][:], bb[:]),
              reads=[("aw", cc), "bb"], writes=[("scr", cc % 2)])
            E("vector", lambda h, cc=cc: h.reduce_sum(
                avc[:, cc:cc + 1], scr[cc % 2][:], axis=AX.X),
              reads=[("scr", cc % 2)], writes=["avc"])
        cell_to_col(138, 4)
        E("vector", lambda h: h.tensor_scalar_mul(avc[:], avc[:],
                                                  tiny[:, 4:5]),
          reads=["avc", ("tiny", 0)], writes=["avc"])
        E("vector", lambda h: h.reduce_sum(tiny[:, 0:1], avc[:], axis=AX.X),
          reads=["avc"], writes=[("tiny", 0)])
        col_to_cell(tiny[:, 0:1], ("tiny", 0), 139, "sum")
        E("vector", lambda h: h.tensor_scalar_mul(
            mrow[0:1, 139:140], mrow[0:1, 139:140], 1.0 / C),
          reads=["mrow"], writes=["mrow"])
        E("scalar", lambda h: h.square(scr2[:, 0:8], avc[:]),
          reads=["avc"], writes=["scr2"])
        E("vector", lambda h: h.reduce_sum(tiny[:, 2:3], scr2[:, 0:8],
                                           axis=AX.X),
          reads=["scr2"], writes=[("tiny", 0)])
        col_to_cell(tiny[:, 2:3], ("tiny", 0), 140, "sum")
        E("vector", lambda h: h.tensor_scalar_mul(
            mrow[0:1, 140:141], mrow[0:1, 140:141], 1.0 / C),
          reads=["mrow"], writes=["mrow"])
        E("vector", lambda h: h.tensor_mul(
            mrow[0:1, 141:142], mrow[0:1, 139:140], mrow[0:1, 139:140]),
          reads=["mrow"], writes=["mrow"])
        E("vector", lambda h: h.tensor_sub(
            mrow[0:1, 140:141], mrow[0:1, 140:141], mrow[0:1, 141:142]),
          reads=["mrow"], writes=["mrow"])
        E("scalar", lambda h: h.activation(
            mrow[0:1, 140:141], mrow[0:1, 140:141], AF.Sqrt,
            bias=epsc[0:1, 0:1]),
          reads=["mrow"], writes=["mrow"])
        E("vector", lambda h: h.reciprocal(mrow[0:1, 140:141],
                                           mrow[0:1, 140:141]),
          reads=["mrow"], writes=["mrow"])
        cell_to_col(139, 5)
        cell_to_col(140, 6)
        E("vector", lambda h: h.tensor_scalar(
            vec8[:], avc[:], tiny[:, 5:6], tiny[:, 6:7],
            op0=ALU.subtract, op1=ALU.mult),
          reads=["avc", ("tiny", 0)], writes=["vec8"])

        for cc in range(8):
            E("tensor", lambda h, cc=cc: h.transpose(
                psSm[0:1, 0:128], vec8[:, cc:cc + 1], idf[:]),
              reads=["vec8"], writes=["psSm"])
            E("scalar", lambda h: h.copy(mrow[0:1, 0:128],
                                         psSm[0:1, 0:128]),
              reads=["psSm"], writes=["mrow"])
            E("tensor", lambda h, cc=cc: h.matmul(
                psB2[:, 128 * (cc % 4):128 * (cc % 4 + 1)], onesf[:],
                mrow[0:1, 0:128], start=True, stop=True,
                skip_group_check=True),
              reads=["mrow", "onesf"], writes=["psB2"])
            E("scalar", lambda h, cc=cc: h.copy(
                vecb[:, 128 * cc:128 * (cc + 1)],
                psB2[:, 128 * (cc % 4):128 * (cc % 4 + 1)]),
              reads=["psB2"], writes=["vecb"])

        def pass3_tile(i):
            sl = i % 2
            tb = 16 * (i % 2)
            E("sync", lambda h: h.dma_start(
                out=xt[sl][:], in_=x_in[128 * i:128 * (i + 1), :]),
              writes=[("xt", sl)], dma=True)
            E("gpsimd", lambda h: h.tensor_add(scr[sl][:], xt[sl][:],
                                               vecb[:]),
              reads=[("xt", sl), "vecb"], writes=[("scr", sl)])
            E("vector", lambda h, tb=tb: h.bn_stats(
                tiny[:, tb:tb + 6], scr[sl][:, 0:512]),
              reads=[("scr", sl)], writes=[("tiny", tb)])
            E("vector", lambda h, tb=tb: h.bn_stats(
                tiny[:, tb + 6:tb + 12], scr[sl][:, 512:C]),
              reads=[("scr", sl)], writes=[("tiny", tb)])
            E("vector", lambda h, tb=tb: h.bn_aggr(
                tiny[:, tb + 12:tb + 14], tiny[:, tb:tb + 12]),
              reads=[("tiny", tb)], writes=[("tiny", tb)])
            E("scalar", lambda h, tb=tb: h.activation(
                tiny[:, tb + 13:tb + 14], tiny[:, tb + 13:tb + 14],
                AF.Sqrt, bias=epsc[:, 0:1]),
              reads=[("tiny", tb)], writes=[("tiny", tb)])
            E("vector", lambda h, tb=tb: h.reciprocal(
                tiny[:, tb + 13:tb + 14], tiny[:, tb + 13:tb + 14]),
              reads=[("tiny", tb)], writes=[("tiny", tb)])
            E("vector", lambda h, tb=tb: h.tensor_scalar(
                scr[sl][:], scr[sl][:], tiny[:, tb + 12:tb + 13],
                tiny[:, tb + 13:tb + 14], op0=ALU.subtract, op1=ALU.mult),
              reads=[("scr", sl), ("tiny", tb)], writes=[("scr", sl)])
            for cc in range(8):
                sl2 = cc % 2
                E("tensor", lambda h, cc=cc, sl2=sl2: h.transpose(
                    psT[sl2][:, 0:128], scr[sl][:, 128 * cc:128 * (cc + 1)],
                    idf[:]),
                  reads=[("scr", sl)], writes=[("psT", sl2)])
                ee = "vector" if cc >= 6 else "scalar"
                E(ee, lambda h, cc=cc, sl2=sl2: h.copy(
                    xs[cc][:, 128 * i:128 * (i + 1)], psT[sl2][:, 0:128])
                  if cc < 6 else h.tensor_copy(
                      xs[cc][:, 128 * i:128 * (i + 1)], psT[sl2][:, 0:128]),
                  reads=[("psT", sl2)], writes=[("x", cc)])

        for i in range(NXT):
            pass3_tile(i)
        for cc in range(8):
            E("vector", lambda h, cc=cc: h.tensor_scalar_mul(
                xs[cc][:, 0:QOFF], xs[cc][:, 0:QOFF], gate[:, 0:1]),
              reads=[("x", cc), "gate"], writes=[("x", cc)])
        if "y2" in dbg:
            d = dbg_t("y2", [8 * 128, XW])
            for cc in range(8):
                E("sync", lambda h, cc=cc, d=d: h.dma_start(
                    out=d[128 * cc:128 * (cc + 1), :], in_=xs[cc][:]),
                  reads=[("x", cc)], dma=True)
        p.barrier()


def build_layer(nc, p, es, l, xs, masks, bigrow, gate, idb, idf, onesb,
                onesf, onecolb, onef1, epsc, onesbb, bcols, qsb, ksb,
                tabs_in, wqkv, qsc, ksc, wo, lng, lnb,
                w1, b1, w2, b2, cch_in, cch_out, dbg, dbg_t):
    E = p.emit
    half_chunks = [list(range(0, 11)), list(range(11, NHC))]

    with contextlib.ExitStack() as ph:
        Sa = lambda n, *a: ph.enter_context(nc.sbuf_tensor(_u(n), *a))
        PSa = lambda n, *a: ph.enter_context(nc.psum_tensor(_u(n), *a))
        brow = [Sa(f"brow{l}_{s}", [1, 1024 + 128], F32) for s in range(2)]
        psCol = [PSa(f"psCol{l}_{s}", [128, 128], F32) for s in range(2)]
        pieces = []
        for pi in range(3):
            off0 = 1024 * pi
            n = min(1024, HID - off0)
            cols = []
            c0 = off0 // 128
            for c in range(c0, min(NHC, (off0 + n + 127) // 128)):
                w = min(128, HID - 128 * c)
                cols.append((c, 128 * c - off0, w))
            pieces.append((b1[l, off0:off0 + n].rearrange(
                "(a n) -> a n", a=1), n, cols))
        for pi in range(3):
            off0 = 1024 * pi
            n = min(1024, HID - off0)
            cols = []
            c0 = off0 // 128
            for c in range(c0, min(NHC, (off0 + n + 127) // 128)):
                w = min(128, HID - 128 * c)
                cols.append((22 + c, 128 * c - off0, w))
            pieces.append((b1[l, HID + off0:HID + off0 + n].rearrange(
                "(a n) -> a n", a=1), n, cols))
        for bix, srct in ((44, b2), (52, lng), (60, lnb)):
            cols = [(bix + c, 128 * c, 128) for c in range(8)]
            pieces.append((srct[l, :].rearrange("(a n) -> a n", a=1), C,
                           cols))
        for pidx, (srcap, n, cols) in enumerate(pieces):
            sp = pidx % 2
            E("sync", lambda h, srcap=srcap, n=n, sp=sp: h.dma_start(
                out=brow[sp][0:1, 0:n], in_=srcap),
              writes=[("brow", sp)], dma=True)

            def colmm(h, cols=cols, sp=sp):
                last = None
                for k, (col, off, w) in enumerate(cols):
                    last = h.matmul(psCol[sp][0:w, k:k + 1],
                                    brow[sp][0:1, off:off + w], onef1[:],
                                    start=True, stop=True,
                                    skip_group_check=True)
                return last
            E("tensor", colmm, reads=[("brow", sp), "onef1"],
              writes=[("psCol", sp)])
            for k, (col, off, w) in enumerate(cols):
                E("scalar", lambda h, col=col, k=k, w=w, sp=sp: h.copy(
                    bcols[0:w, col:col + 1], psCol[sp][0:w, k:k + 1]),
                  reads=[("psCol", sp)], writes=["bcols"])
        for si, (dst, srct, key) in enumerate(
                ((qsb, qsc, "qsb"), (ksb, ksc, "ksb"))):
            sp = si % 2
            E("sync", lambda h, srct=srct, sp=sp: h.dma_start(
                out=brow[sp][0:1, 1024:1024 + DH],
                in_=srct[l, :].rearrange("(a n) -> a n", a=1)),
              writes=[("brow", sp)], dma=True)
            E("tensor", lambda h, sp=sp: h.matmul(
                psCol[sp][:, 0:64], onesf[:], brow[sp][0:1, 1024:1024 + DH],
                start=True, stop=True, skip_group_check=True),
              reads=[("brow", sp), "onesf"], writes=[("psCol", sp)])
            E("scalar", lambda h, dst=dst, sp=sp: h.copy(
                dst[:], psCol[sp][:, 0:64]),
              reads=[("psCol", sp)], writes=[key])
        p.barrier()

    BC = lambda c: ("bcols",)
    b1a_col = lambda c, w=128: bcols[0:w, c:c + 1]
    b1g_col = lambda c, w=128: bcols[0:w, 22 + c:22 + c + 1]
    b2_col = lambda c: bcols[:, 44 + c:44 + c + 1]
    lng_col = lambda c: bcols[:, 52 + c:52 + c + 1]
    lnb_col = lambda c: bcols[:, 60 + c:60 + c + 1]

    def load_wqkv(g):
        for r, cbase in enumerate((256 * g, INNER + 256 * g,
                                   2 * INNER + 256 * g)):
            E("gpsimd", lambda h, r=r, cbase=cbase: h.dma_start(
                out=wqkv_b[:].rearrange("p (ic n) -> p ic n", ic=8)
                [:, :, 256 * r:256 * (r + 1)],
                in_=wqkv[l, :, cbase:cbase + 256].rearrange(
                    "(ic p) n -> p ic n", p=128)),
              writes=["wqkv_b"], dma=True)

    load_wqkv(0)
    E("gpsimd", lambda h: h.dma_start(
        out=wo_b[:].rearrange("p (ic n) -> p ic n", ic=4),
        in_=wo[l, :, :].rearrange("(ic p) n -> p ic n", p=128)),
      writes=["wo_b"], dma=True)

    oscope = contextlib.ExitStack()
    ofm = [oscope.enter_context(nc.sbuf_tensor(_u(f"ofm{l}_{pp}"),
                                               [128, OWN], BF16))
           for pp in range(4)]
    wqkv_b = oscope.enter_context(
        nc.sbuf_tensor(_u("wqkv_b"), [128, 8 * 768], BF16))
    wo_b = oscope.enter_context(
        nc.sbuf_tensor(_u("wo_b"), [128, 4 * C], BF16))
    with contextlib.ExitStack() as lph:
        Sl = lambda n, *a: lph.enter_context(nc.sbuf_tensor(_u(n), *a))
        tabs = [Sl(f"tab{l}_{i}", [128, NKC * DH], F32) for i in range(4)]
        for i in range(4):
            E("sync", lambda h, d=tabs[i], s=tabs_in[i]:
              h.dma_start(out=d[:], in_=s), writes=[("tab", i)], dma=True)
        for ti, scale_t, skey in ((0, qsb, "qsb"), (2, ksb, "ksb")):
            E("vector", lambda h, ti=ti, scale_t=scale_t: h.tensor_mul(
                tabs[ti][:].rearrange("p (j d) -> p j d", d=DH),
                tabs[ti][:].rearrange("p (j d) -> p j d", d=DH),
                scale_t[:].rearrange("p (o d) -> p o d", o=1)
                .broadcast_to([128, NKC, DH])),
              reads=[("tab", ti), skey], writes=[("tab", ti)])
            E("vector", lambda h, ti=ti, scale_t=scale_t: h.tensor_mul(
                tabs[ti + 1][:].rearrange("p (j d) -> p j d", d=DH)
                [:, :, 0:32],
                tabs[ti + 1][:].rearrange("p (j d) -> p j d", d=DH)
                [:, :, 0:32],
                scale_t[:, 32:64].rearrange("p (o d) -> p o d", o=1)
                .broadcast_to([128, NKC, 32])),
              reads=[("tab", ti + 1), skey], writes=[("tab", ti + 1)])
            E("vector", lambda h, ti=ti, scale_t=scale_t: h.tensor_mul(
                tabs[ti + 1][:].rearrange("p (j d) -> p j d", d=DH)
                [:, :, 32:64],
                tabs[ti + 1][:].rearrange("p (j d) -> p j d", d=DH)
                [:, :, 32:64],
                scale_t[:, 0:32].rearrange("p (o d) -> p o d", o=1)
                .broadcast_to([128, NKC, 32])),
              reads=[("tab", ti + 1), skey], writes=[("tab", ti + 1)])

        for g in range(2):
            qkv_and_attention(nc, p, lph, l, g, xs, tabs, masks, bigrow,
                              idb, idf, onesb, onesf, qsb, ksb, wqkv_b,
                              ofm, load_wqkv, dbg, dbg_t)

    with contextlib.ExitStack() as ph:
        PSa = lambda n, *a: ph.enter_context(nc.psum_tensor(_u(n), *a))
        psWO = [PSa(f"psWO{s}", [128, 512], F32) for s in range(2)]
        for tt in range(4):
            for oc in range(8):
                sl = (tt * 8 + oc) % 2

                def wo_fn(h, tt=tt, oc=oc, sl=sl):
                    last = None
                    for ic in range(4):
                        last = h.matmul(
                            psWO[sl][:, :],
                            wo_b[:, 1024 * ic + 128 * oc:
                                 1024 * ic + 128 * (oc + 1)],
                            ofm[ic][:, 512 * tt:512 * (tt + 1)],
                            start=(ic == 0), stop=True,
                            skip_group_check=True)
                    return last
                E("tensor", wo_fn,
                  reads=["wo_b"] + [("ofm", ic) for ic in range(4)],
                  writes=[("psWO", sl)])
                E("vector", lambda h, tt=tt, oc=oc, sl=sl: h.tensor_add(
                    xs[oc][:, QOFF + 512 * tt:QOFF + 512 * (tt + 1)],
                    xs[oc][:, QOFF + 512 * tt:QOFF + 512 * (tt + 1)],
                    psWO[sl][:, :]),
                  reads=[("psWO", sl), ("x", oc)], writes=[("x", oc)])
        p.barrier()
    oscope.close()

    with contextlib.ExitStack() as ph:
        Sa = lambda n, *a: ph.enter_context(nc.sbuf_tensor(_u(n), *a))
        PSa = lambda n, *a: ph.enter_context(nc.psum_tensor(_u(n), *a))
        lnt = [Sa(f"lnt{c}", [128, OWN], BF16) for c in range(8)]
        hg = [Sa(f"hg{k}", [128, OWN], BF16) for k in range(8)]
        w2h = [Sa(f"w2h{k}", [128, C], BF16) for k in range(6)]
        wa_b = [Sa(f"wa{s}", [128, 8 * 128], BF16) for s in range(4)]
        wg_b = [Sa(f"wg{s}", [128, 8 * 128], BF16) for s in range(4)]
        xbf = [Sa(f"xbf{s}", [128, 1024], BF16) for s in range(2)]
        sqf = [Sa(f"sqf{s}", [128, 1024], BF16) for s in range(2)]
        gel = [Sa("gel0", [128, 512], BF16)] * 2
        scrF = Sa("scrF", [128, 1024], F32)
        scrM = Sa("scrM", [128, 1024], F32)
        scrF2 = Sa("scrF2", [128, 1024], F32)
        scrR2 = Sa("scrR2", [128, 1024], F32)
        psBig = [PSa(f"psBig{s}", [128, 1024], F32) for s in range(2)]
        psW1 = [PSa(f"psW1{s}", [128, 512], F32) for s in range(2)]
        psW2 = [PSa(f"psW2{s}", [128, 512], F32) for s in range(2)]

        for tgh in range(2):
            base = QOFF + 1024 * tgh
            for c in range(8):
                sl = c % 2
                sh = 0 if c < 4 else -1
                rds = [("x", c), ("xh", c)] if (sh and tgh == 0) \
                    else [("x", c)]
                if c % 2 == 0:
                    E("vector", lambda h, c=c, sl=sl, sh=sh, base=base:
                      h.tensor_copy(
                          xbf[sl][:], xs[c][:, base + sh:base + sh + 1024]),
                      reads=rds, writes=[("xbf", sl)])
                else:
                    E("scalar", lambda h, c=c, sl=sl, sh=sh, base=base:
                      h.copy(
                          xbf[sl][:], xs[c][:, base + sh:base + sh + 1024]),
                      reads=rds, writes=[("xbf", sl)])
                E("scalar", lambda h, sl=sl: h.square(sqf[sl][:],
                                                      xbf[sl][:]),
                  reads=[("xbf", sl)], writes=[("sqf", sl)])

                def stat_fn(h, c=c, sl=sl, tgh=tgh):
                    last = None
                    for hf in range(2):
                        if tgh == 0:
                            ps_s = psBig[0][:, 512 * hf:512 * (hf + 1)]
                            ps_q = psBig[1][:, 512 * hf:512 * (hf + 1)]
                        else:
                            ps_s = psW1[hf][:, :]
                            ps_q = psW2[hf][:, :]
                        h.matmul(ps_s, onesbb[:],
                                 xbf[sl][:, 512 * hf:512 * (hf + 1)],
                                 start=(c == 0), stop=True,
                                 skip_group_check=True)
                        last = h.matmul(
                            ps_q, onesbb[:],
                            sqf[sl][:, 512 * hf:512 * (hf + 1)],
                            start=(c == 0), stop=True, skip_group_check=True)
                    return last
                wkeys = [("psBig", 0), ("psBig", 1)] if tgh == 0 else \
                    [("psW1", 0), ("psW1", 1), ("psW2", 0), ("psW2", 1)]
                E("tensor", stat_fn,
                  reads=[("xbf", sl), ("sqf", sl), "onesbb"], writes=wkeys)

        E("vector", lambda h: h.tensor_scalar_mul(
            scrF[:], psBig[0][:, :], 1.0 / C),
          reads=[("psBig", 0)], writes=["scrF"])
        E("vector", lambda h: h.tensor_mul(scrM[:], scrF[:], scrF[:]),
          reads=["scrF"], writes=["scrM"])
        E("vector", lambda h: h.scalar_tensor_tensor(
            psBig[1][:, :], psBig[1][:, :], 1.0 / C, scrM[:],
            op0=ALU.mult, op1=ALU.subtract),
          reads=[("psBig", 1), "scrM"], writes=[("psBig", 1)])
        E("scalar", lambda h: h.activation(
            psBig[1][:, :], psBig[1][:, :], AF.Sqrt, bias=epsc[:, 0:1]),
          reads=[("psBig", 1)], writes=[("psBig", 1)])
        E("vector", lambda h: h.reciprocal(psBig[1][:, :], psBig[1][:, :]),
          reads=[("psBig", 1)], writes=[("psBig", 1)])
        for hf in range(2):
            E("vector", lambda h, hf=hf: h.tensor_scalar_mul(
                scrF2[:, 512 * hf:512 * (hf + 1)], psW1[hf][:, :], 1.0 / C),
              reads=[("psW1", hf)], writes=["scrF2"])
        E("vector", lambda h: h.tensor_mul(scrM[:], scrF2[:], scrF2[:]),
          reads=["scrF2"], writes=["scrM"])
        for hf in range(2):
            E("vector", lambda h, hf=hf: h.scalar_tensor_tensor(
                psW2[hf][:, :], psW2[hf][:, :], 1.0 / C,
                scrM[:, 512 * hf:512 * (hf + 1)],
                op0=ALU.mult, op1=ALU.subtract),
              reads=[("psW2", hf), "scrM"], writes=[("psW2", hf)])
            E("scalar", lambda h, hf=hf: h.activation(
                psW2[hf][:, :], psW2[hf][:, :], AF.Sqrt, bias=epsc[:, 0:1]),
              reads=[("psW2", hf)], writes=[("psW2", hf)])
            E("vector", lambda h, hf=hf: h.reciprocal(
                scrR2[:, 512 * hf:512 * (hf + 1)], psW2[hf][:, :]),
              reads=[("psW2", hf)], writes=["scrR2"])

        for tgh in range(2):
            base = QOFF + 1024 * tgh
            mu, rs = (scrF, psBig[1]) if tgh == 0 else (scrF2, scrR2)
            mukey, rskey = ("scrF", ("psBig", 1)) if tgh == 0 else \
                ("scrF2", "scrR2")
            for c in range(8):
                sh = 0 if c < 4 else -1
                rds2 = [("x", c), ("xh", c), mukey] if (sh and tgh == 0) \
                    else [("x", c), mukey]
                E("vector", lambda h, c=c, sh=sh, base=base, mu=mu:
                  h.tensor_sub(
                      scrM[:], xs[c][:, base + sh:base + sh + 1024], mu[:]),
                  reads=rds2, writes=["scrM"])
                E("vector", lambda h, c=c, rs=rs: h.tensor_mul(
                    scrM[:], scrM[:], rs[:, :]),
                  reads=["scrM", rskey], writes=["scrM"])
                E("vector", lambda h, c=c, tgh=tgh: h.tensor_scalar(
                    lnt[c][:, 1024 * tgh:1024 * (tgh + 1)], scrM[:],
                    lng_col(c), lnb_col(c), op0=ALU.mult, op1=ALU.add),
                  reads=["scrM", "bcols"], writes=[("lnt", c, tgh)])

        if f"lnt{l}" in dbg:
            d = dbg_t(f"lnt{l}", [8 * 128, OWN])
            for c in range(8):
                E("gpsimd", lambda h, c=c, d=d: h.dma_start(
                    out=d[128 * c:128 * (c + 1), :], in_=lnt[c][:]),
                  reads=[("lnt", c, 0), ("lnt", c, 1)], dma=True)

        chunks = list(range(NHC))

        def load_chunk(k, hc):
            w = min(128, HID - 128 * hc)
            wsl = k % 4
            E("gpsimd", lambda h, hc=hc, w=w, wsl=wsl: h.dma_start(
                out=wa_b[wsl][:, 0:8 * w].rearrange(
                    "p (ic n) -> p ic n", ic=8),
                in_=w1[l, :, 128 * hc:128 * hc + w].rearrange(
                    "(ic p) n -> p ic n", p=128)),
              writes=[("wa_b", wsl)], dma=True)
            E("gpsimd", lambda h, hc=hc, w=w, wsl=wsl: h.dma_start(
                out=wg_b[wsl][:, 0:8 * w].rearrange(
                    "p (ic n) -> p ic n", ic=8),
                in_=w1[l, :, HID + 128 * hc:HID + 128 * hc + w]
                .rearrange("(ic p) n -> p ic n", p=128)),
              writes=[("wg_b", wsl)], dma=True)


        def emit_w2_quarter(qi, hcs, is_last):
            for k2, hc2 in enumerate(hcs):
                w2w = min(128, HID - 128 * hc2)
                E("gpsimd", lambda h, hc2=hc2, w2w=w2w, k2=k2: h.dma_start(
                    out=w2h[k2][0:w2w, :],
                    in_=w2[l, 128 * hc2:128 * hc2 + w2w, :]),
                  writes=[("w2h", k2)], dma=True)
            for sub in (3, 2, 1, 0):
                for oc in range(8):
                    sl2 = oc % 2

                    def w2_fn(h, sub=sub, oc=oc, sl2=sl2, hcs=hcs):
                        last = None
                        for k2, hc2 in enumerate(hcs):
                            w2w = min(128, HID - 128 * hc2)
                            last = h.matmul(
                                psW2[sl2][:, :],
                                w2h[k2][0:w2w, 128 * oc:128 * (oc + 1)],
                                hg[hc2 % 8][0:w2w,
                                            512 * sub:512 * (sub + 1)],
                                start=(k2 == 0), stop=True,
                                skip_group_check=True)
                        return last
                    E("tensor", w2_fn,
                      reads=[("w2h", k2) for k2 in range(len(hcs))] +
                            [("hg", hc2 % 8) for hc2 in hcs],
                      writes=[("psW2", sl2)])
                    if not is_last:
                        E("vector", lambda h, oc=oc, sub=sub, sl2=sl2:
                          h.tensor_add(
                              xs[oc][:, QOFF + 512 * sub:
                                     QOFF + 512 * (sub + 1)],
                              xs[oc][:, QOFF + 512 * sub:
                                     QOFF + 512 * (sub + 1)],
                              psW2[sl2][:, :]),
                          reads=[("psW2", sl2), ("x", oc)],
                          writes=[("x", oc)])
                    else:
                        E("vector", lambda h, oc=oc, sub=sub, sl2=sl2:
                          h.scalar_tensor_tensor(
                              xs[oc][:, QOFF + 512 * sub:
                                     QOFF + 512 * (sub + 1)],
                              psW2[sl2][:, :], b2_col(oc),
                              xs[oc][:, QOFF + 512 * sub:
                                     QOFF + 512 * (sub + 1)],
                              op0=ALU.add, op1=ALU.add),
                          reads=[("psW2", sl2), ("x", oc), "bcols"],
                          writes=[("x", oc)])
                if is_last and sub == 2 and l == 0:
                    for c in range(8):
                        E("scalar", lambda h, c=c: h.dma_start(
                            out=cch_in[128 * c:128 * (c + 1), :],
                            in_=xs[c][:, XW - HALO:XW]),
                          reads=[("x", c)], writes=["cch_in"], dma=True)
                    E("gpsimd", lambda h: h.collective_compute(
                        "AllGather", ALU.bypass, replica_groups=RG_PAIRS,
                        ins=[cch_in.ap().opt()], outs=[cch_out.ap().opt()]),
                      reads=["cch_in"], writes=["cch_out"])

        load_chunk(0, 0)
        load_chunk(1, 1)
        for k, hc in enumerate(chunks):
            w = min(128, HID - 128 * hc)
            wsl = k % 4
            kk = k % 8
            if k + 2 < NHC:
                load_chunk(k + 2, hc + 2)
            for sub in range(4):
                par = (4 * k + sub) % 2
                if par == 0:
                    psa, psakey = psW1[0][0:w, :], ("psW1", 0)
                    psg, psgkey = psW1[1][0:w, :], ("psW1", 1)
                else:
                    psa, psakey = psBig[0][0:w, 0:512], ("psBig", 0)
                    psg, psgkey = psBig[0][0:w, 512:1024], ("psBig", 0)

                def w1a_fn(h, w=w, wsl=wsl, sub=sub, psa=psa):
                    last = None
                    for c in range(8):
                        last = h.matmul(
                            psa, wa_b[wsl][:, w * c:w * (c + 1)],
                            lnt[c][:, 512 * sub:512 * (sub + 1)],
                            start=(c == 0), stop=True,
                            skip_group_check=True)
                    return last

                def w1g_fn(h, w=w, wsl=wsl, sub=sub, psg=psg):
                    last = None
                    for c in range(8):
                        last = h.matmul(
                            psg, wg_b[wsl][:, w * c:w * (c + 1)],
                            lnt[c][:, 512 * sub:512 * (sub + 1)],
                            start=(c == 0), stop=True,
                            skip_group_check=True)
                    return last
                E("tensor", w1a_fn,
                  reads=[("wa_b", wsl)] +
                        [("lnt", c, sub // 2) for c in range(8)],
                  writes=[psakey])
                E("tensor", w1g_fn,
                  reads=[("wg_b", wsl)] +
                        [("lnt", c, sub // 2) for c in range(8)],
                  writes=[psgkey])
                E("scalar", lambda h, w=w, hc=hc, sub=sub, psg=psg:
                  h.activation(gel[sub % 2][0:w, :], psg, AF.Gelu,
                               bias=b1g_col(hc, w)),
                  reads=[psgkey, "bcols"], writes=["gel"])
                E("vector", lambda h, w=w, hc=hc, kk=kk, sub=sub, psa=psa:
                  h.scalar_tensor_tensor(
                      hg[kk][0:w, 512 * sub:512 * (sub + 1)],
                      psa, b1a_col(hc, w), gel[sub % 2][0:w, :],
                      op0=ALU.add, op1=ALU.mult),
                  reads=[psakey, "gel", "bcols"],
                  writes=[("hg", kk)])

            if k == 5:
                emit_w2_quarter(0, chunks[0:6], False)
            elif k == 10:
                emit_w2_quarter(1, chunks[6:11], False)
            elif k == 16:
                emit_w2_quarter(2, chunks[11:17], False)
        emit_w2_quarter(3, chunks[17:NHC], True)
        p.barrier()

    if l == 0:
        for c in range(8):
            E("sync", lambda h, c=c: h.dma_start(
                out=xs[c][:, DEAD:DEAD + HALO],
                in_=cch_out[128 * c:128 * (c + 1), :]),
              reads=["cch_out"], writes=[("xh", c)], dma=True)
            E("vector", lambda h, c=c: h.tensor_scalar_mul(
                xs[c][:, 0:QOFF], xs[c][:, 0:QOFF], gate[:, 0:1]),
              reads=[("xh", c), "gate"], writes=[("xh", c)])
    if f"xl{l}" in dbg:
        d = dbg_t(f"xl{l}", [8 * 128, XW])
        for cc in range(8):
            E("sync", lambda h, cc=cc, d=d: h.dma_start(
                out=d[128 * cc:128 * (cc + 1), :], in_=xs[cc][:]),
              reads=[("x", cc)], dma=True)
    p.barrier()


def qkv_and_attention(nc, p, lph, l, g, xs, tabs, masks, bigrow, idb, idf,
                      onesb, onesf, qsb, ksb, wqkv_b, ofm, load_wqkv,
                      dbg, dbg_t):
    E = p.emit
    with contextlib.ExitStack() as ph:
        Sa = lambda n, *a: ph.enter_context(nc.sbuf_tensor(_u(n), *a))
        PSa = lambda n, *a: ph.enter_context(nc.psum_tensor(_u(n), *a))
        xb = [Sa(f"xb{c}", [128, 132], BF16) for c in range(16)]
        qh = [Sa(f"qh{pp}", [128, OWN], BF16) for pp in range(2)]
        kh = [Sa(f"kh{pp}", [128, KW], BF16) for pp in range(2)]
        vx = [Sa(f"vx{j}", [128, 4 * 65], BF16) for j in range(NKC)]
        qkraw = [Sa(f"qkraw{s}", [128, 512], F32) for s in range(2)]
        sqt = [Sa(f"sqt{s}", [128, 512], F32) for s in range(2)]
        t2s = [Sa(f"t2s{s}", [128, 512], F32) for s in range(2)]
        q0s = [Sa(f"q0s{s}", [128, 512], F32) for s in range(2)]
        qkt = [Sa(f"qkt{s}", [128, 512], BF16) for s in range(3)]
        nrm = [Sa(f"nrm{s}", [128, 8], F32) for s in range(2)]
        qps = contextlib.ExitStack()
        PSq = lambda n, *a: qps.enter_context(nc.psum_tensor(_u(n), *a))
        psQKV = [PSq(f"psQKV{s}", [128, 768], F32) for s in range(2)]
        psTrQ = PSq("psTrQ", [128, 128], BF16)
        psTrK = PSq("psTrK", [128, 128], BF16)

        def emit_casts(j):
            xcol = KOFF + 128 * j
            sl8 = 8 * (j % 2)
            halo = xcol - 1 < QOFF
            for c in range(8):
                rds = [("x", c), ("xh", c)] if halo else [("x", c)]
                if c % 2 == 0:
                    E("vector", lambda h, c=c, xcol=xcol, sl8=sl8:
                      h.tensor_copy(xb[sl8 + c][:, 0:129],
                                    xs[c][:, xcol - 1:xcol + 128]),
                      reads=rds, writes=[("xb", sl8 + c)])
                else:
                    E("scalar", lambda h, c=c, xcol=xcol, sl8=sl8:
                      h.copy(xb[sl8 + c][:, 0:129],
                             xs[c][:, xcol - 1:xcol + 128]),
                      reads=rds, writes=[("xb", sl8 + c)])

        def emit_mm(j):
            own = j >= 4
            ps = psQKV[j % 2]
            pskey = ("psQKV", j % 2)
            sl8 = 8 * (j % 2)
            r0 = 0 if own else 1

            def qkv_fn(h, sl8=sl8):
                last = None
                for c in range(8):
                    lhs = xb[sl8 + c][:, 1:129] if c < 4 \
                        else xb[sl8 + c][:, 0:128]
                    for r in range(r0, 3):
                        last = h.matmul(
                            ps[:, 256 * r:256 * (r + 1)], lhs,
                            wqkv_b[:, 768 * c + 256 * r:768 * c + 256 * (r + 1)],
                            start=(c == 0 and (r == r0 or r == 2)),
                            stop=True, skip_group_check=True)
                return last
            E("tensor", qkv_fn,
              reads=[("xb", sl8 + c) for c in range(8)] + ["wqkv_b"],
              writes=[pskey])

        def emit_chunk(j):
            own = j >= 4
            xcol = KOFF + 128 * j
            ps = psQKV[j % 2]
            pskey = ("psQKV", j % 2)
            r0 = 0 if own else 1
            E("scalar", lambda h: h.copy(
                vx[j][:].rearrange("p (hh d) -> p hh d", hh=4)[:, :, 0:64],
                ps[:, 512:768].rearrange("p (hh d) -> p hh d", hh=4)),
              reads=[pskey], writes=[("vx", j)])
            E("gpsimd", lambda h: h.memset(
                vx[j][:].rearrange("p (hh d) -> p hh d", d=65)[:, :, 64:65],
                1.0),
              reads=[("vx", j)], writes=[("vx", j)])
            a = 0 if own else 256
            nh = (512 - a) // 64
            sq = j % 2
            E("scalar", lambda h, a=a, sq=sq: h.square(
                sqt[sq][:, a:512], ps[:, a:512]),
              reads=[pskey], writes=[("sqt", sq)])
            E("scalar", lambda h, a=a, sq=sq: h.copy(qkraw[sq][:, a:512],
                                                     ps[:, a:512]),
              reads=[pskey], writes=[("qkraw", sq)])
            E("vector", lambda h, a=a, nh=nh, sq=sq: h.reduce_sum(
                nrm[sq][:, a // 64:8],
                sqt[sq][:, a:512].rearrange("p (hh d) -> p hh d", d=64),
                axis=AX.X),
              reads=[("sqt", sq)], writes=[("nrm", sq)])
            E("scalar", lambda h, a=a, sq=sq: h.activation(
                nrm[sq][:, a // 64:8], nrm[sq][:, a // 64:8], AF.Sqrt),
              reads=[("nrm", sq)], writes=[("nrm", sq)])
            E("vector", lambda h, a=a, sq=sq: h.tensor_scalar_max(
                nrm[sq][:, a // 64:8], nrm[sq][:, a // 64:8], 1e-12),
              reads=[("nrm", sq)], writes=[("nrm", sq)])
            E("vector", lambda h, a=a, sq=sq: h.reciprocal(
                nrm[sq][:, a // 64:8], nrm[sq][:, a // 64:8]),
              reads=[("nrm", sq)], writes=[("nrm", sq)])
            E("vector", lambda h, a=a, nh=nh, sq=sq: h.tensor_mul(
                q0s[sq][:, a:512].rearrange("p (hh d) -> p hh d", d=64),
                qkraw[sq][:, a:512].rearrange("p (hh d) -> p hh d", d=64),
                nrm[sq][:, a // 64:8].rearrange("p (hh o) -> p hh o", o=1)
                .broadcast_to([128, nh, 64])),
              reads=[("qkraw", sq), ("nrm", sq)], writes=[("q0s", sq)])

            def rot_fn(h, a, b_, ct, st, sq=sq):
                nh2 = (b_ - a) // 64
                h.tensor_tensor(
                    sqt[sq][:, a:b_].rearrange("p (hh d) -> p hh d", d=64),
                    q0s[sq][:, a:b_].rearrange("p (hh d) -> p hh d", d=64),
                    ct[:, DH * j:DH * (j + 1)].rearrange(
                        "p (o d) -> p o d", o=1).broadcast_to([128, nh2, 64]),
                    op=ALU.mult)
                h.tensor_tensor(
                    t2s[sq][:, a:b_].rearrange("p (hh d) -> p hh d", d=64)
                    [:, :, 0:32],
                    q0s[sq][:, a:b_].rearrange("p (hh d) -> p hh d", d=64)
                    [:, :, 32:64],
                    st[:, DH * j:DH * j + 32].rearrange(
                        "p (o d) -> p o d", o=1).broadcast_to([128, nh2, 32]),
                    op=ALU.mult)
                last2 = h.tensor_tensor(
                    t2s[sq][:, a:b_].rearrange("p (hh d) -> p hh d", d=64)
                    [:, :, 32:64],
                    q0s[sq][:, a:b_].rearrange("p (hh d) -> p hh d", d=64)
                    [:, :, 0:32],
                    st[:, DH * j + 32:DH * (j + 1)].rearrange(
                        "p (o d) -> p o d", o=1).broadcast_to([128, nh2, 32]),
                    op=ALU.mult)
                return last2

            if own:
                E("vector", lambda h: rot_fn(h, 0, 256, tabs[0], tabs[1]),
                  reads=[("q0s", sq), ("tab", 0), ("tab", 1)],
                  writes=[("sqt", sq), ("t2s", sq)])
            E("vector", lambda h: rot_fn(h, 256, 512, tabs[2], tabs[3]),
              reads=[("q0s", sq), ("tab", 2), ("tab", 3)],
              writes=[("sqt", sq), ("t2s", sq)])
            a0 = 0 if own else 256
            E("gpsimd", lambda h, a0=a0, j=j, sq=sq: h.tensor_add(
                qkt[j % 3][:, a0:512], sqt[sq][:, a0:512],
                t2s[sq][:, a0:512]),
              reads=[("sqt", sq), ("t2s", sq)], writes=[("qkt", j % 3)])
        def emit_transposes(j):
            own = j >= 4
            for pp in range(2):
                if own:
                    E("tensor", lambda h, pp=pp, j=j: h.transpose(
                        psTrQ[:], qkt[j % 3][:, 128 * pp:128 * (pp + 1)],
                        idb[:]),
                      reads=[("qkt", j % 3)], writes=["psTrQ"])
                    E("scalar", lambda h, pp=pp, j=j: h.copy(
                        qh[pp][:, 128 * (j - 4):128 * (j - 3)],
                        psTrQ[:]),
                      reads=["psTrQ"], writes=[("qh", pp)])
                E("tensor", lambda h, pp=pp, j=j: h.transpose(
                    psTrK[:],
                    qkt[j % 3][:, 256 + 128 * pp:256 + 128 * (pp + 1)],
                    idb[:]),
                  reads=[("qkt", j % 3)], writes=["psTrK"])
                E("scalar", lambda h, pp=pp, j=j: h.copy(
                    kh[pp][:, 128 * j:128 * (j + 1)], psTrK[:]),
                  reads=["psTrK"], writes=[("kh", pp)])

        order = [(jj + 4) % NKC for jj in range(NKC)]
        emit_casts(order[0])
        for i, j in enumerate(order):
            emit_mm(j)
            if i + 1 < len(order):
                emit_casts(order[i + 1])
            emit_chunk(j)
            if i > 1:
                emit_transposes(order[i - 2])
        emit_transposes(order[-2])
        emit_transposes(order[-1])
        p.barrier()
        qps.close()
        if g == 0:
            load_wqkv(1)

        with contextlib.ExitStack() as ph2:
            Sb = lambda n, *a: ph2.enter_context(nc.sbuf_tensor(_u(n), *a))
            PSb = lambda n, *a: ph2.enter_context(nc.psum_tensor(_u(n), *a))
            psS = PSb("psS", [128, 640], F32)
            psST = [PSb(f"psST{s}", [128, 640], F32) for s in range(2)]
            psO = [PSb(f"psO{s}", [128, 128], F32) for s in range(2)]
            PT = [Sb(f"PT{s}", [128, 640], BF16) for s in range(2)]
            negmS = Sb("negmS", [128, 2], F32)
            negmrow = [Sb(f"negmrow{s}", [1, 128], BF16) for s in range(2)]
            negmrow4 = [Sb(f"negmrow4{s}", [1, 512], BF16) for s in range(2)]
            ouq = [Sb(f"ouq{s}", [128, 128], F32) for s in range(2)]
            denr = [Sb(f"denr{s}", [1, 128], F32) for s in range(2)]

            for hh in range(4):
                hglob = 4 * g + hh
                pp, hb = hh // 2, 64 * (hh % 2)

                def av_mm(qc, hglob=hglob):
                    sl = qc % 2

                    def av_fn(h, qc=qc, sl=sl, hglob=hglob):
                        last = None
                        for i in range(5):
                            last = h.matmul(
                                psO[sl][0:65, :],
                                vx[qc + i][:, 65 * (hglob % 4):
                                           65 * (hglob % 4) + 65],
                                PT[sl][:, 128 * i:128 * (i + 1)],
                                start=(i == 0), stop=True,
                                skip_group_check=True)
                        return last
                    E("tensor", av_fn,
                      reads=[("PT", sl)] + [("vx", qc + i)
                                            for i in range(5)],
                      writes=[("psO", sl)])
                    E("scalar", lambda h, sl=sl: h.copy(
                        ouq[sl][0:65, :], psO[sl][0:65, :]),
                      reads=[("psO", sl)], writes=[("ouq", sl)])

                def av_recip(qc):
                    sl = qc % 2
                    E("vector", lambda h, sl=sl: h.reciprocal(
                        denr[sl][:], ouq[sl][64:65, :]),
                      reads=[("ouq", sl)], writes=[("denr", sl)])

                def av_norm(qc, pp=pp, hb=hb):
                    sl = qc % 2
                    tb = qc % 2
                    E("tensor", lambda h, sl=sl, tb=tb: h.matmul(
                        psO[tb][0:64, :], onesf[0:1, 0:64], denr[sl][:],
                        start=True, stop=True, skip_group_check=True),
                      reads=[("denr", sl)], writes=[("psO", tb)])
                    E("vector", lambda h, sl=sl, tb=tb, qc=qc, pp=pp,
                      hb=hb: h.tensor_mul(
                          ofm[2 * g + pp][hb:hb + 64,
                                          128 * qc:128 * (qc + 1)],
                          ouq[sl][0:64, :], psO[tb][0:64, :]),
                      reads=[("ouq", sl), ("psO", tb)],
                      writes=[("ofm", 2 * g + pp)])

                def sweep1(qc):
                    sl = qc % 2

                    def s_fn(h, qc=qc, pp=pp, hb=hb):
                        h.matmul(psS[:, 0:512],
                                 qh[pp][hb:hb + 64,
                                        128 * qc:128 * (qc + 1)],
                                 kh[pp][hb:hb + 64,
                                        128 * qc:128 * qc + 512],
                                 start=True, stop=True,
                                 skip_group_check=True)
                        h.matmul(psS[:, 512:640],
                                 qh[pp][hb:hb + 64,
                                        128 * qc:128 * (qc + 1)],
                                 kh[pp][hb:hb + 64,
                                        128 * qc + 512:128 * qc + 640],
                                 start=True, stop=True,
                                 skip_group_check=True)
                        h.matmul(psS[:, 0:128], idb[:], masks[0][:],
                                 start=False, stop=True,
                                 skip_group_check=True)
                        last = h.matmul(psS[:, 512:640], idb[:],
                                        masks[1][:], start=False,
                                        stop=True, skip_group_check=True)
                        for i in range(max(0, 4 - qc)):
                            last = h.matmul(
                                psS[:, 128 * i:128 * (i + 1)], onesb[:],
                                bigrow[:], start=False, stop=True,
                                skip_group_check=True)
                        return last
                    E("tensor", s_fn,
                      reads=[("qh", pp), ("kh", pp), ("msk", 0),
                             ("msk", 1), "bigrow", "idb", "onesb"],
                      writes=["psS"])
                    E("vector", lambda h, sl=sl: h.reduce_max(
                        negmS[:, sl:sl + 1], psS[:, 0:640], axis=AX.X,
                        negate=True),
                      reads=["psS"], writes=["negmS"])

                def st_a(qc):
                    sl = qc % 2

                    def fn(h, qc=qc, sl=sl, pp=pp, hb=hb):
                        for i in range(5):
                            h.matmul(
                                psST[sl][:, 128 * i:128 * (i + 1)],
                                kh[pp][hb:hb + 64,
                                       128 * (qc + i):128 * (qc + i + 1)],
                                qh[pp][hb:hb + 64,
                                       128 * qc:128 * (qc + 1)],
                                start=(i == 0 or i == 4), stop=True,
                                skip_group_check=True)
                        h.matmul(psST[sl][:, 0:128], idb[:], masks[2][:],
                                 start=False, stop=True,
                                 skip_group_check=True)
                        last = h.matmul(psST[sl][:, 512:640], idb[:],
                                        masks[3][:], start=False,
                                        stop=True, skip_group_check=True)
                        for i in range(max(0, 4 - qc)):
                            last = h.matmul(
                                psST[sl][:, 128 * i:128 * (i + 1)],
                                onesb[:], bigrow[:], start=False,
                                stop=True, skip_group_check=True)
                        return last
                    E("tensor", fn,
                      reads=[("qh", pp), ("kh", pp), ("msk", 2), ("msk", 3),
                             "bigrow", "idb", "onesb"],
                      writes=[("psST", sl)])

                def negm_row(qc):
                    sl = qc % 2
                    E("tensor", lambda h, sl=sl: h.transpose(
                        psS[0:1, 0:128], negmS[:, sl:sl + 1], idf[:]),
                      reads=["negmS"], writes=["psS"])
                    E("scalar", lambda h, sl=sl: h.copy(
                        negmrow[sl][0:1, :], psS[0:1, 0:128]),
                      reads=["psS"], writes=[("negmrow", sl)])
                    E("vector", lambda h, sl=sl: h.tensor_copy(
                        negmrow4[sl][0:1, :].rearrange(
                            "p (o n) -> p o n", o=4),
                        negmrow[sl][0:1, :].rearrange(
                            "p (o n) -> p o n", o=1)
                        .broadcast_to([1, 4, 128])),
                      reads=[("negmrow", sl)], writes=[("negmrow4", sl)])

                def st_b(qc):
                    sl = qc % 2

                    def fn(h, sl=sl):
                        h.matmul(psST[sl][:, 0:512], onesb[:],
                                 negmrow4[sl][0:1, :], start=False,
                                 stop=True, skip_group_check=True)
                        return h.matmul(psST[sl][:, 512:640], onesb[:],
                                        negmrow[sl][0:1, :], start=False,
                                        stop=True, skip_group_check=True)
                    E("tensor", fn,
                      reads=[("negmrow4", sl), ("negmrow", sl)],
                      writes=[("psST", sl)])
                    E("scalar", lambda h, sl=sl: h.activation(
                        PT[sl][:], psST[sl][:], AF.Exp, scale=QK_SCALE),
                      reads=[("psST", sl)], writes=[("PT", sl)])

                for qc in range(NQC):
                    if qc > 2:
                        av_recip(qc - 3)
                    sweep1(qc)
                    st_a(qc)
                    negm_row(qc)
                    if qc > 0:
                        st_b(qc - 1)
                    if qc > 1:
                        av_mm(qc - 2)
                    if qc > 2:
                        av_norm(qc - 3)
                st_b(NQC - 1)
                av_mm(NQC - 2)
                av_recip(NQC - 3)
                av_norm(NQC - 3)
                av_mm(NQC - 1)
                av_recip(NQC - 2)
                av_norm(NQC - 2)
                av_recip(NQC - 1)
                av_norm(NQC - 1)

            p.barrier()


def finale(nc, p, es, x_in, out_d, xs, idf, onesf, epsc, vec8):
    E = p.emit
    with contextlib.ExitStack() as ph:
        Sa = lambda n, *a: ph.enter_context(nc.sbuf_tensor(_u(n), *a))
        PSa = lambda n, *a: ph.enter_context(nc.psum_tensor(_u(n), *a))
        xt = [Sa(f"xtF{s}", [128, C], F32) for s in range(2)]
        ft = [Sa(f"ftF{s}", [128, C], F32) for s in range(2)]
        ot = [Sa(f"otF{s}", [128, C], F32) for s in range(2)]
        tiny = Sa("tinyF", [128, 32], F32)
        mrow = Sa("mrowF", [1, 128], F32)
        vecb = Sa("vecbF", [128, C], F32)
        psF = [PSa(f"psF{s}", [128, 512], F32) for s in range(4)]
        psV = PSa("psVF", [128, 128], F32)

        for cc in range(8):
            E("tensor", lambda h, cc=cc: h.transpose(
                psV[0:1, 0:128], vec8[:, cc:cc + 1], idf[:]),
              reads=["vec8"], writes=["psV"])
            E("scalar", lambda h: h.copy(mrow[0:1, 0:128],
                                         psV[0:1, 0:128]),
              reads=["psV"], writes=["mrowF"])
            E("tensor", lambda h, cc=cc: h.matmul(
                psV[:, 0:128], onesf[:], mrow[0:1, 0:128],
                start=True, stop=True, skip_group_check=True),
              reads=["mrowF", "onesf"], writes=["psV"])
            E("scalar", lambda h, cc=cc: h.copy(
                vecb[:, 128 * cc:128 * (cc + 1)], psV[:, 0:128]),
              reads=["psV"], writes=["vecbF"])

        def fin_tile(i):
            sl = i % 2
            col = QOFF + 128 * i
            E("sync", lambda h: h.dma_start(
                out=xt[sl][:],
                in_=x_in[QOFF + 128 * i:QOFF + 128 * (i + 1), :]),
              writes=[("xtF", sl)], dma=True)
            for cc in range(8):
                pf = 2 * sl + cc // 4
                E("tensor", lambda h, cc=cc, pf=pf: h.transpose(
                    psF[pf][:, 128 * (cc % 4):128 * (cc % 4 + 1)],
                    xs[cc][:, col:col + 128], idf[:]),
                  reads=[("x", cc)], writes=[("psF", pf)],
                  inc=(cc % 4 == 3))
            for hf in range(2):
                E("scalar", lambda h, hf=hf: h.copy(
                    ft[sl][:, 512 * hf:512 * (hf + 1)],
                    psF[2 * sl + hf][:, 0:512]),
                  reads=[("psF", 2 * sl + hf)], writes=[("ftF", sl)])
            tb = 16 * sl
            E("vector", lambda h, tb=tb: h.bn_stats(
                tiny[:, tb:tb + 6], ft[sl][:, 0:512]),
              reads=[("ftF", sl)], writes=[("tinyF", sl)])
            E("vector", lambda h, tb=tb: h.bn_stats(
                tiny[:, tb + 6:tb + 12], ft[sl][:, 512:C]),
              reads=[("ftF", sl)], writes=[("tinyF", sl)])
            E("vector", lambda h, tb=tb: h.bn_aggr(
                tiny[:, tb + 12:tb + 14], tiny[:, tb:tb + 12]),
              reads=[("tinyF", sl)], writes=[("tinyF", sl)])
            E("scalar", lambda h, tb=tb: h.activation(
                tiny[:, tb + 13:tb + 14], tiny[:, tb + 13:tb + 14],
                AF.Sqrt, bias=epsc[:, 0:1]),
              reads=[("tinyF", sl)], writes=[("tinyF", sl)])
            E("vector", lambda h, tb=tb: h.reciprocal(
                tiny[:, tb + 13:tb + 14], tiny[:, tb + 13:tb + 14]),
              reads=[("tinyF", sl)], writes=[("tinyF", sl)])
            E("vector", lambda h, tb=tb: h.tensor_scalar(
                ft[sl][:], ft[sl][:], tiny[:, tb + 12:tb + 13],
                tiny[:, tb + 13:tb + 14],
                op0=ALU.subtract, op1=ALU.mult),
              reads=[("ftF", sl), ("tinyF", sl)], writes=[("ftF", sl)])
            E("vector", lambda h: h.tensor_add(ot[sl][:], xt[sl][:],
                                               vecb[:]),
              reads=[("xtF", sl), "vecbF"], writes=[("otF", sl)])
            E("vector", lambda h: h.tensor_add(ot[sl][:], ot[sl][:],
                                               ft[sl][:]),
              reads=[("otF", sl), ("ftF", sl)], writes=[("otF", sl)])
            E("sync", lambda h: h.dma_start(
                out=out_d[128 * i:128 * (i + 1), :], in_=ot[sl][:]),
              reads=[("otF", sl)], writes=[("outd", i)], dma=True)

        for i in range(NQC):
            fin_tile(i)
        p.barrier()


_BUILT = {}


def _get_program(dbg=()):
    key = tuple(sorted(map(str, dbg)))
    if key not in _BUILT:
        _BUILT[key] = build_program(dbg)
    return _BUILT[key]


def make_core_inputs(inputs, core):
    b, half = core // 2, core % 2
    t0 = half * OWN
    x = np.asarray(inputs["L0_attn_x"])
    xc = np.zeros((XW, C), dtype=np.float32)
    xc[QOFF:] = x[b, t0:t0 + OWN]
    if half == 1:
        xc[DEAD:QOFF] = x[b, t0 - HALO:t0]
    cs_q, sn_q, cs_k, sn_k = _rotary_tables(t0 - 512)
    upS, loS, upT, loT = _masks()
    big = np.full((1, 128), NEG if half == 0 else 0.0, dtype=np.float32)
    return {
        "x": xc,
        "tabi0": cs_q, "tabi1": sn_q, "tabi2": cs_k, "tabi3": sn_k,
        "maski0": np.asarray(upS), "maski1": np.asarray(loS),
        "maski2": np.asarray(upT), "maski3": np.asarray(loT),
        "bigrow": big.astype(ml_dtypes.bfloat16),
        "gate": np.full((128, 1), float(half), dtype=np.float32),
        "a_wqkv": np.asarray(inputs["L0_attn_w_qkv"], dtype=np.float32),
        "a_wc1": np.asarray(inputs["L0_attn_w_conv1"],
                            dtype=np.float32).reshape(1, C),
        "f_wqkv": np.asarray(inputs["L0_ffn_wqkv"], dtype=np.float32),
        "f_qs": np.asarray(inputs["L0_ffn_q_scale"], dtype=np.float32),
        "f_ks": np.asarray(inputs["L0_ffn_k_scale"], dtype=np.float32),
        "f_wo": np.asarray(inputs["L0_ffn_wo"], dtype=np.float32),
        "f_lng": np.asarray(inputs["L0_ffn_ln_g"], dtype=np.float32),
        "f_lnb": np.asarray(inputs["L0_ffn_ln_b"], dtype=np.float32),
        "f_w1": np.asarray(inputs["L0_ffn_w1"], dtype=np.float32),
        "f_b1": np.asarray(inputs["L0_ffn_b1"], dtype=np.float32),
        "f_w2": np.asarray(inputs["L0_ffn_w2"], dtype=np.float32),
        "f_b2": np.asarray(inputs["L0_ffn_b2"], dtype=np.float32),
    }


def run(inputs, dbg=()):
    nc, dbg_out = _get_program(dbg)
    in_maps = [make_core_inputs(inputs, c) for c in range(NCORE)]
    return run_bass_kernel_spmd(nc, in_maps, core_ids=list(range(NCORE)))


def kernel(**inputs):
    res = run(inputs)
    out = np.zeros((B, N, C), dtype=np.float32)
    for c in range(NCORE):
        b, half = c // 2, c % 2
        out[b, half * OWN:(half + 1) * OWN] = res.results[c]["out"]
    return out


# version 29 (final)
# speedup vs baseline: 1.1135x; 1.1113x over previous
import contextlib
import numpy as np
import ml_dtypes

import concourse.bass as bass
import concourse.mybir as mybir
from concourse.bass_utils import run_bass_kernel_spmd

F32 = mybir.dt.float32
BF16 = mybir.dt.bfloat16
AF = mybir.ActivationFunctionType
ALU = mybir.AluOpType
AX = mybir.AxisListType

B, N, C = 4, 4096, 1024
HEADS, DH, INNER = 8, 64, 512
HID = 2730
NHC = 22
NCORE = 8
OWN = 2048
HALO = 513
DEAD = 127
XW = DEAD + HALO + OWN
QOFF = DEAD + HALO
KOFF = 128
KW = 2560
NKC = 20
NQC = 16
NXT = 21
NEG = -1.0e30
EPS = 1e-5
QK_SCALE = 8.0
RG_PAIRS = [[0, 1], [2, 3], [4, 5], [6, 7]]
ENGINES = ("tensor", "vector", "scalar", "gpsimd", "sync")
_UID = [0]


def _u(name):
    _UID[0] += 1
    return f"{name}_u{_UID[0]}"



def _rotary_tables(t0):
    t = t0 + np.arange(KW, dtype=np.float64)
    half = np.arange(0, DH, 2, dtype=np.float64)
    inv_freq = 1.0 / 10000.0 ** (half / DH)
    fr = t[:, None] * inv_freq[None, :]
    fr = np.concatenate([fr, fr], axis=1)
    base = (half + 0.4 * DH) / (1.4 * DH)
    power = (t - N // 2) / 256.0
    sc = base[None, :] ** power[:, None]
    sc = np.concatenate([sc, sc], axis=1)
    cos, sin = np.cos(fr), np.sin(fr)
    sgn = np.where(np.arange(DH) < DH // 2, -1.0, 1.0)[None, :]
    out = []
    for tab in (cos * sc, sgn * sin * sc, cos / sc, sgn * sin / sc):
        m = np.zeros((128, NKC * DH), dtype=np.float32)
        for j in range(NKC):
            m[:, DH * j:DH * (j + 1)] = tab[128 * j:128 * (j + 1), :]
        out.append(m)
    return out


def _masks():
    r = np.arange(128)
    up = np.where(r[None, :] >= r[:, None], 0.0, NEG).astype(np.float32)
    lo = np.where(r[None, :] <= r[:, None], 0.0, NEG).astype(np.float32)
    bf = ml_dtypes.bfloat16
    return (up.astype(bf), lo.astype(bf),
            up.T.copy().astype(bf), lo.T.copy().astype(bf))


class P:
    def __init__(self, nc, es):
        self.nc = nc
        self.ops = {e: [] for e in ENGINES}
        self.sem = {e: es.enter_context(nc.semaphore(f"sm_{e}")) for e in ENGINES}
        self.sem["gdma"] = es.enter_context(nc.semaphore("sm_gdma"))
        self.cnt = {e: 0 for e in ENGINES}
        self.cnt["gdma"] = 0
        self.waited = {}
        self.tiles = {}

    def _edge(self, edges, eng, cnt):
        if cnt is None:
            cnt = self.cnt[eng]
        if cnt > 0:
            edges[eng] = max(edges.get(eng, 0), cnt)

    def emit(self, eng, fn, reads=(), writes=(), dma=False, inc=True):
        edges = {}
        for k in reads:
            st = self.tiles.get(k)
            if st and st["w"] is not None:
                self._edge(edges, *st["w"])
        for k in writes:
            st = self.tiles.get(k)
            if st:
                if st["w"] is not None:
                    self._edge(edges, *st["w"])
                for re_, rc in st["r"].items():
                    self._edge(edges, re_, rc)
        if eng == "tensor":
            edges.pop("tensor", None)
        real = []
        for se, c in edges.items():
            if self.waited.get((se, eng), 0) >= c:
                continue
            real.append((self.sem[se], c))
            self.waited[(se, eng)] = c
        sem_eng = "gdma" if (dma and eng == "gpsimd") else eng
        amt = 16 if dma else 1
        if inc:
            self.cnt[sem_eng] += amt
        my = self.cnt[sem_eng]
        me = self.sem[sem_eng]

        def thunk(h, real=real, fn=fn, inc=inc, me=me, amt=amt):
            for s, c in real:
                h.wait_ge(s, c)
            inst = fn(h)
            if inc:
                inst.then_inc(me, amt)

        self.ops[eng].append(thunk)
        wrec = (sem_eng, my if (inc and not dma) else None)
        for k in writes:
            self.tiles[k] = {"w": wrec, "r": {}}
        for k in reads:
            st = self.tiles.setdefault(k, {"w": None, "r": {}})
            if inc and not dma:
                prev = st["r"].get(eng)
                st["r"][eng] = my if prev is not None or eng not in st["r"] \
                    else my
            else:
                st["r"][sem_eng] = None

    def barrier(self):
        counts = dict(self.cnt)
        for e in ENGINES:
            waits = []
            for o in list(ENGINES) + ["gdma"]:
                if o == e or counts[o] == 0:
                    continue
                if self.waited.get((o, e), 0) < counts[o]:
                    waits.append((self.sem[o], counts[o]))
                    self.waited[(o, e)] = counts[o]
            if waits:
                def thunk(h, waits=waits):
                    for s, c in waits:
                        h.wait_ge(s, c)
                self.ops[e].append(thunk)

    def finalize(self, block):
        for e, reg in (("tensor", block.tensor), ("vector", block.vector),
                       ("scalar", block.scalar), ("gpsimd", block.gpsimd),
                       ("sync", block.sync)):
            ops = self.ops[e]

            def run(h, ops=ops):
                for f in ops:
                    f(h)

            reg(run)


def build_program(dbg=()):
    nc = bass.Bass("TRN2", target_bir_lowering=False)
    dt = nc.dram_tensor

    x_in = dt("x", [XW, C], F32, kind="ExternalInput").ap()
    tabs_in = [dt(f"tabi{i}", [128, NKC * DH], F32, kind="ExternalInput").ap()
               for i in range(4)]
    m_in = [dt(f"maski{i}", [128, 128], BF16, kind="ExternalInput").ap()
            for i in range(4)]
    bigrow_in = dt("bigrow", [1, 128], BF16, kind="ExternalInput").ap()
    gate_in = dt("gate", [128, 1], F32, kind="ExternalInput").ap()
    aw_qkv = dt("a_wqkv", [3 * C, C], F32, kind="ExternalInput").ap()
    aw_c1 = dt("a_wc1", [1, C], F32, kind="ExternalInput").ap()
    wqkv = dt("f_wqkv", [2, C, 3 * INNER], F32, kind="ExternalInput").ap()
    qsc = dt("f_qs", [2, DH], F32, kind="ExternalInput").ap()
    ksc = dt("f_ks", [2, DH], F32, kind="ExternalInput").ap()
    wo = dt("f_wo", [2, INNER, C], F32, kind="ExternalInput").ap()
    lng = dt("f_lng", [2, C], F32, kind="ExternalInput").ap()
    lnb = dt("f_lnb", [2, C], F32, kind="ExternalInput").ap()
    w1 = dt("f_w1", [2, C, 2 * HID], F32, kind="ExternalInput").ap()
    b1 = dt("f_b1", [2, 2 * HID], F32, kind="ExternalInput").ap()
    w2 = dt("f_w2", [2, HID, C], F32, kind="ExternalInput").ap()
    b2 = dt("f_b2", [2, C], F32, kind="ExternalInput").ap()
    out_d = dt("out", [OWN, C], F32, kind="ExternalOutput").ap()

    cc1_in = dt("cc1_in", [1, C], F32)
    cc1_out = dt("cc1_out", [2, C], F32)
    cc2_in = dt("cc2_in", [1, C + 8], F32)
    cc2_out = dt("cc2_out", [2, C + 8], F32)
    cch_in = dt("cch_in", [8 * 128, HALO], F32)
    cch_out = dt("cch_out", [2 * 8 * 128, HALO], F32)

    dbg_out = {}

    def dbg_t(name, shape):
        if name not in dbg_out:
            dbg_out[name] = dt("dbg_" + name, shape, F32,
                               kind="ExternalOutput").ap()
        return dbg_out[name]

    es = contextlib.ExitStack()
    with es, nc.Block() as block:
        S = lambda *a: es.enter_context(nc.sbuf_tensor(*a))
        p = P(nc, es)
        E = p.emit

        xs = [S(f"xs{c}", [128, XW], F32) for c in range(8)]
        masks = [S(f"msk{i}", [128, 128], BF16) for i in range(4)]
        bigrow = S("bigrowS", [1, 128], BF16)
        gate = S("gateS", [128, 1], F32)
        idb = S("idb", [128, 128], BF16)
        idf = S("idf", [128, 128], F32)
        onesb = S("onesb", [1, 128], BF16)
        onesf = S("onesf", [1, 128], F32)
        onecolb = S("onecolb", [128, 1], BF16)
        onesbb = S("onesbb", [128, 128], BF16)
        onef1 = S("onef1", [1, 1], F32)
        epsc = S("epscS", [128, 1], F32)
        vec8 = S("vec8", [128, 8], F32)
        bcols = S("bcolsS", [128, 136], F32)
        qsb = S("qsbS", [128, DH], F32)
        ksb = S("ksbS", [128, DH], F32)

        for i in range(4):
            E("sync", lambda h, d=masks[i], s=m_in[i]:
              h.dma_start(out=d[:], in_=s), writes=[("msk", i)], dma=True)
        E("sync", lambda h: h.dma_start(out=bigrow[:], in_=bigrow_in),
          writes=["bigrow"], dma=True)
        E("sync", lambda h: h.dma_start(out=gate[:], in_=gate_in),
          writes=["gate"], dma=True)
        for t, v in ((idb, 1.0), (idf, 1.0), (onesb, 1.0), (onesf, 1.0),
                     (onecolb, 1.0), (onef1, 1.0), (epsc, EPS),
                     (onesbb, 1.0)):
            E("gpsimd", lambda h, t=t, v=v: h.memset(t[:], v),
              writes=[t.name])
        for t in (idb, idf):
            E("gpsimd", lambda h, t=t: h.affine_select(
                out=t[:], in_=t[:], compare_op=ALU.is_equal, fill=0.0,
                pattern=[[-1, 128]], base=0, channel_multiplier=1),
              reads=[t.name], writes=[t.name])

        phase_a(nc, p, es, x_in, aw_qkv, aw_c1, cc1_in, cc1_out, cc2_in,
                cc2_out, xs, gate, idf, onesb, onesf, onef1, epsc, vec8,
                dbg, dbg_t)

        for l in range(2):
            build_layer(nc, p, es, l, xs, masks, bigrow, gate,
                        idb, idf, onesb, onesf, onecolb, onef1, epsc,
                        onesbb, bcols, qsb, ksb,
                        tabs_in, wqkv, qsc, ksc, wo, lng, lnb,
                        w1, b1, w2, b2, cch_in, cch_out, dbg, dbg_t)

        finale(nc, p, es, x_in, out_d, xs, idf, onesf, epsc, vec8)

        p.finalize(block)
    return nc, dbg_out


def phase_a(nc, p, es, x_in, aw_qkv, aw_c1, cc1_in, cc1_out, cc2_in, cc2_out,
            xs, gate, idf, onesb, onesf, onef1, epsc, vec8, dbg, dbg_t):
    E = p.emit
    with contextlib.ExitStack() as ph:
        Sa = lambda n, *a: ph.enter_context(nc.sbuf_tensor(_u(n), *a))
        PSa = lambda n, *a: ph.enter_context(nc.psum_tensor(_u(n), *a))
        xt = [Sa(f"xtA{s}", [128, C], F32) for s in range(2)]
        ln0 = [Sa(f"ln0_{i}", [128, C], BF16) for i in range(NQC)]
        aw = [Sa(f"awA{i}", [128, C], BF16) for i in range(8)]
        scr = [Sa(f"scrA{s}", [128, C], F32) for s in range(2)]
        scr2 = Sa("scrA2", [128, 16], F32)
        rowA = Sa("rowA", [1, C + 8], F32)
        rowG0 = Sa("rowG0", [1, C + 8], F32)
        rowG1 = Sa("rowG1", [1, C + 8], F32)
        rowB = Sa("rowBA", [1, C], BF16)
        ub = Sa("ubA", [128, C], BF16)
        bb = Sa("bbA", [128, C], BF16)
        vecb = Sa("vecbA", [128, C], F32)
        agcol = Sa("agcolA", [128, NQC], F32)
        colb = Sa("colbA", [128, NQC], BF16)
        scol = Sa("scolA", [128, NQC], F32)
        t1c = Sa("t1cA", [128, 8], F32)
        avc = Sa("avcA", [128, 8], F32)
        stat = Sa("statA", [128, 3 * NXT], F32)
        tiny = Sa("tinyA", [128, 32], F32)
        mrow = Sa("mrowA", [1, 160], F32)
        wc1c = Sa("wc1cA", [128, 8], F32)
        wc1cb = Sa("wc1cbA", [128, 8], BF16)
        psT = [PSa(f"psTA{s}", [128, 512], F32) for s in range(2)]
        psRow = PSa("psRowA", [1, C], F32)
        psB = PSa("psBA", [128, 512], F32)
        psB2 = PSa("psB2A", [128, 512], F32)
        psSm = PSa("psSmA", [128, 128], F32)

        def col_to_cell(colap, colkey, cell, op="max", negate=False):
            E("tensor", lambda h: h.transpose(psSm[0:1, 0:128], colap,
                                              idf[:]),
              reads=[colkey], writes=["psSm"])
            E("scalar", lambda h: h.copy(mrow[0:1, 0:128],
                                         psSm[0:1, 0:128]),
              reads=["psSm"], writes=["mrow"])
            if op == "max":
                E("vector", lambda h: h.reduce_max(
                    mrow[0:1, cell:cell + 1], mrow[0:1, 0:128], axis=AX.X,
                    negate=negate),
                  reads=["mrow"], writes=["mrow"])
            else:
                E("vector", lambda h: h.reduce_sum(
                    mrow[0:1, cell:cell + 1], mrow[0:1, 0:128], axis=AX.X),
                  reads=["mrow"], writes=["mrow"])

        def cell_to_col(cell, tcol):
            E("tensor", lambda h: h.matmul(
                psSm[:, 0:1], onesf[:], mrow[0:1, cell:cell + 1],
                start=True, stop=True, skip_group_check=True),
              reads=["mrow", "onesf"], writes=["psSm"])
            E("scalar", lambda h: h.copy(tiny[:, tcol:tcol + 1],
                                         psSm[:, 0:1]),
              reads=["psSm"], writes=[("tiny", 0)])

        def load_aw(which):
            for i in range(8):
                E("gpsimd", lambda h, d=aw[i],
                  s=aw_qkv[C * which + 128 * i:C * which + 128 * (i + 1), :]:
                  h.dma_start(out=d[:], in_=s), writes=[("aw", i)], dma=True)

        load_aw(0)
        E("sync", lambda h: h.dma_start(out=rowA[0:1, 0:C], in_=aw_c1),
          writes=["rowA"], dma=True)

        def wc1_cols(h):
            last = None
            for i in range(8):
                last = h.matmul(psSm[:, i:i + 1],
                                rowA[0:1, 128 * i:128 * (i + 1)], onef1[:],
                                start=True, stop=True, skip_group_check=True)
            return last
        E("tensor", wc1_cols, reads=["rowA", "onef1"], writes=["psSm"])
        E("scalar", lambda h: h.copy(wc1c[:], psSm[:, 0:8]),
          reads=["psSm"], writes=["wc1c"])
        E("vector", lambda h: h.tensor_copy(wc1cb[:], wc1c[:]),
          reads=["wc1c"], writes=["wc1cb"])

        def u_mm(h):
            last = None
            for i in range(8):
                for hf in range(2):
                    last = h.matmul(
                        psRow[0:1, 512 * hf:512 * (hf + 1)],
                        wc1cb[:, i:i + 1],
                        aw[i][:, 512 * hf:512 * (hf + 1)],
                        start=(i == 0), stop=True, skip_group_check=True)
            return last
        E("tensor", u_mm, reads=["wc1cb"] + [("aw", i) for i in range(8)],
          writes=["psRow"])
        E("scalar", lambda h: h.copy(rowA[0:1, 0:C], psRow[:]),
          reads=["psRow"], writes=["rowA"])
        E("vector", lambda h: h.tensor_copy(rowB[:], rowA[0:1, 0:C]),
          reads=["rowA"], writes=["rowB"])

        def bcast_row_to(dst, dstkey):
            for i in range(2):
                E("tensor", lambda h, i=i: h.matmul(
                    psB[:, :], onesb[:], rowB[0:1, 512 * i:512 * (i + 1)],
                    start=True, stop=True, skip_group_check=True),
                  reads=["rowB", "onesb"], writes=["psB"])
                E("scalar", lambda h, i=i: h.copy(
                    dst[:, 512 * i:512 * (i + 1)], psB[:]),
                  reads=["psB"], writes=[dstkey])

        bcast_row_to(ub, "ub")

        def pass1_tile(i):
            sl = i % 2
            tb = 16 * (i % 2)
            E("sync", lambda h: h.dma_start(
                out=xt[sl][:], in_=x_in[128 * i:128 * (i + 1), :]),
              writes=[("xt", sl)], dma=True)
            E("vector", lambda h, tb=tb: h.bn_stats(
                tiny[:, tb:tb + 6], xt[sl][:, 0:512]),
              reads=[("xt", sl)], writes=[("tiny", tb)])
            E("vector", lambda h, tb=tb: h.bn_stats(
                tiny[:, tb + 6:tb + 12], xt[sl][:, 512:C]),
              reads=[("xt", sl)], writes=[("tiny", tb)])
            E("vector", lambda h, tb=tb: h.bn_aggr(
                stat[:, 3 * i:3 * i + 2], tiny[:, tb:tb + 12]),
              reads=[("tiny", tb)], writes=[("stat", i)])
            E("scalar", lambda h: h.activation(
                stat[:, 3 * i + 2:3 * i + 3], stat[:, 3 * i + 1:3 * i + 2],
                AF.Sqrt, bias=epsc[:, 0:1]),
              reads=[("stat", i)], writes=[("stat", i)])
            E("vector", lambda h: h.reciprocal(
                stat[:, 3 * i + 2:3 * i + 3], stat[:, 3 * i + 2:3 * i + 3]),
              reads=[("stat", i)], writes=[("stat", i)])
            if i < 5:
                return
            o = i - 5
            E("vector", lambda h: h.tensor_scalar(
                ln0[o][:], xt[sl][:], stat[:, 3 * i:3 * i + 1],
                stat[:, 3 * i + 2:3 * i + 3], op0=ALU.subtract, op1=ALU.mult),
              reads=[("xt", sl), ("stat", i)], writes=[("ln0", o)])
            E("gpsimd", lambda h: h.tensor_mul(scr[sl][:], ln0[o][:],
                                               ub[:]),
              reads=[("ln0", o), "ub"], writes=[("scr", sl)])

        def pass1_finish(o):
            sl = (o + 5) % 2
            E("vector", lambda h: h.reduce_sum(agcol[:, o:o + 1],
                                               scr[sl][:], axis=AX.X),
              reads=[("scr", sl)], writes=["agcol"])
            E("scalar", lambda h: h.copy(colb[:, o:o + 1],
                                         agcol[:, o:o + 1]),
              reads=["agcol"], writes=["colb"])

            def m_mm(h):
                last = None
                for hf in range(2):
                    last = h.matmul(
                        psRow[0:1, 512 * hf:512 * (hf + 1)], colb[:, o:o + 1],
                        ln0[o][:, 512 * hf:512 * (hf + 1)],
                        start=(o == 0), stop=True, skip_group_check=True)
                return last
            E("tensor", m_mm, reads=["colb", ("ln0", o)], writes=["psRow"],
              inc=(o == NQC - 1))

        for i in range(NXT):
            pass1_tile(i)
            if i >= 6:
                pass1_finish(i - 6)
        pass1_finish(NQC - 1)

        E("scalar", lambda h: h.copy(rowA[0:1, 0:C], psRow[:]),
          reads=["psRow"], writes=["rowA"])
        E("scalar", lambda h: h.dma_start(out=cc1_in[:, :],
                                          in_=rowA[0:1, 0:C]),
          reads=["rowA"], writes=["cc1_in"], dma=True)
        E("gpsimd", lambda h: h.collective_compute(
            "AllGather", ALU.bypass, replica_groups=RG_PAIRS,
            ins=[cc1_in.ap().opt()], outs=[cc1_out.ap().opt()]),
          reads=["cc1_in"], writes=["cc1_out"])
        load_aw(1)
        E("sync", lambda h: h.dma_start(out=rowG0[0:1, 0:C],
                                        in_=cc1_out[0:1, :]),
          reads=["cc1_out"], writes=["rowG"], dma=True)
        E("sync", lambda h: h.dma_start(out=rowG1[0:1, 0:C],
                                        in_=cc1_out[1:2, :]),
          reads=["cc1_out"], writes=["rowG"], dma=True)
        E("vector", lambda h: h.tensor_add(rowA[0:1, 0:C], rowG0[0:1, 0:C],
                                           rowG1[0:1, 0:C]),
          reads=["rowG"], writes=["rowA"])
        E("vector", lambda h: h.tensor_copy(rowB[:], rowA[0:1, 0:C]),
          reads=["rowA"], writes=["rowB"])
        bcast_row_to(bb, "bb")

        for cc in range(8):
            me = "gpsimd" if cc % 2 == 0 else "vector"
            E(me, lambda h, cc=cc: h.tensor_mul(
                scr[cc % 2][:], aw[cc][:], bb[:]),
              reads=[("aw", cc), "bb"], writes=[("scr", cc % 2)])
            E("vector", lambda h, cc=cc: h.reduce_sum(
                t1c[:, cc:cc + 1], scr[cc % 2][:], axis=AX.X),
              reads=[("scr", cc % 2)], writes=["t1c"])
        E("vector", lambda h: h.tensor_scalar_mul(t1c[:], t1c[:],
                                                  float(C) ** (-0.5)),
          reads=["t1c"], writes=["t1c"])

        def softmax_cols(src, nsrc, dstb, key):
            E("vector", lambda h: h.reduce_max(tiny[:, 0:1], src[:, 0:nsrc],
                                               axis=AX.X),
              reads=[key], writes=[("tiny", 0)])
            col_to_cell(tiny[:, 0:1], ("tiny", 0), 128, "max", negate=True)
            cell_to_col(128, 1)
            E("scalar", lambda h: h.activation(src[:, 0:nsrc], src[:, 0:nsrc],
                                               AF.Exp, bias=tiny[:, 1:2]),
              reads=[key, ("tiny", 0)], writes=[key])
            E("vector", lambda h: h.reduce_sum(tiny[:, 2:3], src[:, 0:nsrc],
                                               axis=AX.X),
              reads=[key], writes=[("tiny", 0)])
            col_to_cell(tiny[:, 2:3], ("tiny", 0), 129, "sum")
            E("vector", lambda h: h.reciprocal(mrow[0:1, 130:131],
                                               mrow[0:1, 129:130]),
              reads=["mrow"], writes=["mrow"])
            cell_to_col(130, 3)
            E("vector", lambda h: h.tensor_scalar_mul(
                src[:, 0:nsrc], src[:, 0:nsrc], tiny[:, 3:4]),
              reads=[key, ("tiny", 0)], writes=[key])
            E("vector", lambda h: h.tensor_copy(dstb[:, 0:nsrc],
                                                src[:, 0:nsrc]),
              reads=[key], writes=["colb"])

        softmax_cols(t1c, 8, colb, "t1c")
        load_aw(2)

        def wt_mm(h):
            last = None
            for cc in range(8):
                for hf in range(2):
                    last = h.matmul(
                        psRow[0:1, 512 * hf:512 * (hf + 1)],
                        colb[:, cc:cc + 1],
                        aw[cc][:, 512 * hf:512 * (hf + 1)],
                        start=(cc == 0), stop=True, skip_group_check=True)
            return last
        E("tensor", wt_mm,
          reads=["colb"] + [("aw", cc) for cc in range(8)],
          writes=["psRow"])
        E("scalar", lambda h: h.copy(rowA[0:1, 0:C], psRow[:]),
          reads=["psRow"], writes=["rowA"])
        E("vector", lambda h: h.tensor_copy(rowB[:], rowA[0:1, 0:C]),
          reads=["rowA"], writes=["rowB"])
        bcast_row_to(bb, "bb")

        for o in range(NQC):
            me = "gpsimd" if o % 2 == 0 else "vector"
            E(me, lambda h, o=o: h.tensor_mul(
                scr[o % 2][:], ln0[o][:], bb[:]),
              reads=[("ln0", o), "bb"], writes=[("scr", o % 2)])
            E("vector", lambda h, o=o: h.reduce_sum(
                scol[:, o:o + 1], scr[o % 2][:], axis=AX.X),
              reads=[("scr", o % 2)], writes=["scol"])
        E("vector", lambda h: h.reduce_max(tiny[:, 0:1], scol[:], axis=AX.X),
          reads=["scol"], writes=[("tiny", 0)])
        col_to_cell(tiny[:, 0:1], ("tiny", 0), 131, "max")
        col_to_cell(tiny[:, 0:1], ("tiny", 0), 132, "max", negate=True)
        cell_to_col(132, 1)
        E("scalar", lambda h: h.activation(scol[:], scol[:], AF.Exp,
                                           bias=tiny[:, 1:2]),
          reads=["scol", ("tiny", 0)], writes=["scol"])
        E("vector", lambda h: h.reduce_sum(tiny[:, 2:3], scol[:], axis=AX.X),
          reads=["scol"], writes=[("tiny", 0)])
        col_to_cell(tiny[:, 2:3], ("tiny", 0), 133, "sum")
        E("vector", lambda h: h.tensor_copy(colb[:], scol[:]),
          reads=["scol"], writes=["colb"])

        def p_mm(h):
            last = None
            for o in range(NQC):
                for hf in range(2):
                    last = h.matmul(
                        psRow[0:1, 512 * hf:512 * (hf + 1)],
                        colb[:, o:o + 1], ln0[o][:, 512 * hf:512 * (hf + 1)],
                        start=(o == 0), stop=True, skip_group_check=True)
            return last
        E("tensor", p_mm,
          reads=["colb"] + [("ln0", o) for o in range(NQC)], writes=["psRow"])
        E("scalar", lambda h: h.copy(rowA[0:1, 0:C], psRow[:]),
          reads=["psRow"], writes=["rowA"])
        E("vector", lambda h: h.tensor_copy(rowA[0:1, C:C + 1],
                                            mrow[0:1, 131:132]),
          reads=["mrow"], writes=["rowA"])
        E("vector", lambda h: h.tensor_copy(rowA[0:1, C + 1:C + 2],
                                            mrow[0:1, 133:134]),
          reads=["mrow"], writes=["rowA"])
        E("scalar", lambda h: h.dma_start(out=cc2_in[:, :],
                                          in_=rowA[0:1, 0:C + 8]),
          reads=["rowA"], writes=["cc2_in"], dma=True)
        E("gpsimd", lambda h: h.collective_compute(
            "AllGather", ALU.bypass, replica_groups=RG_PAIRS,
            ins=[cc2_in.ap().opt()], outs=[cc2_out.ap().opt()]),
          reads=["cc2_in"], writes=["cc2_out"])
        E("sync", lambda h: h.dma_start(out=rowG0[:, :],
                                        in_=cc2_out[0:1, :]),
          reads=["cc2_out"], writes=["rowG"], dma=True)
        E("sync", lambda h: h.dma_start(out=rowG1[:, :],
                                        in_=cc2_out[1:2, :]),
          reads=["cc2_out"], writes=["rowG"], dma=True)
        E("vector", lambda h: h.tensor_max(
            mrow[0:1, 134:135], rowG0[0:1, C:C + 1], rowG1[0:1, C:C + 1]),
          reads=["rowG"], writes=["mrow"])
        for i, rgi in ((0, 0), (1, 1)):
            rg = (rowG0, rowG1)[rgi]
            E("vector", lambda h, rg=rg, i=i: h.tensor_sub(
                mrow[0:1, 135 + i:136 + i], rg[0:1, C:C + 1],
                mrow[0:1, 134:135]),
              reads=["rowG", "mrow"], writes=["mrow"])
            E("scalar", lambda h, i=i: h.activation(
                mrow[0:1, 135 + i:136 + i], mrow[0:1, 135 + i:136 + i],
                AF.Exp),
              reads=["mrow"], writes=["mrow"])
        E("vector", lambda h: h.tensor_mul(
            mrow[0:1, 137:138], mrow[0:1, 135:136], rowG0[0:1, C + 1:C + 2]),
          reads=["mrow", "rowG"], writes=["mrow"])
        E("vector", lambda h: h.scalar_tensor_tensor(
            mrow[0:1, 137:138], rowG1[0:1, C + 1:C + 2], mrow[0:1, 136:137],
            mrow[0:1, 137:138], op0=ALU.mult, op1=ALU.add),
          reads=["rowG", "mrow"], writes=["mrow"])
        E("vector", lambda h: h.reciprocal(mrow[0:1, 138:139],
                                           mrow[0:1, 137:138]),
          reads=["mrow"], writes=["mrow"])
        E("vector", lambda h: h.tensor_scalar_mul(
            rowA[0:1, 0:C], rowG0[0:1, 0:C], mrow[0:1, 135:136]),
          reads=["rowG", "mrow"], writes=["rowA"])
        E("vector", lambda h: h.scalar_tensor_tensor(
            rowA[0:1, 0:C], rowG1[0:1, 0:C], mrow[0:1, 136:137],
            rowA[0:1, 0:C], op0=ALU.mult, op1=ALU.add),
          reads=["rowG", "mrow"], writes=["rowA"])
        E("vector", lambda h: h.tensor_copy(rowB[:], rowA[0:1, 0:C]),
          reads=["rowA"], writes=["rowB"])
        bcast_row_to(bb, "bb")
        for cc in range(8):
            me = "gpsimd" if cc % 2 == 0 else "vector"
            E(me, lambda h, cc=cc: h.tensor_mul(
                scr[cc % 2][:], aw[cc][:], bb[:]),
              reads=[("aw", cc), "bb"], writes=[("scr", cc % 2)])
            E("vector", lambda h, cc=cc: h.reduce_sum(
                avc[:, cc:cc + 1], scr[cc % 2][:], axis=AX.X),
              reads=[("scr", cc % 2)], writes=["avc"])
        cell_to_col(138, 4)
        E("vector", lambda h: h.tensor_scalar_mul(avc[:], avc[:],
                                                  tiny[:, 4:5]),
          reads=["avc", ("tiny", 0)], writes=["avc"])
        E("vector", lambda h: h.reduce_sum(tiny[:, 0:1], avc[:], axis=AX.X),
          reads=["avc"], writes=[("tiny", 0)])
        col_to_cell(tiny[:, 0:1], ("tiny", 0), 139, "sum")
        E("vector", lambda h: h.tensor_scalar_mul(
            mrow[0:1, 139:140], mrow[0:1, 139:140], 1.0 / C),
          reads=["mrow"], writes=["mrow"])
        E("scalar", lambda h: h.square(scr2[:, 0:8], avc[:]),
          reads=["avc"], writes=["scr2"])
        E("vector", lambda h: h.reduce_sum(tiny[:, 2:3], scr2[:, 0:8],
                                           axis=AX.X),
          reads=["scr2"], writes=[("tiny", 0)])
        col_to_cell(tiny[:, 2:3], ("tiny", 0), 140, "sum")
        E("vector", lambda h: h.tensor_scalar_mul(
            mrow[0:1, 140:141], mrow[0:1, 140:141], 1.0 / C),
          reads=["mrow"], writes=["mrow"])
        E("vector", lambda h: h.tensor_mul(
            mrow[0:1, 141:142], mrow[0:1, 139:140], mrow[0:1, 139:140]),
          reads=["mrow"], writes=["mrow"])
        E("vector", lambda h: h.tensor_sub(
            mrow[0:1, 140:141], mrow[0:1, 140:141], mrow[0:1, 141:142]),
          reads=["mrow"], writes=["mrow"])
        E("scalar", lambda h: h.activation(
            mrow[0:1, 140:141], mrow[0:1, 140:141], AF.Sqrt,
            bias=epsc[0:1, 0:1]),
          reads=["mrow"], writes=["mrow"])
        E("vector", lambda h: h.reciprocal(mrow[0:1, 140:141],
                                           mrow[0:1, 140:141]),
          reads=["mrow"], writes=["mrow"])
        cell_to_col(139, 5)
        cell_to_col(140, 6)
        E("vector", lambda h: h.tensor_scalar(
            vec8[:], avc[:], tiny[:, 5:6], tiny[:, 6:7],
            op0=ALU.subtract, op1=ALU.mult),
          reads=["avc", ("tiny", 0)], writes=["vec8"])

        for cc in range(8):
            E("tensor", lambda h, cc=cc: h.transpose(
                psSm[0:1, 0:128], vec8[:, cc:cc + 1], idf[:]),
              reads=["vec8"], writes=["psSm"])
            E("scalar", lambda h: h.copy(mrow[0:1, 0:128],
                                         psSm[0:1, 0:128]),
              reads=["psSm"], writes=["mrow"])
            E("tensor", lambda h, cc=cc: h.matmul(
                psB2[:, 128 * (cc % 4):128 * (cc % 4 + 1)], onesf[:],
                mrow[0:1, 0:128], start=True, stop=True,
                skip_group_check=True),
              reads=["mrow", "onesf"], writes=["psB2"])
            E("scalar", lambda h, cc=cc: h.copy(
                vecb[:, 128 * cc:128 * (cc + 1)],
                psB2[:, 128 * (cc % 4):128 * (cc % 4 + 1)]),
              reads=["psB2"], writes=["vecb"])

        def pass3_tile(i):
            sl = i % 2
            tb = 16 * (i % 2)
            E("sync", lambda h: h.dma_start(
                out=xt[sl][:], in_=x_in[128 * i:128 * (i + 1), :]),
              writes=[("xt", sl)], dma=True)
            E("gpsimd", lambda h: h.tensor_add(scr[sl][:], xt[sl][:],
                                               vecb[:]),
              reads=[("xt", sl), "vecb"], writes=[("scr", sl)])
            E("vector", lambda h, tb=tb: h.bn_stats(
                tiny[:, tb:tb + 6], scr[sl][:, 0:512]),
              reads=[("scr", sl)], writes=[("tiny", tb)])
            E("vector", lambda h, tb=tb: h.bn_stats(
                tiny[:, tb + 6:tb + 12], scr[sl][:, 512:C]),
              reads=[("scr", sl)], writes=[("tiny", tb)])
            E("vector", lambda h, tb=tb: h.bn_aggr(
                tiny[:, tb + 12:tb + 14], tiny[:, tb:tb + 12]),
              reads=[("tiny", tb)], writes=[("tiny", tb)])
            E("scalar", lambda h, tb=tb: h.activation(
                tiny[:, tb + 13:tb + 14], tiny[:, tb + 13:tb + 14],
                AF.Sqrt, bias=epsc[:, 0:1]),
              reads=[("tiny", tb)], writes=[("tiny", tb)])
            E("vector", lambda h, tb=tb: h.reciprocal(
                tiny[:, tb + 13:tb + 14], tiny[:, tb + 13:tb + 14]),
              reads=[("tiny", tb)], writes=[("tiny", tb)])
            E("vector", lambda h, tb=tb: h.tensor_scalar(
                scr[sl][:], scr[sl][:], tiny[:, tb + 12:tb + 13],
                tiny[:, tb + 13:tb + 14], op0=ALU.subtract, op1=ALU.mult),
              reads=[("scr", sl), ("tiny", tb)], writes=[("scr", sl)])
            for cc in range(8):
                sl2 = cc % 2
                E("tensor", lambda h, cc=cc, sl2=sl2: h.transpose(
                    psT[sl2][:, 0:128], scr[sl][:, 128 * cc:128 * (cc + 1)],
                    idf[:]),
                  reads=[("scr", sl)], writes=[("psT", sl2)])
                ee = "vector" if cc >= 6 else "scalar"
                E(ee, lambda h, cc=cc, sl2=sl2: h.copy(
                    xs[cc][:, 128 * i:128 * (i + 1)], psT[sl2][:, 0:128])
                  if cc < 6 else h.tensor_copy(
                      xs[cc][:, 128 * i:128 * (i + 1)], psT[sl2][:, 0:128]),
                  reads=[("psT", sl2)], writes=[("x", cc)])

        for i in range(NXT):
            pass3_tile(i)
        for cc in range(8):
            E("vector", lambda h, cc=cc: h.tensor_scalar_mul(
                xs[cc][:, 0:QOFF], xs[cc][:, 0:QOFF], gate[:, 0:1]),
              reads=[("x", cc), "gate"], writes=[("x", cc)])
        if "y2" in dbg:
            d = dbg_t("y2", [8 * 128, XW])
            for cc in range(8):
                E("sync", lambda h, cc=cc, d=d: h.dma_start(
                    out=d[128 * cc:128 * (cc + 1), :], in_=xs[cc][:]),
                  reads=[("x", cc)], dma=True)
        p.barrier()


def build_layer(nc, p, es, l, xs, masks, bigrow, gate, idb, idf, onesb,
                onesf, onecolb, onef1, epsc, onesbb, bcols, qsb, ksb,
                tabs_in, wqkv, qsc, ksc, wo, lng, lnb,
                w1, b1, w2, b2, cch_in, cch_out, dbg, dbg_t):
    E = p.emit
    half_chunks = [list(range(0, 11)), list(range(11, NHC))]

    with contextlib.ExitStack() as ph:
        Sa = lambda n, *a: ph.enter_context(nc.sbuf_tensor(_u(n), *a))
        PSa = lambda n, *a: ph.enter_context(nc.psum_tensor(_u(n), *a))
        brow = [Sa(f"brow{l}_{s}", [1, 1024 + 128], F32) for s in range(2)]
        psCol = [PSa(f"psCol{l}_{s}", [128, 128], F32) for s in range(2)]
        pieces = []
        for pi in range(3):
            off0 = 1024 * pi
            n = min(1024, HID - off0)
            cols = []
            c0 = off0 // 128
            for c in range(c0, min(NHC, (off0 + n + 127) // 128)):
                w = min(128, HID - 128 * c)
                cols.append((c, 128 * c - off0, w))
            pieces.append((b1[l, off0:off0 + n].rearrange(
                "(a n) -> a n", a=1), n, cols))
        for pi in range(3):
            off0 = 1024 * pi
            n = min(1024, HID - off0)
            cols = []
            c0 = off0 // 128
            for c in range(c0, min(NHC, (off0 + n + 127) // 128)):
                w = min(128, HID - 128 * c)
                cols.append((22 + c, 128 * c - off0, w))
            pieces.append((b1[l, HID + off0:HID + off0 + n].rearrange(
                "(a n) -> a n", a=1), n, cols))
        for bix, srct in ((44, b2), (52, lng), (60, lnb)):
            cols = [(bix + c, 128 * c, 128) for c in range(8)]
            pieces.append((srct[l, :].rearrange("(a n) -> a n", a=1), C,
                           cols))
        for pidx, (srcap, n, cols) in enumerate(pieces):
            sp = pidx % 2
            E("sync", lambda h, srcap=srcap, n=n, sp=sp: h.dma_start(
                out=brow[sp][0:1, 0:n], in_=srcap),
              writes=[("brow", sp)], dma=True)

            def colmm(h, cols=cols, sp=sp):
                last = None
                for k, (col, off, w) in enumerate(cols):
                    last = h.matmul(psCol[sp][0:w, k:k + 1],
                                    brow[sp][0:1, off:off + w], onef1[:],
                                    start=True, stop=True,
                                    skip_group_check=True)
                return last
            E("tensor", colmm, reads=[("brow", sp), "onef1"],
              writes=[("psCol", sp)])
            for k, (col, off, w) in enumerate(cols):
                E("scalar", lambda h, col=col, k=k, w=w, sp=sp: h.copy(
                    bcols[0:w, col:col + 1], psCol[sp][0:w, k:k + 1]),
                  reads=[("psCol", sp)], writes=["bcols"])
        for si, (dst, srct, key) in enumerate(
                ((qsb, qsc, "qsb"), (ksb, ksc, "ksb"))):
            sp = si % 2
            E("sync", lambda h, srct=srct, sp=sp: h.dma_start(
                out=brow[sp][0:1, 1024:1024 + DH],
                in_=srct[l, :].rearrange("(a n) -> a n", a=1)),
              writes=[("brow", sp)], dma=True)
            E("tensor", lambda h, sp=sp: h.matmul(
                psCol[sp][:, 0:64], onesf[:], brow[sp][0:1, 1024:1024 + DH],
                start=True, stop=True, skip_group_check=True),
              reads=[("brow", sp), "onesf"], writes=[("psCol", sp)])
            E("scalar", lambda h, dst=dst, sp=sp: h.copy(
                dst[:], psCol[sp][:, 0:64]),
              reads=[("psCol", sp)], writes=[key])
        p.barrier()

    BC = lambda c: ("bcols",)
    b1a_col = lambda c, w=128: bcols[0:w, c:c + 1]
    b1g_col = lambda c, w=128: bcols[0:w, 22 + c:22 + c + 1]
    b2_col = lambda c: bcols[:, 44 + c:44 + c + 1]
    lng_col = lambda c: bcols[:, 52 + c:52 + c + 1]
    lnb_col = lambda c: bcols[:, 60 + c:60 + c + 1]

    def load_wqkv(g):
        for r, cbase in enumerate((256 * g, INNER + 256 * g,
                                   2 * INNER + 256 * g)):
            E("gpsimd", lambda h, r=r, cbase=cbase: h.dma_start(
                out=wqkv_b[:].rearrange("p (ic n) -> p ic n", ic=8)
                [:, :, 256 * r:256 * (r + 1)],
                in_=wqkv[l, :, cbase:cbase + 256].rearrange(
                    "(ic p) n -> p ic n", p=128)),
              writes=["wqkv_b"], dma=True)

    load_wqkv(0)
    E("gpsimd", lambda h: h.dma_start(
        out=wo_b[:].rearrange("p (ic n) -> p ic n", ic=4),
        in_=wo[l, :, :].rearrange("(ic p) n -> p ic n", p=128)),
      writes=["wo_b"], dma=True)

    oscope = contextlib.ExitStack()
    ofm = [oscope.enter_context(nc.sbuf_tensor(_u(f"ofm{l}_{pp}"),
                                               [128, OWN], BF16))
           for pp in range(4)]
    wqkv_b = oscope.enter_context(
        nc.sbuf_tensor(_u("wqkv_b"), [128, 8 * 768], BF16))
    wo_b = oscope.enter_context(
        nc.sbuf_tensor(_u("wo_b"), [128, 4 * C], BF16))
    with contextlib.ExitStack() as lph:
        Sl = lambda n, *a: lph.enter_context(nc.sbuf_tensor(_u(n), *a))
        tabs = [Sl(f"tab{l}_{i}", [128, NKC * DH], F32) for i in range(4)]
        for i in range(4):
            E("sync", lambda h, d=tabs[i], s=tabs_in[i]:
              h.dma_start(out=d[:], in_=s), writes=[("tab", i)], dma=True)
        for ti, scale_t, skey in ((0, qsb, "qsb"), (2, ksb, "ksb")):
            E("vector", lambda h, ti=ti, scale_t=scale_t: h.tensor_mul(
                tabs[ti][:].rearrange("p (j d) -> p j d", d=DH),
                tabs[ti][:].rearrange("p (j d) -> p j d", d=DH),
                scale_t[:].rearrange("p (o d) -> p o d", o=1)
                .broadcast_to([128, NKC, DH])),
              reads=[("tab", ti), skey], writes=[("tab", ti)])
            E("vector", lambda h, ti=ti, scale_t=scale_t: h.tensor_mul(
                tabs[ti + 1][:].rearrange("p (j d) -> p j d", d=DH)
                [:, :, 0:32],
                tabs[ti + 1][:].rearrange("p (j d) -> p j d", d=DH)
                [:, :, 0:32],
                scale_t[:, 32:64].rearrange("p (o d) -> p o d", o=1)
                .broadcast_to([128, NKC, 32])),
              reads=[("tab", ti + 1), skey], writes=[("tab", ti + 1)])
            E("vector", lambda h, ti=ti, scale_t=scale_t: h.tensor_mul(
                tabs[ti + 1][:].rearrange("p (j d) -> p j d", d=DH)
                [:, :, 32:64],
                tabs[ti + 1][:].rearrange("p (j d) -> p j d", d=DH)
                [:, :, 32:64],
                scale_t[:, 0:32].rearrange("p (o d) -> p o d", o=1)
                .broadcast_to([128, NKC, 32])),
              reads=[("tab", ti + 1), skey], writes=[("tab", ti + 1)])

        for g in range(2):
            qkv_and_attention(nc, p, lph, l, g, xs, tabs, masks, bigrow,
                              idb, idf, onesb, onesf, qsb, ksb, wqkv_b,
                              ofm, load_wqkv, dbg, dbg_t)

    with contextlib.ExitStack() as ph:
        PSa = lambda n, *a: ph.enter_context(nc.psum_tensor(_u(n), *a))
        psWO = [PSa(f"psWO{s}", [128, 512], F32) for s in range(2)]
        for tt in range(4):
            for oc in range(8):
                sl = (tt * 8 + oc) % 2

                def wo_fn(h, tt=tt, oc=oc, sl=sl):
                    last = None
                    for ic in range(4):
                        last = h.matmul(
                            psWO[sl][:, :],
                            wo_b[:, 1024 * ic + 128 * oc:
                                 1024 * ic + 128 * (oc + 1)],
                            ofm[ic][:, 512 * tt:512 * (tt + 1)],
                            start=(ic == 0), stop=True,
                            skip_group_check=True)
                    return last
                E("tensor", wo_fn,
                  reads=["wo_b"] + [("ofm", ic) for ic in range(4)],
                  writes=[("psWO", sl)])
                E("vector", lambda h, tt=tt, oc=oc, sl=sl: h.tensor_add(
                    xs[oc][:, QOFF + 512 * tt:QOFF + 512 * (tt + 1)],
                    xs[oc][:, QOFF + 512 * tt:QOFF + 512 * (tt + 1)],
                    psWO[sl][:, :]),
                  reads=[("psWO", sl), ("x", oc)], writes=[("x", oc)])
        p.barrier()
    oscope.close()

    with contextlib.ExitStack() as ph:
        Sa = lambda n, *a: ph.enter_context(nc.sbuf_tensor(_u(n), *a))
        PSa = lambda n, *a: ph.enter_context(nc.psum_tensor(_u(n), *a))
        lnt = [Sa(f"lnt{c}", [128, OWN], BF16) for c in range(8)]
        hg = [Sa(f"hg{k}", [128, OWN], BF16) for k in range(8)]
        w2h = [Sa(f"w2h{k}", [128, C], BF16) for k in range(6)]
        wa_b = [Sa(f"wa{s}", [128, 8 * 128], BF16) for s in range(4)]
        wg_b = [Sa(f"wg{s}", [128, 8 * 128], BF16) for s in range(4)]
        xbf = [Sa(f"xbf{s}", [128, 1024], BF16) for s in range(2)]
        sqf = [Sa(f"sqf{s}", [128, 1024], BF16) for s in range(2)]
        gel = [Sa("gel0", [128, 512], BF16)] * 2
        scrF = Sa("scrF", [128, 1024], F32)
        scrM = Sa("scrM", [128, 1024], F32)
        scrF2 = Sa("scrF2", [128, 1024], F32)
        scrR2 = Sa("scrR2", [128, 1024], F32)
        psBig = [PSa(f"psBig{s}", [128, 1024], F32) for s in range(2)]
        psW1 = [PSa(f"psW1{s}", [128, 512], F32) for s in range(2)]
        psW2 = [PSa(f"psW2{s}", [128, 512], F32) for s in range(2)]

        for tgh in range(2):
            base = QOFF + 1024 * tgh
            for c in range(8):
                sl = c % 2
                sh = 0 if c < 4 else -1
                rds = [("x", c), ("xh", c)] if (sh and tgh == 0) \
                    else [("x", c)]
                if c % 2 == 0:
                    E("vector", lambda h, c=c, sl=sl, sh=sh, base=base:
                      h.tensor_copy(
                          xbf[sl][:], xs[c][:, base + sh:base + sh + 1024]),
                      reads=rds, writes=[("xbf", sl)])
                else:
                    E("scalar", lambda h, c=c, sl=sl, sh=sh, base=base:
                      h.copy(
                          xbf[sl][:], xs[c][:, base + sh:base + sh + 1024]),
                      reads=rds, writes=[("xbf", sl)])
                E("scalar", lambda h, sl=sl: h.square(sqf[sl][:],
                                                      xbf[sl][:]),
                  reads=[("xbf", sl)], writes=[("sqf", sl)])

                def stat_fn(h, c=c, sl=sl, tgh=tgh):
                    last = None
                    for hf in range(2):
                        if tgh == 0:
                            ps_s = psBig[0][:, 512 * hf:512 * (hf + 1)]
                            ps_q = psBig[1][:, 512 * hf:512 * (hf + 1)]
                        else:
                            ps_s = psW1[hf][:, :]
                            ps_q = psW2[hf][:, :]
                        h.matmul(ps_s, onesbb[:],
                                 xbf[sl][:, 512 * hf:512 * (hf + 1)],
                                 start=(c == 0), stop=True,
                                 skip_group_check=True)
                        last = h.matmul(
                            ps_q, onesbb[:],
                            sqf[sl][:, 512 * hf:512 * (hf + 1)],
                            start=(c == 0), stop=True, skip_group_check=True)
                    return last
                wkeys = [("psBig", 0), ("psBig", 1)] if tgh == 0 else \
                    [("psW1", 0), ("psW1", 1), ("psW2", 0), ("psW2", 1)]
                E("tensor", stat_fn,
                  reads=[("xbf", sl), ("sqf", sl), "onesbb"], writes=wkeys)

        E("vector", lambda h: h.tensor_scalar_mul(
            scrF[:], psBig[0][:, :], 1.0 / C),
          reads=[("psBig", 0)], writes=["scrF"])
        E("vector", lambda h: h.tensor_mul(scrM[:], scrF[:], scrF[:]),
          reads=["scrF"], writes=["scrM"])
        E("vector", lambda h: h.scalar_tensor_tensor(
            psBig[1][:, :], psBig[1][:, :], 1.0 / C, scrM[:],
            op0=ALU.mult, op1=ALU.subtract),
          reads=[("psBig", 1), "scrM"], writes=[("psBig", 1)])
        E("scalar", lambda h: h.activation(
            psBig[1][:, :], psBig[1][:, :], AF.Sqrt, bias=epsc[:, 0:1]),
          reads=[("psBig", 1)], writes=[("psBig", 1)])
        E("vector", lambda h: h.reciprocal(psBig[1][:, :], psBig[1][:, :]),
          reads=[("psBig", 1)], writes=[("psBig", 1)])
        for hf in range(2):
            E("vector", lambda h, hf=hf: h.tensor_scalar_mul(
                scrF2[:, 512 * hf:512 * (hf + 1)], psW1[hf][:, :], 1.0 / C),
              reads=[("psW1", hf)], writes=["scrF2"])
        E("vector", lambda h: h.tensor_mul(scrM[:], scrF2[:], scrF2[:]),
          reads=["scrF2"], writes=["scrM"])
        for hf in range(2):
            E("vector", lambda h, hf=hf: h.scalar_tensor_tensor(
                psW2[hf][:, :], psW2[hf][:, :], 1.0 / C,
                scrM[:, 512 * hf:512 * (hf + 1)],
                op0=ALU.mult, op1=ALU.subtract),
              reads=[("psW2", hf), "scrM"], writes=[("psW2", hf)])
            E("scalar", lambda h, hf=hf: h.activation(
                psW2[hf][:, :], psW2[hf][:, :], AF.Sqrt, bias=epsc[:, 0:1]),
              reads=[("psW2", hf)], writes=[("psW2", hf)])
            E("vector", lambda h, hf=hf: h.reciprocal(
                scrR2[:, 512 * hf:512 * (hf + 1)], psW2[hf][:, :]),
              reads=[("psW2", hf)], writes=["scrR2"])

        for tgh in range(2):
            base = QOFF + 1024 * tgh
            mu, rs = (scrF, psBig[1]) if tgh == 0 else (scrF2, scrR2)
            mukey, rskey = ("scrF", ("psBig", 1)) if tgh == 0 else \
                ("scrF2", "scrR2")
            for c in range(8):
                sh = 0 if c < 4 else -1
                rds2 = [("x", c), ("xh", c), mukey] if (sh and tgh == 0) \
                    else [("x", c), mukey]
                E("vector", lambda h, c=c, sh=sh, base=base, mu=mu:
                  h.tensor_sub(
                      scrM[:], xs[c][:, base + sh:base + sh + 1024], mu[:]),
                  reads=rds2, writes=["scrM"])
                E("vector", lambda h, c=c, rs=rs: h.tensor_mul(
                    scrM[:], scrM[:], rs[:, :]),
                  reads=["scrM", rskey], writes=["scrM"])
                E("vector", lambda h, c=c, tgh=tgh: h.tensor_scalar(
                    lnt[c][:, 1024 * tgh:1024 * (tgh + 1)], scrM[:],
                    lng_col(c), lnb_col(c), op0=ALU.mult, op1=ALU.add),
                  reads=["scrM", "bcols"], writes=[("lnt", c, tgh)])

        if f"lnt{l}" in dbg:
            d = dbg_t(f"lnt{l}", [8 * 128, OWN])
            for c in range(8):
                E("gpsimd", lambda h, c=c, d=d: h.dma_start(
                    out=d[128 * c:128 * (c + 1), :], in_=lnt[c][:]),
                  reads=[("lnt", c, 0), ("lnt", c, 1)], dma=True)

        chunks = list(range(NHC))

        def load_chunk(k, hc):
            w = min(128, HID - 128 * hc)
            wsl = k % 4
            E("gpsimd", lambda h, hc=hc, w=w, wsl=wsl: h.dma_start(
                out=wa_b[wsl][:, 0:8 * w].rearrange(
                    "p (ic n) -> p ic n", ic=8),
                in_=w1[l, :, 128 * hc:128 * hc + w].rearrange(
                    "(ic p) n -> p ic n", p=128)),
              writes=[("wa_b", wsl)], dma=True)
            E("gpsimd", lambda h, hc=hc, w=w, wsl=wsl: h.dma_start(
                out=wg_b[wsl][:, 0:8 * w].rearrange(
                    "p (ic n) -> p ic n", ic=8),
                in_=w1[l, :, HID + 128 * hc:HID + 128 * hc + w]
                .rearrange("(ic p) n -> p ic n", p=128)),
              writes=[("wg_b", wsl)], dma=True)


        def emit_w2_quarter(qi, hcs, is_last):
            for k2, hc2 in enumerate(hcs):
                w2w = min(128, HID - 128 * hc2)
                E("gpsimd", lambda h, hc2=hc2, w2w=w2w, k2=k2: h.dma_start(
                    out=w2h[k2][0:w2w, :],
                    in_=w2[l, 128 * hc2:128 * hc2 + w2w, :]),
                  writes=[("w2h", k2)], dma=True)
            for sub in (3, 2, 1, 0):
                for oc in range(8):
                    sl2 = oc % 2

                    def w2_fn(h, sub=sub, oc=oc, sl2=sl2, hcs=hcs):
                        last = None
                        for k2, hc2 in enumerate(hcs):
                            w2w = min(128, HID - 128 * hc2)
                            last = h.matmul(
                                psW2[sl2][:, :],
                                w2h[k2][0:w2w, 128 * oc:128 * (oc + 1)],
                                hg[hc2 % 8][0:w2w,
                                            512 * sub:512 * (sub + 1)],
                                start=(k2 == 0), stop=True,
                                skip_group_check=True)
                        return last
                    E("tensor", w2_fn,
                      reads=[("w2h", k2) for k2 in range(len(hcs))] +
                            [("hg", hc2 % 8) for hc2 in hcs],
                      writes=[("psW2", sl2)])
                    if not is_last:
                        E("vector", lambda h, oc=oc, sub=sub, sl2=sl2:
                          h.tensor_add(
                              xs[oc][:, QOFF + 512 * sub:
                                     QOFF + 512 * (sub + 1)],
                              xs[oc][:, QOFF + 512 * sub:
                                     QOFF + 512 * (sub + 1)],
                              psW2[sl2][:, :]),
                          reads=[("psW2", sl2), ("x", oc)],
                          writes=[("x", oc)])
                    else:
                        E("vector", lambda h, oc=oc, sub=sub, sl2=sl2:
                          h.scalar_tensor_tensor(
                              xs[oc][:, QOFF + 512 * sub:
                                     QOFF + 512 * (sub + 1)],
                              psW2[sl2][:, :], b2_col(oc),
                              xs[oc][:, QOFF + 512 * sub:
                                     QOFF + 512 * (sub + 1)],
                              op0=ALU.add, op1=ALU.add),
                          reads=[("psW2", sl2), ("x", oc), "bcols"],
                          writes=[("x", oc)])
                if is_last and sub == 2 and l == 0:
                    for c in range(8):
                        E("scalar", lambda h, c=c: h.dma_start(
                            out=cch_in[128 * c:128 * (c + 1), :],
                            in_=xs[c][:, XW - HALO:XW]),
                          reads=[("x", c)], writes=["cch_in"], dma=True)
                    E("gpsimd", lambda h: h.collective_compute(
                        "AllGather", ALU.bypass, replica_groups=RG_PAIRS,
                        ins=[cch_in.ap().opt()], outs=[cch_out.ap().opt()]),
                      reads=["cch_in"], writes=["cch_out"])

        load_chunk(0, 0)
        load_chunk(1, 1)
        for k, hc in enumerate(chunks):
            w = min(128, HID - 128 * hc)
            wsl = k % 4
            kk = k % 8
            if k + 2 < NHC:
                load_chunk(k + 2, hc + 2)
            for sub in range(4):
                par = (4 * k + sub) % 2
                if par == 0:
                    psa, psakey = psW1[0][0:w, :], ("psW1", 0)
                    psg, psgkey = psW1[1][0:w, :], ("psW1", 1)
                else:
                    psa, psakey = psBig[0][0:w, 0:512], ("psBig", 0)
                    psg, psgkey = psBig[0][0:w, 512:1024], ("psBig", 0)

                def w1a_fn(h, w=w, wsl=wsl, sub=sub, psa=psa):
                    last = None
                    for c in range(8):
                        last = h.matmul(
                            psa, wa_b[wsl][:, w * c:w * (c + 1)],
                            lnt[c][:, 512 * sub:512 * (sub + 1)],
                            start=(c == 0), stop=True,
                            skip_group_check=True)
                    return last

                def w1g_fn(h, w=w, wsl=wsl, sub=sub, psg=psg):
                    last = None
                    for c in range(8):
                        last = h.matmul(
                            psg, wg_b[wsl][:, w * c:w * (c + 1)],
                            lnt[c][:, 512 * sub:512 * (sub + 1)],
                            start=(c == 0), stop=True,
                            skip_group_check=True)
                    return last
                E("tensor", w1a_fn,
                  reads=[("wa_b", wsl)] +
                        [("lnt", c, sub // 2) for c in range(8)],
                  writes=[psakey])
                E("tensor", w1g_fn,
                  reads=[("wg_b", wsl)] +
                        [("lnt", c, sub // 2) for c in range(8)],
                  writes=[psgkey])
                E("scalar", lambda h, w=w, hc=hc, sub=sub, psg=psg:
                  h.activation(gel[sub % 2][0:w, :], psg, AF.Gelu,
                               bias=b1g_col(hc, w)),
                  reads=[psgkey, "bcols"], writes=["gel"])
                E("vector", lambda h, w=w, hc=hc, kk=kk, sub=sub, psa=psa:
                  h.scalar_tensor_tensor(
                      hg[kk][0:w, 512 * sub:512 * (sub + 1)],
                      psa, b1a_col(hc, w), gel[sub % 2][0:w, :],
                      op0=ALU.add, op1=ALU.mult),
                  reads=[psakey, "gel", "bcols"],
                  writes=[("hg", kk)])

            if k == 5:
                emit_w2_quarter(0, chunks[0:6], False)
            elif k == 10:
                emit_w2_quarter(1, chunks[6:11], False)
            elif k == 16:
                emit_w2_quarter(2, chunks[11:17], False)
        emit_w2_quarter(3, chunks[17:NHC], True)
        p.barrier()

    if l == 0:
        for c in range(8):
            E("sync", lambda h, c=c: h.dma_start(
                out=xs[c][:, DEAD:DEAD + HALO],
                in_=cch_out[128 * c:128 * (c + 1), :]),
              reads=["cch_out"], writes=[("xh", c)], dma=True)
            E("vector", lambda h, c=c: h.tensor_scalar_mul(
                xs[c][:, 0:QOFF], xs[c][:, 0:QOFF], gate[:, 0:1]),
              reads=[("xh", c), "gate"], writes=[("xh", c)])
    if f"xl{l}" in dbg:
        d = dbg_t(f"xl{l}", [8 * 128, XW])
        for cc in range(8):
            E("sync", lambda h, cc=cc, d=d: h.dma_start(
                out=d[128 * cc:128 * (cc + 1), :], in_=xs[cc][:]),
              reads=[("x", cc)], dma=True)
    p.barrier()


def qkv_and_attention(nc, p, lph, l, g, xs, tabs, masks, bigrow, idb, idf,
                      onesb, onesf, qsb, ksb, wqkv_b, ofm, load_wqkv,
                      dbg, dbg_t):
    E = p.emit
    with contextlib.ExitStack() as ph:
        Sa = lambda n, *a: ph.enter_context(nc.sbuf_tensor(_u(n), *a))
        PSa = lambda n, *a: ph.enter_context(nc.psum_tensor(_u(n), *a))
        xb = [Sa(f"xb{c}", [128, 132], BF16) for c in range(16)]
        qh = [Sa(f"qh{pp}", [128, OWN], BF16) for pp in range(2)]
        kh = [Sa(f"kh{pp}", [128, KW], BF16) for pp in range(2)]
        vx = [Sa(f"vx{j}", [128, 4 * 65], BF16) for j in range(NKC)]
        qkraw = [Sa(f"qkraw{s}", [128, 512], F32) for s in range(2)]
        sqt = [Sa(f"sqt{s}", [128, 512], F32) for s in range(2)]
        t2s = [Sa(f"t2s{s}", [128, 512], F32) for s in range(2)]
        q0s = [Sa(f"q0s{s}", [128, 512], F32) for s in range(2)]
        qkt = [Sa(f"qkt{s}", [128, 512], BF16) for s in range(3)]
        nrm = [Sa(f"nrm{s}", [128, 8], F32) for s in range(2)]
        qps = contextlib.ExitStack()
        PSq = lambda n, *a: qps.enter_context(nc.psum_tensor(_u(n), *a))
        psQKV = [PSq(f"psQKV{s}", [128, 768], F32) for s in range(2)]
        psTrQ = PSq("psTrQ", [128, 128], BF16)
        psTrK = PSq("psTrK", [128, 128], BF16)

        def emit_casts(j):
            xcol = KOFF + 128 * j
            sl8 = 8 * (j % 2)
            halo = xcol - 1 < QOFF
            for c in range(8):
                rds = [("x", c), ("xh", c)] if halo else [("x", c)]
                if c % 2 == 0:
                    E("vector", lambda h, c=c, xcol=xcol, sl8=sl8:
                      h.tensor_copy(xb[sl8 + c][:, 0:129],
                                    xs[c][:, xcol - 1:xcol + 128]),
                      reads=rds, writes=[("xb", sl8 + c)])
                else:
                    E("scalar", lambda h, c=c, xcol=xcol, sl8=sl8:
                      h.copy(xb[sl8 + c][:, 0:129],
                             xs[c][:, xcol - 1:xcol + 128]),
                      reads=rds, writes=[("xb", sl8 + c)])

        def emit_mm(j):
            own = j >= 4
            ps = psQKV[j % 2]
            pskey = ("psQKV", j % 2)
            sl8 = 8 * (j % 2)
            r0 = 0 if own else 1

            def qkv_fn(h, sl8=sl8):
                last = None
                for c in range(8):
                    lhs = xb[sl8 + c][:, 1:129] if c < 4 \
                        else xb[sl8 + c][:, 0:128]
                    for r in range(r0, 3):
                        last = h.matmul(
                            ps[:, 256 * r:256 * (r + 1)], lhs,
                            wqkv_b[:, 768 * c + 256 * r:768 * c + 256 * (r + 1)],
                            start=(c == 0 and (r == r0 or r == 2)),
                            stop=True, skip_group_check=True)
                return last
            E("tensor", qkv_fn,
              reads=[("xb", sl8 + c) for c in range(8)] + ["wqkv_b"],
              writes=[pskey])

        def emit_chunk(j):
            own = j >= 4
            xcol = KOFF + 128 * j
            ps = psQKV[j % 2]
            pskey = ("psQKV", j % 2)
            r0 = 0 if own else 1
            E("scalar", lambda h: h.copy(
                vx[j][:].rearrange("p (hh d) -> p hh d", hh=4)[:, :, 0:64],
                ps[:, 512:768].rearrange("p (hh d) -> p hh d", hh=4)),
              reads=[pskey], writes=[("vx", j)])
            E("gpsimd", lambda h: h.memset(
                vx[j][:].rearrange("p (hh d) -> p hh d", d=65)[:, :, 64:65],
                1.0),
              reads=[("vx", j)], writes=[("vx", j)])
            a = 0 if own else 256
            nh = (512 - a) // 64
            sq = j % 2
            E("scalar", lambda h, a=a, sq=sq: h.square(
                sqt[sq][:, a:512], ps[:, a:512]),
              reads=[pskey], writes=[("sqt", sq)])
            E("scalar", lambda h, a=a, sq=sq: h.copy(qkraw[sq][:, a:512],
                                                     ps[:, a:512]),
              reads=[pskey], writes=[("qkraw", sq)])
            E("vector", lambda h, a=a, nh=nh, sq=sq: h.reduce_sum(
                nrm[sq][:, a // 64:8],
                sqt[sq][:, a:512].rearrange("p (hh d) -> p hh d", d=64),
                axis=AX.X),
              reads=[("sqt", sq)], writes=[("nrm", sq)])
            E("scalar", lambda h, a=a, sq=sq: h.activation(
                nrm[sq][:, a // 64:8], nrm[sq][:, a // 64:8], AF.Sqrt),
              reads=[("nrm", sq)], writes=[("nrm", sq)])
            E("vector", lambda h, a=a, sq=sq: h.tensor_scalar_max(
                nrm[sq][:, a // 64:8], nrm[sq][:, a // 64:8], 1e-12),
              reads=[("nrm", sq)], writes=[("nrm", sq)])
            E("vector", lambda h, a=a, sq=sq: h.reciprocal(
                nrm[sq][:, a // 64:8], nrm[sq][:, a // 64:8]),
              reads=[("nrm", sq)], writes=[("nrm", sq)])
            E("vector", lambda h, a=a, nh=nh, sq=sq: h.tensor_mul(
                q0s[sq][:, a:512].rearrange("p (hh d) -> p hh d", d=64),
                qkraw[sq][:, a:512].rearrange("p (hh d) -> p hh d", d=64),
                nrm[sq][:, a // 64:8].rearrange("p (hh o) -> p hh o", o=1)
                .broadcast_to([128, nh, 64])),
              reads=[("qkraw", sq), ("nrm", sq)], writes=[("q0s", sq)])

            def rot_fn(h, a, b_, ct, st, sq=sq):
                nh2 = (b_ - a) // 64
                h.tensor_tensor(
                    sqt[sq][:, a:b_].rearrange("p (hh d) -> p hh d", d=64),
                    q0s[sq][:, a:b_].rearrange("p (hh d) -> p hh d", d=64),
                    ct[:, DH * j:DH * (j + 1)].rearrange(
                        "p (o d) -> p o d", o=1).broadcast_to([128, nh2, 64]),
                    op=ALU.mult)
                h.tensor_tensor(
                    t2s[sq][:, a:b_].rearrange("p (hh d) -> p hh d", d=64)
                    [:, :, 0:32],
                    q0s[sq][:, a:b_].rearrange("p (hh d) -> p hh d", d=64)
                    [:, :, 32:64],
                    st[:, DH * j:DH * j + 32].rearrange(
                        "p (o d) -> p o d", o=1).broadcast_to([128, nh2, 32]),
                    op=ALU.mult)
                last2 = h.tensor_tensor(
                    t2s[sq][:, a:b_].rearrange("p (hh d) -> p hh d", d=64)
                    [:, :, 32:64],
                    q0s[sq][:, a:b_].rearrange("p (hh d) -> p hh d", d=64)
                    [:, :, 0:32],
                    st[:, DH * j + 32:DH * (j + 1)].rearrange(
                        "p (o d) -> p o d", o=1).broadcast_to([128, nh2, 32]),
                    op=ALU.mult)
                return last2

            if own:
                E("vector", lambda h: rot_fn(h, 0, 256, tabs[0], tabs[1]),
                  reads=[("q0s", sq), ("tab", 0), ("tab", 1)],
                  writes=[("sqt", sq), ("t2s", sq)])
            E("vector", lambda h: rot_fn(h, 256, 512, tabs[2], tabs[3]),
              reads=[("q0s", sq), ("tab", 2), ("tab", 3)],
              writes=[("sqt", sq), ("t2s", sq)])
            a0 = 0 if own else 256
            E("gpsimd", lambda h, a0=a0, j=j, sq=sq: h.tensor_add(
                qkt[j % 3][:, a0:512], sqt[sq][:, a0:512],
                t2s[sq][:, a0:512]),
              reads=[("sqt", sq), ("t2s", sq)], writes=[("qkt", j % 3)])
        def emit_transposes(j):
            own = j >= 4
            for pp in range(2):
                if own:
                    E("tensor", lambda h, pp=pp, j=j: h.transpose(
                        psTrQ[:], qkt[j % 3][:, 128 * pp:128 * (pp + 1)],
                        idb[:]),
                      reads=[("qkt", j % 3)], writes=["psTrQ"])
                    E("scalar", lambda h, pp=pp, j=j: h.copy(
                        qh[pp][:, 128 * (j - 4):128 * (j - 3)],
                        psTrQ[:]),
                      reads=["psTrQ"], writes=[("qh", pp)])
                E("tensor", lambda h, pp=pp, j=j: h.transpose(
                    psTrK[:],
                    qkt[j % 3][:, 256 + 128 * pp:256 + 128 * (pp + 1)],
                    idb[:]),
                  reads=[("qkt", j % 3)], writes=["psTrK"])
                E("scalar", lambda h, pp=pp, j=j: h.copy(
                    kh[pp][:, 128 * j:128 * (j + 1)], psTrK[:]),
                  reads=["psTrK"], writes=[("kh", pp)])

        order = [(jj + 4) % NKC for jj in range(NKC)]
        emit_casts(order[0])
        for i, j in enumerate(order):
            emit_mm(j)
            if i + 1 < len(order):
                emit_casts(order[i + 1])
            emit_chunk(j)
            if i > 1:
                emit_transposes(order[i - 2])
        emit_transposes(order[-2])
        emit_transposes(order[-1])
        p.barrier()
        qps.close()
        if g == 0:
            load_wqkv(1)

        with contextlib.ExitStack() as ph2:
            Sb = lambda n, *a: ph2.enter_context(nc.sbuf_tensor(_u(n), *a))
            PSb = lambda n, *a: ph2.enter_context(nc.psum_tensor(_u(n), *a))
            psS = PSb("psS", [128, 640], F32)
            psST = [PSb(f"psST{s}", [128, 640], F32) for s in range(2)]
            psO = [PSb(f"psO{s}", [128, 128], F32) for s in range(2)]
            PT = [Sb(f"PT{s}", [128, 640], BF16) for s in range(2)]
            negmS = Sb("negmS", [128, 2], F32)
            negmrow = [Sb(f"negmrow{s}", [1, 128], BF16) for s in range(2)]
            negmrow4 = [Sb(f"negmrow4{s}", [1, 512], BF16) for s in range(2)]
            ouq = [Sb(f"ouq{s}", [128, 128], F32) for s in range(2)]
            denr = [Sb(f"denr{s}", [1, 128], F32) for s in range(2)]

            for hh in range(4):
                hglob = 4 * g + hh
                pp, hb = hh // 2, 64 * (hh % 2)

                def av_mm(qc, hglob=hglob):
                    sl = qc % 2

                    def av_fn(h, qc=qc, sl=sl, hglob=hglob):
                        last = None
                        for i in range(5):
                            last = h.matmul(
                                psO[sl][0:65, :],
                                vx[qc + i][:, 65 * (hglob % 4):
                                           65 * (hglob % 4) + 65],
                                PT[sl][:, 128 * i:128 * (i + 1)],
                                start=(i == 0), stop=True,
                                skip_group_check=True)
                        return last
                    E("tensor", av_fn,
                      reads=[("PT", sl)] + [("vx", qc + i)
                                            for i in range(5)],
                      writes=[("psO", sl)])
                    E("scalar", lambda h, sl=sl: h.copy(
                        ouq[sl][0:65, :], psO[sl][0:65, :]),
                      reads=[("psO", sl)], writes=[("ouq", sl)])

                def av_recip(qc):
                    sl = qc % 2
                    E("vector", lambda h, sl=sl: h.reciprocal(
                        denr[sl][:], ouq[sl][64:65, :]),
                      reads=[("ouq", sl)], writes=[("denr", sl)])

                def av_norm(qc, pp=pp, hb=hb):
                    sl = qc % 2
                    tb = qc % 2
                    E("tensor", lambda h, sl=sl, tb=tb: h.matmul(
                        psO[tb][0:64, :], onesf[0:1, 0:64], denr[sl][:],
                        start=True, stop=True, skip_group_check=True),
                      reads=[("denr", sl)], writes=[("psO", tb)])
                    E("vector", lambda h, sl=sl, tb=tb, qc=qc, pp=pp,
                      hb=hb: h.tensor_mul(
                          ofm[2 * g + pp][hb:hb + 64,
                                          128 * qc:128 * (qc + 1)],
                          ouq[sl][0:64, :], psO[tb][0:64, :]),
                      reads=[("ouq", sl), ("psO", tb)],
                      writes=[("ofm", 2 * g + pp)])

                def sweep1(qc):
                    sl = qc % 2

                    def s_fn(h, qc=qc, pp=pp, hb=hb):
                        h.matmul(psS[:, 0:512],
                                 qh[pp][hb:hb + 64,
                                        128 * qc:128 * (qc + 1)],
                                 kh[pp][hb:hb + 64,
                                        128 * qc:128 * qc + 512],
                                 start=True, stop=True,
                                 skip_group_check=True)
                        h.matmul(psS[:, 512:640],
                                 qh[pp][hb:hb + 64,
                                        128 * qc:128 * (qc + 1)],
                                 kh[pp][hb:hb + 64,
                                        128 * qc + 512:128 * qc + 640],
                                 start=True, stop=True,
                                 skip_group_check=True)
                        h.matmul(psS[:, 0:128], idb[:], masks[0][:],
                                 start=False, stop=True,
                                 skip_group_check=True)
                        last = h.matmul(psS[:, 512:640], idb[:],
                                        masks[1][:], start=False,
                                        stop=True, skip_group_check=True)
                        for i in range(max(0, 4 - qc)):
                            last = h.matmul(
                                psS[:, 128 * i:128 * (i + 1)], onesb[:],
                                bigrow[:], start=False, stop=True,
                                skip_group_check=True)
                        return last
                    E("tensor", s_fn,
                      reads=[("qh", pp), ("kh", pp), ("msk", 0),
                             ("msk", 1), "bigrow", "idb", "onesb"],
                      writes=["psS"])
                    E("vector", lambda h, sl=sl: h.reduce_max(
                        negmS[:, sl:sl + 1], psS[:, 0:640], axis=AX.X,
                        negate=True),
                      reads=["psS"], writes=["negmS"])

                def st_a(qc):
                    sl = qc % 2

                    def fn(h, qc=qc, sl=sl, pp=pp, hb=hb):
                        for i in range(5):
                            h.matmul(
                                psST[sl][:, 128 * i:128 * (i + 1)],
                                kh[pp][hb:hb + 64,
                                       128 * (qc + i):128 * (qc + i + 1)],
                                qh[pp][hb:hb + 64,
                                       128 * qc:128 * (qc + 1)],
                                start=(i == 0 or i == 4), stop=True,
                                skip_group_check=True)
                        h.matmul(psST[sl][:, 0:128], idb[:], masks[2][:],
                                 start=False, stop=True,
                                 skip_group_check=True)
                        last = h.matmul(psST[sl][:, 512:640], idb[:],
                                        masks[3][:], start=False,
                                        stop=True, skip_group_check=True)
                        for i in range(max(0, 4 - qc)):
                            last = h.matmul(
                                psST[sl][:, 128 * i:128 * (i + 1)],
                                onesb[:], bigrow[:], start=False,
                                stop=True, skip_group_check=True)
                        return last
                    E("tensor", fn,
                      reads=[("qh", pp), ("kh", pp), ("msk", 2), ("msk", 3),
                             "bigrow", "idb", "onesb"],
                      writes=[("psST", sl)])

                def negm_row(qc):
                    sl = qc % 2
                    E("tensor", lambda h, sl=sl: h.transpose(
                        psS[0:1, 0:128], negmS[:, sl:sl + 1], idf[:]),
                      reads=["negmS"], writes=["psS"])
                    E("scalar", lambda h, sl=sl: h.copy(
                        negmrow[sl][0:1, :], psS[0:1, 0:128]),
                      reads=["psS"], writes=[("negmrow", sl)])
                    E("vector", lambda h, sl=sl: h.tensor_copy(
                        negmrow4[sl][0:1, :].rearrange(
                            "p (o n) -> p o n", o=4),
                        negmrow[sl][0:1, :].rearrange(
                            "p (o n) -> p o n", o=1)
                        .broadcast_to([1, 4, 128])),
                      reads=[("negmrow", sl)], writes=[("negmrow4", sl)])

                def st_b(qc):
                    sl = qc % 2

                    def fn(h, sl=sl):
                        h.matmul(psST[sl][:, 0:512], onesb[:],
                                 negmrow4[sl][0:1, :], start=False,
                                 stop=True, skip_group_check=True)
                        return h.matmul(psST[sl][:, 512:640], onesb[:],
                                        negmrow[sl][0:1, :], start=False,
                                        stop=True, skip_group_check=True)
                    E("tensor", fn,
                      reads=[("negmrow4", sl), ("negmrow", sl)],
                      writes=[("psST", sl)])
                    E("scalar", lambda h, sl=sl: h.activation(
                        PT[sl][:], psST[sl][:], AF.Exp, scale=QK_SCALE),
                      reads=[("psST", sl)], writes=[("PT", sl)])

                for qc in range(NQC):
                    if qc > 2:
                        av_recip(qc - 3)
                    sweep1(qc)
                    st_a(qc)
                    negm_row(qc)
                    if qc > 0:
                        st_b(qc - 1)
                    if qc > 1:
                        av_mm(qc - 2)
                    if qc > 2:
                        av_norm(qc - 3)
                st_b(NQC - 1)
                av_mm(NQC - 2)
                av_recip(NQC - 3)
                av_norm(NQC - 3)
                av_mm(NQC - 1)
                av_recip(NQC - 2)
                av_norm(NQC - 2)
                av_recip(NQC - 1)
                av_norm(NQC - 1)

            p.barrier()


def finale(nc, p, es, x_in, out_d, xs, idf, onesf, epsc, vec8):
    E = p.emit
    with contextlib.ExitStack() as ph:
        Sa = lambda n, *a: ph.enter_context(nc.sbuf_tensor(_u(n), *a))
        PSa = lambda n, *a: ph.enter_context(nc.psum_tensor(_u(n), *a))
        xt = [Sa(f"xtF{s}", [128, C], F32) for s in range(2)]
        ft = [Sa(f"ftF{s}", [128, C], F32) for s in range(2)]
        ot = [Sa(f"otF{s}", [128, C], F32) for s in range(2)]
        tiny = Sa("tinyF", [128, 32], F32)
        mrow = Sa("mrowF", [1, 128], F32)
        vecb = Sa("vecbF", [128, C], F32)
        psF = [PSa(f"psF{s}", [128, 512], F32) for s in range(4)]
        psV = PSa("psVF", [128, 128], F32)

        for cc in range(8):
            E("tensor", lambda h, cc=cc: h.transpose(
                psV[0:1, 0:128], vec8[:, cc:cc + 1], idf[:]),
              reads=["vec8"], writes=["psV"])
            E("scalar", lambda h: h.copy(mrow[0:1, 0:128],
                                         psV[0:1, 0:128]),
              reads=["psV"], writes=["mrowF"])
            E("tensor", lambda h, cc=cc: h.matmul(
                psV[:, 0:128], onesf[:], mrow[0:1, 0:128],
                start=True, stop=True, skip_group_check=True),
              reads=["mrowF", "onesf"], writes=["psV"])
            E("scalar", lambda h, cc=cc: h.copy(
                vecb[:, 128 * cc:128 * (cc + 1)], psV[:, 0:128]),
              reads=["psV"], writes=["vecbF"])

        def fin_tile(i):
            sl = i % 2
            col = QOFF + 128 * i
            E("sync", lambda h: h.dma_start(
                out=xt[sl][:],
                in_=x_in[QOFF + 128 * i:QOFF + 128 * (i + 1), :]),
              writes=[("xtF", sl)], dma=True)
            for cc in range(8):
                pf = 2 * sl + cc // 4
                E("tensor", lambda h, cc=cc, pf=pf: h.transpose(
                    psF[pf][:, 128 * (cc % 4):128 * (cc % 4 + 1)],
                    xs[cc][:, col:col + 128], idf[:]),
                  reads=[("x", cc)], writes=[("psF", pf)],
                  inc=(cc % 4 == 3))
            for hf in range(2):
                E("scalar", lambda h, hf=hf: h.copy(
                    ft[sl][:, 512 * hf:512 * (hf + 1)],
                    psF[2 * sl + hf][:, 0:512]),
                  reads=[("psF", 2 * sl + hf)], writes=[("ftF", sl)])
            tb = 16 * sl
            E("vector", lambda h, tb=tb: h.bn_stats(
                tiny[:, tb:tb + 6], ft[sl][:, 0:512]),
              reads=[("ftF", sl)], writes=[("tinyF", sl)])
            E("vector", lambda h, tb=tb: h.bn_stats(
                tiny[:, tb + 6:tb + 12], ft[sl][:, 512:C]),
              reads=[("ftF", sl)], writes=[("tinyF", sl)])
            E("vector", lambda h, tb=tb: h.bn_aggr(
                tiny[:, tb + 12:tb + 14], tiny[:, tb:tb + 12]),
              reads=[("tinyF", sl)], writes=[("tinyF", sl)])
            E("scalar", lambda h, tb=tb: h.activation(
                tiny[:, tb + 13:tb + 14], tiny[:, tb + 13:tb + 14],
                AF.Sqrt, bias=epsc[:, 0:1]),
              reads=[("tinyF", sl)], writes=[("tinyF", sl)])
            E("vector", lambda h, tb=tb: h.reciprocal(
                tiny[:, tb + 13:tb + 14], tiny[:, tb + 13:tb + 14]),
              reads=[("tinyF", sl)], writes=[("tinyF", sl)])
            E("vector", lambda h, tb=tb: h.tensor_scalar(
                ft[sl][:], ft[sl][:], tiny[:, tb + 12:tb + 13],
                tiny[:, tb + 13:tb + 14],
                op0=ALU.subtract, op1=ALU.mult),
              reads=[("ftF", sl), ("tinyF", sl)], writes=[("ftF", sl)])
            E("vector", lambda h: h.tensor_add(ot[sl][:], xt[sl][:],
                                               vecb[:]),
              reads=[("xtF", sl), "vecbF"], writes=[("otF", sl)])
            E("vector", lambda h: h.tensor_add(ot[sl][:], ot[sl][:],
                                               ft[sl][:]),
              reads=[("otF", sl), ("ftF", sl)], writes=[("otF", sl)])
            E("sync", lambda h: h.dma_start(
                out=out_d[128 * i:128 * (i + 1), :], in_=ot[sl][:]),
              reads=[("otF", sl)], writes=[("outd", i)], dma=True)

        for i in range(NQC):
            fin_tile(i)
        p.barrier()


_BUILT = {}


def _get_program(dbg=()):
    key = tuple(sorted(map(str, dbg)))
    if key not in _BUILT:
        _BUILT[key] = build_program(dbg)
    return _BUILT[key]


def make_core_inputs(inputs, core):
    b, half = core // 2, core % 2
    t0 = half * OWN
    x = np.asarray(inputs["L0_attn_x"])
    xc = np.zeros((XW, C), dtype=np.float32)
    xc[QOFF:] = x[b, t0:t0 + OWN]
    if half == 1:
        xc[DEAD:QOFF] = x[b, t0 - HALO:t0]
    cs_q, sn_q, cs_k, sn_k = _rotary_tables(t0 - 512)
    upS, loS, upT, loT = _masks()
    big = np.full((1, 128), NEG if half == 0 else 0.0, dtype=np.float32)
    return {
        "x": xc,
        "tabi0": cs_q, "tabi1": sn_q, "tabi2": cs_k, "tabi3": sn_k,
        "maski0": np.asarray(upS), "maski1": np.asarray(loS),
        "maski2": np.asarray(upT), "maski3": np.asarray(loT),
        "bigrow": big.astype(ml_dtypes.bfloat16),
        "gate": np.full((128, 1), float(half), dtype=np.float32),
        "a_wqkv": np.asarray(inputs["L0_attn_w_qkv"], dtype=np.float32),
        "a_wc1": np.asarray(inputs["L0_attn_w_conv1"],
                            dtype=np.float32).reshape(1, C),
        "f_wqkv": np.asarray(inputs["L0_ffn_wqkv"], dtype=np.float32),
        "f_qs": np.asarray(inputs["L0_ffn_q_scale"], dtype=np.float32),
        "f_ks": np.asarray(inputs["L0_ffn_k_scale"], dtype=np.float32),
        "f_wo": np.asarray(inputs["L0_ffn_wo"], dtype=np.float32),
        "f_lng": np.asarray(inputs["L0_ffn_ln_g"], dtype=np.float32),
        "f_lnb": np.asarray(inputs["L0_ffn_ln_b"], dtype=np.float32),
        "f_w1": np.asarray(inputs["L0_ffn_w1"], dtype=np.float32),
        "f_b1": np.asarray(inputs["L0_ffn_b1"], dtype=np.float32),
        "f_w2": np.asarray(inputs["L0_ffn_w2"], dtype=np.float32),
        "f_b2": np.asarray(inputs["L0_ffn_b2"], dtype=np.float32),
    }


def run(inputs, dbg=()):
    nc, dbg_out = _get_program(dbg)
    in_maps = [make_core_inputs(inputs, c) for c in range(NCORE)]
    return run_bass_kernel_spmd(nc, in_maps, core_ids=list(range(NCORE)))


def kernel(**inputs):
    res = run(inputs)
    out = np.zeros((B, N, C), dtype=np.float32)
    for c in range(NCORE):
        b, half = c // 2, c % 2
        out[b, half * OWN:(half + 1) * OWN] = res.results[c]["out"]
    return out
